# version 12
# baseline (speedup 1.0000x reference)
import numpy as np
from contextlib import ExitStack

import concourse.bass as bass
import concourse.mybir as mybir
from concourse.bass_utils import run_bass_kernel_spmd

F32 = mybir.dt.float32
AF = mybir.ActivationFunctionType
ALU = mybir.AluOpType
AX = mybir.AxisListType

NB = 2
S = 2048
D = 1024
NC = D // 128
NT = S // 128
H = 16
DH = 64
BIG = 1.0e30
NEGAUG = -8.0e30
EPS = 1e-5
N_BISECT = 45


class Ctr:

    def __init__(self, sem):
        self.sem = sem
        self.v = 0

    def bump(self, inst, n=1):
        inst.then_inc(self.sem, n)
        self.v += n
        return self.v


class K:

    def __init__(self, nc, ctx, dbg):
        self.nc = nc
        self.ctx = ctx
        self.dbg = dbg
        self.sy = nc.sync
        self.ve = nc.vector
        self.sc = nc.scalar
        self.te = nc.tensor
        self.gp = nc.gpsimd
        self.ins = {}
        self.scr = {}
        self.sb = {}

    _uid = 0

    def sbuf(self, name, shape, dtype=F32):
        t = self.ctx.enter_context(self.nc.sbuf_tensor(name, list(shape), dtype))
        self.sb[name] = t
        return t

    def tmp(self, ctx, name, shape, dtype=F32):
        K._uid += 1
        return ctx.enter_context(self.nc.sbuf_tensor(f"{name}_{K._uid}", list(shape), dtype))

    def ptmp(self, ctx, name, shape, dtype=F32):
        K._uid += 1
        return ctx.enter_context(self.nc.psum_tensor(f"{name}_{K._uid}", list(shape), dtype))

    def psum(self, name, shape, dtype=F32):
        return self.ctx.enter_context(self.nc.psum_tensor(name, list(shape), dtype))

    _fl = 0

    def vflush(self):
        K._fl = (K._fl + 1) % 4
        self.ve.wait_ge(self.s_ve.sem, self.flush_c0)
        c = self.s_ve.bump(self.ve.tensor_copy(
            self.sb["flush_sb"][0:1, 4 + K._fl:5 + K._fl],
            self.sb["flush_sb"][0:1, 0:1]))
        return c

    def vs(self, inst):
        c = self.s_ve.bump(inst)
        self.ve.wait_ge(self.s_ve.sem, c)
        return c

    def ss(self, inst):
        c = self.s_sc.bump(inst)
        self.sc.wait_ge(self.s_sc.sem, c)
        return c

    def sem(self, name):
        return Ctr(self.ctx.enter_context(self.nc.semaphore(name)))

    def barrier(self):
        self.nc.all_engine_barrier()


def declare_io(k: K):
    nc = k.nc
    def inp(name, shape):
        k.ins[name] = nc.dram_tensor(name, list(shape), F32, kind="ExternalInput").ap()
    inp("x", [NB, S, D])
    for w in ("w0aq", "w0ak", "w0av", "w0ao"):
        inp(w, [D, D])
    for b in ("b0aq", "b0ak", "b0av", "b0ao"):
        inp(b, [128, NC])
    inp("w0fg", [128, 2 * NC])
    for w in ("w0fq", "w0fk", "w0fv", "w0fo"):
        inp(w, [D, D])
    inp("w1aq", [D, D]); inp("w1ao", [D, D])
    inp("w1ak", [D, 256]); inp("w1av", [D, 256])
    inp("b1aq", [128, NC]); inp("b1ao", [128, NC])
    inp("b1ak", [128, 2]); inp("b1av", [128, 2])
    inp("slopes", [1, H])
    inp("w1g", [D, 8]); inp("b1g", [8, 1])
    inp("w1e1", [8, D, 2 * D]); inp("b1e1", [8, 128, 16])
    inp("w1e2", [8, 2 * D, D]); inp("b1e2", [8, 128, 8])
    inp("gamma", [128, NC]); inp("beta", [128, NC])

    k.out = nc.dram_tensor("out", [NB, S, D], F32, kind="ExternalOutput").ap()

    kind = "ExternalOutput" if k.dbg else "Internal"
    def scr(name, shape):
        k.scr[name] = nc.dram_tensor(name, list(shape), F32, kind=kind).ap()
    scr("xT", [NB, D, S])
    scr("xnT", [NB, D, S])
    scr("qT", [NB, D, S])
    scr("kT", [NB, D, S])
    scr("vTm", [NB, S, D])
    scr("oaT", [NB, D, S])
    scr("ofT", [NB, D, S])
    scr("x1T", [NB, D, S])
    scr("xn1T", [NB, D, S])
    scr("a1T", [NB, D, S])
    scr("mixT", [NB, D, S])
    scr("augr", [NB, S])
    scr("negm", [16, 128])
    if k.dbg:
        scr("attn_dbg", [NB, 4, S])
    if k.dbg:
        scr("zdbg", [NB, 8, S])


def emit_init(k: K):
    ident = k.sbuf("ident", [128, 128])
    ones_row = k.sbuf("ones_row", [1, 128])
    onescol = k.sbuf("onescol", [128, 1])
    r512 = k.sbuf("r512", [128, 512])
    slop = k.sbuf("slop", [1, H])
    bias_sb = {}
    for nm in ("b0aq", "b0ak", "b0av", "b0ao", "b1aq", "b1ao", "gamma", "beta"):
        bias_sb[nm] = k.sbuf(nm + "_sb", [128, NC])
    for nm in ("b1ak", "b1av"):
        bias_sb[nm] = k.sbuf(nm + "_sb", [128, 2])
    bias_sb["b1g"] = k.sbuf("b1g_sb", [8, 1])
    bias_sb["w0fg"] = k.sbuf("w0fg_sb", [128, 2 * NC])
    k.bias = bias_sb

    LD, GPc = k.s_ld, k.s_gp
    for nm in ("b0aq", "b0ak", "b0av", "b0ao", "b1aq", "b1ao", "gamma", "beta",
               "b1ak", "b1av", "b1g", "w0fg"):
        c = LD.bump(k.sy.dma_start(out=bias_sb[nm][:, :], in_=k.ins[nm]), 16)
    c = LD.bump(k.sy.dma_start(out=slop[:, :], in_=k.ins["slopes"]), 16)
    ld_done = LD.v

    g = k.gp
    g.iota(ident[:, :], pattern=[[1, 128]], base=0, channel_multiplier=-1,
           allow_small_or_imprecise_dtypes=True)
    gc = GPc.bump(g.iota(r512[:, :], pattern=[[1, 512]], base=0, channel_multiplier=-1,
                         allow_small_or_imprecise_dtypes=True))
    v = k.ve
    k.flush_c0 = 0
    k.flush_c0 = k.vs(v.memset(k.sb["flush_sb"][0:1, :], 0.0))
    v.wait_ge(GPc.sem, gc)
    v.tensor_single_scalar(ident[:, :], ident[:, :], 0.0, op=ALU.is_equal)
    v.memset(ones_row[:, :], 1.0)
    v.memset(onescol[:, :], 1.0)
    v.wait_ge(LD.sem, LD.v)
    k.ident = ident
    k.ones_row = ones_row
    k.onescol = onescol
    k.r512 = r512
    k.slop = slop


def emit_p0(k: K, b: int):
    nc = k.nc
    LD, PE, VE, SC, ST = k.s_ld, k.s_pe, k.s_ve, k.s_sc, k.s_st
    with ExitStack() as ctx:
        xt = [k.tmp(ctx, "p0x", [128, D]) for i in range(2)]
        stg_xn = k.tmp(ctx, "p0sxn", [128, D])
        stg_x = k.tmp(ctx, "p0sx", [128, D])
        xc = k.tmp(ctx, "p0xc", [128, D])
        sqd = k.tmp(ctx, "p0sqd", [128, D])
        st1 = k.tmp(ctx, "p0st", [128, 12])
        ps_t = k.ptmp(ctx, "p0pst", [128, 1024])
        ps_t2 = k.ptmp(ctx, "p0pst2", [128, 1024])
        ps_z = k.ptmp(ctx, "p0psz", [128, 128])

        ld0 = [0, 0]
        pe_done = [0] * NT
        sc_evac = [0] * NT
        v, s, t, g = k.ve, k.sc, k.te, k.gp
        for tt in range(NT):
            buf = xt[tt % 2]
            if tt >= 2:
                k.sy.wait_ge(PE.sem, pe_done[tt - 2])
            LDP = k.s_lda if tt % 2 == 0 else k.s_ldb
            ld0[tt % 2] = LDP.bump(k.sy.dma_start(
                out=buf[:, :], in_=k.ins["x"][b, tt * 128:(tt + 1) * 128, :]), 16)

            v.wait_ge(LDP.sem, ld0[tt % 2])
            k.vs(v.reduce_sum(st1[:, 0:1], buf[:, :], axis=AX.X))
            vc = k.vs(v.tensor_scalar(st1[:, 1:2], st1[:, 0:1], -1.0 / D, None,
                                      op0=ALU.mult))
            s.wait_ge(VE.sem, vc)
            sc0 = SC.bump(s.activation(sqd[:, :], buf[:, :], AF.Square, bias=0.0,
                                       scale=1.0, accum_out=st1[:, 2:3]))
            v.wait_ge(SC.sem, sc0)
            v.tensor_scalar(st1[:, 3:4], st1[:, 2:3], 1.0 / D, None, op0=ALU.mult)
            k.vs(v.tensor_mul(st1[:, 4:5], st1[:, 1:2], st1[:, 1:2]))
            k.vs(v.tensor_sub(st1[:, 3:4], st1[:, 3:4], st1[:, 4:5]))
            vc = k.vs(v.tensor_scalar(st1[:, 3:4], st1[:, 3:4], EPS, None,
                                      op0=ALU.add))
            s.wait_ge(VE.sem, vc)
            k.ss(s.activation(st1[:, 7:8], st1[:, 3:4], AF.Ln, bias=0.0, scale=1.0))
            sc1 = SC.bump(s.activation(st1[:, 4:5], st1[:, 7:8], AF.Exp, bias=0.0,
                                       scale=-0.5))
            v.wait_ge(SC.sem, sc1)
            k.vs(v.tensor_mul(st1[:, 5:6], st1[:, 4:5], st1[:, 4:5]))
            k.vs(v.tensor_mul(st1[:, 5:6], st1[:, 5:6], st1[:, 3:4]))
            k.vs(v.tensor_scalar(st1[:, 5:6], st1[:, 5:6], -0.5, 1.5, op0=ALU.mult,
                                 op1=ALU.add))
            k.vs(v.tensor_mul(st1[:, 6:7], st1[:, 4:5], st1[:, 5:6]))
            if tt >= 1:
                v.wait_ge(PE.sem, pe_done[tt - 1])
            k.vs(v.tensor_scalar(xc[:, :], buf[:, :], st1[:, 1:2], None, op0=ALU.add))
            vxn = VE.bump(v.tensor_scalar(xc[:, :], xc[:, :], st1[:, 6:7], None,
                                          op0=ALU.mult))

            t.wait_ge(VE.sem, vxn)
            if tt >= 1:
                t.wait_ge(SC.sem, sc_evac[tt - 1])
            for c in range(NC):
                t.transpose(ps_t[:, c * 128:c * 128 + 128],
                            xc[:, c * 128:(c + 1) * 128], k.ident[:, :])
            last = None
            for c in range(NC):
                last = t.transpose(ps_t2[:, c * 128:c * 128 + 128],
                                   buf[:, c * 128:(c + 1) * 128], k.ident[:, :])
            pe_done[tt] = PE.bump(last)

            s.wait_ge(PE.sem, pe_done[tt])
            for c in range(NC):
                s.copy(stg_xn[:, c * 128:(c + 1) * 128], ps_t[:, c * 128:c * 128 + 128])
            last = None
            for c in range(NC):
                last = s.copy(stg_x[:, c * 128:(c + 1) * 128], ps_t2[:, c * 128:c * 128 + 128])
            sc_evac[tt] = SC.bump(last)

            t.wait_ge(SC.sem, sc_evac[tt])
            for c in range(NC):
                last = t.matmul(ps_z[0:2, 0:128], k.bias["w0fg"][:, 2 * c:2 * c + 2],
                                stg_xn[:, c * 128:(c + 1) * 128],
                                start=(c == 0), stop=(c == NC - 1))
            pez = PE.bump(last)
            s.wait_ge(PE.sem, pez)
            s.copy(k.sb["z_sb"][0:1, tt * 128:(tt + 1) * 128], ps_z[0:1, 0:128])

            g.wait_ge(SC.sem, sc_evac[tt])
            ST.bump(g.dma_start(
                out=k.scr["xnT"][b].rearrange("(c p) t -> p c t", p=128)[:, :, tt * 128:(tt + 1) * 128],
                in_=stg_xn[:, :].rearrange("p (c t) -> p c t", c=NC)), 16)
            ST.bump(g.dma_start(
                out=k.scr["xT"][b].rearrange("(c p) t -> p c t", p=128)[:, :, tt * 128:(tt + 1) * 128],
                in_=stg_x[:, :].rearrange("p (c t) -> p c t", c=NC)), 16)
            if tt < NT - 1:
                k.sc.wait_ge(ST.sem, ST.v)
        k.gp.wait_ge(ST.sem, ST.v)


def emit_bisect(k: K, b: int):
    nc = k.nc
    VE, SC, PE, ST, LD = k.s_ve, k.s_sc, k.s_pe, k.s_st, k.s_ld
    v, s, t, g = k.ve, k.sc, k.te, k.gp
    z = k.sb["z_sb"]
    st0 = k.sb["bst_sb"]
    zr = z[0:1, :]
    st = lambda i: st0[0:1, 16 * b + i:16 * b + i + 1]
    with ExitStack() as ctx:
        cmp_row = k.tmp(ctx, "bcmp", [1, S])
        aug = k.tmp(ctx, "baug", [1, S])
        aug_a = k.tmp(ctx, "bauga", [1, S])
        gsc = k.tmp(ctx, "bgsc", [1, S])
        psg = k.ptmp(ctx, "bpsg", [128, 128])

        k.vs(v.reduce_max(st(0), zr, axis=AX.X))
        k.vs(v.tensor_reduce(st(1), zr, axis=AX.X, op=ALU.min))
        v.tensor_scalar(st(2), st(1), -1.0, None, op0=ALU.add)
        k.vs(v.tensor_scalar(st(3), st(0), 1.0, None, op0=ALU.add))
        for it in range(N_BISECT):
            k.vs(v.tensor_add(st(4), st(2), st(3)))
            k.vs(v.tensor_scalar(st(4), st(4), 0.5, None, op0=ALU.mult))
            k.vs(v.tensor_scalar(cmp_row[0:1, :], zr, st(4), 0.0,
                                 op0=ALU.is_ge, op1=ALU.add, accum_out=st(5)))
            k.vs(v.tensor_scalar(st(6), st(5), 1024.0, None, op0=ALU.is_ge))
            k.vs(v.tensor_sub(st(7), st(4), st(2)))
            k.vs(v.tensor_mul(st(7), st(7), st(6)))
            k.vs(v.tensor_scalar(st(8), st(6), -1.0, 1.0, op0=ALU.mult, op1=ALU.add))
            k.vs(v.tensor_sub(st(9), st(4), st(3)))
            k.vs(v.tensor_mul(st(9), st(9), st(8)))
            k.vs(v.tensor_add(st(2), st(2), st(7)))
            k.vs(v.tensor_add(st(3), st(3), st(9)))
        vau = k.vs(v.tensor_scalar(aug[0:1, :], zr, st(2), NEGAUG,
                                   op0=ALU.is_lt, op1=ALU.mult))
        s.wait_ge(VE.sem, vau)
        saug = SC.bump(s.copy(aug_a[0:1, :], aug[0:1, :]))
        g.wait_ge(SC.sem, saug)
        sta = ST.bump(g.dma_start(out=k.scr["augr"][b, :], in_=aug_a[0:1, :]), 16)
        s.wait_ge(VE.sem, vau)
        sc0 = SC.bump(s.activation(cmp_row[0:1, :], zr, AF.Exp, bias=0.0, scale=-1.0))
        v.wait_ge(SC.sem, sc0)
        k.vs(v.tensor_scalar(cmp_row[0:1, :], cmp_row[0:1, :], 1.0, None, op0=ALU.add))
        k.vs(v.reciprocal(gsc[0:1, :], cmp_row[0:1, :]))
        k.vs(v.tensor_scalar(cmp_row[0:1, :], zr, st(2), None, op0=ALU.is_ge))
        if k.dbg:
            g.wait_ge(VE.sem, VE.v)
            ST.bump(g.dma_start(out=k.scr["zdbg"][b, 4, :], in_=gsc[0:1, :]), 16)
            ST.bump(g.dma_start(out=k.scr["zdbg"][b, 5, :], in_=cmp_row[0:1, :]), 16)
            g.wait_ge(ST.sem, ST.v)
        vg = k.vs(v.tensor_mul(gsc[0:1, :], gsc[0:1, :], cmp_row[0:1, :]))
        t.wait_ge(VE.sem, vg)
        for tt in range(NT):
            pc = PE.bump(t.transpose(psg[:, tt % 2:tt % 2 + 1],
                        gsc[0:1, tt * 128:(tt + 1) * 128], k.ident[0:1, 0:1]))
            s.wait_ge(PE.sem, pc)
            scv = SC.bump(s.copy(k.sb["gsccol_sb"][:, b * NT + tt:b * NT + tt + 1],
                                 psg[:, tt % 2:tt % 2 + 1]))
            if tt < NT - 1:
                t.wait_ge(SC.sem, scv)
        v.wait_ge(SC.sem, SC.v)
        vgm = k.vs(v.tensor_scalar(k.sb["lngsc_sb"][:, b * NT:(b + 1) * NT],
                                   k.sb["gsccol_sb"][:, b * NT:(b + 1) * NT],
                                   1e-30, None, op0=ALU.max))
        s.wait_ge(VE.sem, vgm)
        sgl = SC.bump(s.activation(k.sb["lngsc_sb"][:, b * NT:(b + 1) * NT],
                                   k.sb["lngsc_sb"][:, b * NT:(b + 1) * NT],
                                   AF.Ln, bias=0.0, scale=1.0))
        v.wait_ge(SC.sem, sgl)
        k.vs(v.tensor_scalar(k.sb["lngsc_sb"][:, b * NT:(b + 1) * NT],
                             k.sb["lngsc_sb"][:, b * NT:(b + 1) * NT],
                             8.0, None, op0=ALU.mult))
        if k.dbg:
            g.wait_ge(VE.sem, VE.v)
            g.wait_ge(SC.sem, SC.v)
            ST.bump(g.dma_start(out=k.scr["zdbg"][b, 0, :], in_=zr), 16)
            ST.bump(g.dma_start(out=k.scr["zdbg"][b, 1, :], in_=aug[0:1, :]), 16)
            ST.bump(g.dma_start(out=k.scr["zdbg"][b, 2, :], in_=gsc[0:1, :]), 16)
            ST.bump(g.dma_start(out=k.scr["zdbg"][b, 3, 0:16],
                                in_=st0[0:1, 16 * b:16 * b + 16]), 16)
        g.wait_ge(ST.sem, ST.v)


def emit_proj(k: K, b: int, xin, specs):
    nc = k.nc
    LD, PE, VE, SC, ST = k.s_ld, k.s_pe, k.s_ve, k.s_sc, k.s_st
    v, s, t, g, sy = k.ve, k.sc, k.te, k.gp, k.sy
    with ExitStack() as ctx:
        xn_sb = k.tmp(ctx, "prx", [128, NC * S])
        w_sb = k.tmp(ctx, "prw", [128, NC * D])
        stg = [k.tmp(ctx, "prs", [128, 512]) for _ in range(2)]
        vstg = k.tmp(ctx, "prv", [128, 512])
        vt_stage = k.tmp(ctx, "prvt", [128, 4 * D])
        ps = k.ptmp(ctx, "prps", [128, 1024])
        ps_t = k.ptmp(ctx, "prpst", [128, 512])

        ldx = LD.bump(sy.dma_start(
            out=xn_sb[:, :].rearrange("p (c t) -> p c t", c=NC),
            in_=xin.rearrange("(c p) t -> p c t", p=128)), 16)

        for si, spec in enumerate(specs):
            noc = spec["nout"] // 128
            if si > 0:
                sy.wait_ge(PE.sem, PE.v)
            ldw = LD.bump(sy.dma_start(
                out=w_sb[:, 0:NC * spec["nout"]].rearrange("p (c o) -> p c o", c=NC),
                in_=k.ins[spec["w"]].rearrange("(c p) o -> p c o", p=128)), 16)
            bias = k.bias[spec["bias"]] if spec["bias"] else None
            t.wait_ge(LD.sem, max(ldx, ldw))

            if spec.get("dst_tm") is None:
                evd = {}
                std = {}
                it = 0
                for oc in range(noc):
                    for ts4 in range(4):
                        bank = it % 2
                        if it >= 2:
                            t.wait_ge(VE.sem if bias else SC.sem, evd[it - 2])
                        last = None
                        for c in range(NC):
                            last = t.matmul(
                                ps[:, bank * 512:bank * 512 + 512],
                                w_sb[:, c * spec["nout"] + oc * 128:
                                     c * spec["nout"] + oc * 128 + 128],
                                xn_sb[:, c * S + ts4 * 512:c * S + ts4 * 512 + 512],
                                start=(c == 0), stop=(c == NC - 1))
                        pc = PE.bump(last)
                        if it >= 2:
                            STP2 = k.s_sta if it % 2 == 0 else k.s_stb
                            (v if bias else s).wait_ge(STP2.sem, std[it - 2])
                        if bias:
                            v.wait_ge(PE.sem, pc)
                            evd[it] = VE.bump(v.tensor_scalar(
                                stg[bank][:, :], ps[:, bank * 512:bank * 512 + 512],
                                bias[:, oc:oc + 1], None, op0=ALU.add))
                            evd[it] = k.vflush()
                            g.wait_ge(VE.sem, evd[it])
                        else:
                            s.wait_ge(PE.sem, pc)
                            evd[it] = SC.bump(s.copy(
                                stg[bank][:, :], ps[:, bank * 512:bank * 512 + 512]))
                            g.wait_ge(SC.sem, evd[it])
                        STP = k.s_sta if it % 2 == 0 else k.s_stb
                        std[it] = STP.bump(g.dma_start(
                            out=spec["dst"][oc * 128:(oc + 1) * 128,
                                            ts4 * 512:(ts4 + 1) * 512],
                            in_=stg[bank][:, :]), 16)
                        it += 1
                g.wait_ge(k.s_sta.sem, k.s_sta.v)
                g.wait_ge(k.s_stb.sem, k.s_stb.v)
            else:
                for ts4 in range(4):
                    for oc in range(noc):
                        it = ts4 * noc + oc
                        bank = it % 2
                        if it >= 2:
                            t.wait_ge(VE.sem if bias else SC.sem, evd2[it - 2])
                        last = None
                        for c in range(NC):
                            last = t.matmul(
                                ps[:, bank * 512:bank * 512 + 512],
                                w_sb[:, c * spec["nout"] + oc * 128:
                                     c * spec["nout"] + oc * 128 + 128],
                                xn_sb[:, c * S + ts4 * 512:c * S + ts4 * 512 + 512],
                                start=(c == 0), stop=(c == NC - 1))
                        pc = PE.bump(last)
                        if it == 0:
                            evd2, trd, cpd = {}, {}, {}
                        if bias:
                            v.wait_ge(PE.sem, pc)
                            evd2[it] = VE.bump(v.tensor_scalar(
                                vstg[:, :], ps[:, bank * 512:bank * 512 + 512],
                                bias[:, oc:oc + 1], None, op0=ALU.add))
                            t.wait_ge(VE.sem, evd2[it])
                        else:
                            s.wait_ge(PE.sem, pc)
                            evd2[it] = SC.bump(s.copy(
                                vstg[:, :], ps[:, bank * 512:bank * 512 + 512]))
                            t.wait_ge(SC.sem, evd2[it])
                        if it >= 1:
                            t.wait_ge(SC.sem, cpd[it - 1])
                        last = None
                        for tc in range(4):
                            last = t.transpose(ps_t[:, tc * 128:tc * 128 + 128],
                                               vstg[:, tc * 128:tc * 128 + 128],
                                               k.ident[:, :])
                        trd[it] = PE.bump(last)
                        s.wait_ge(PE.sem, trd[it])
                        last = None
                        for tc in range(4):
                            last = s.copy(
                                vt_stage[:, tc * spec["nout"] + oc * 128:
                                         tc * spec["nout"] + oc * 128 + 128],
                                ps_t[:, tc * 128:tc * 128 + 128])
                        cpd[it] = SC.bump(last)
                    g.wait_ge(SC.sem, cpd[ts4 * noc + noc - 1])
                    stv = ST.bump(g.dma_start(
                        out=spec["dst_tm"].rearrange("(tb tc p) f -> tb p tc f",
                                                     p=128, tc=4)[ts4][:, :,
                                                     spec["col0"]:spec["col0"] + spec["nout"]],
                        in_=vt_stage[:, 0:4 * spec["nout"]].rearrange(
                            "p (tc f) -> p tc f", tc=4)), 16)
                    s.wait_ge(ST.sem, stv)
        g.wait_ge(ST.sem, ST.v)
        g.wait_ge(k.s_sta.sem, k.s_sta.v)
        g.wait_ge(k.s_stb.sem, k.s_stb.v)


def emit_attn(k: K, b: int, kind: str):
    nc = k.nc
    LD, PE, VE, SC, ST = k.s_ld, k.s_pe, k.s_ve, k.s_sc, k.s_st
    v, s, t, g, sy = k.ve, k.sc, k.te, k.gp, k.sy
    scale = {"l0a": 1.0 / 32.0, "l0f": 1.0 / 8.0, "l1": 1.0 / 8.0}[kind]
    inv_scale = 1.0 / scale
    CK = 128
    causal = (kind == "l1")
    outscr = k.scr["ofT"] if kind == "l0f" else k.scr["oaT"]

    with ExitStack() as ctx:
        qh = [k.tmp(ctx, "aq", [128, S]) for _ in range(2)]
        kh = [k.tmp(ctx, "ak", [128, S]) for _ in range(2)]
        vt = [k.tmp(ctx, "av", [128, NT * DH]) for _ in range(2)]
        uT = [k.tmp(ctx, "au", [128, NT * 512]) for _ in range(2)]
        pdum = k.tmp(ctx, "apd", [128, S])
        ofm = k.tmp(ctx, "aof", [128, S])
        mall = k.tmp(ctx, "am", [128, NT])
        sall = k.tmp(ctx, "as", [128, NT])
        lns = k.tmp(ctx, "al", [128, NT])
        smt = k.tmp(ctx, "asm", [128, 8])
        nm16 = k.tmp(ctx, "anm", [16, 128])
        srow = k.tmp(ctx, "asr", [1, S])
        srow_a = k.tmp(ctx, "asra", [1, S])
        ones2k = k.tmp(ctx, "ao2", [1, S])
        if causal:
            iota_j = k.tmp(ctx, "aij", [1, S])
            mtri = k.tmp(ctx, "amt", [128, 4 * 512])
            mc1 = k.tmp(ctx, "amc", [128, 4 * 512])
            dstage = k.tmp(ctx, "ads", [128, 4 * 512])
        mgall = k.tmp(ctx, "amg", [128, 64])
        sgall = k.tmp(ctx, "asg", [128, 64])
        tm64 = k.tmp(ctx, "atm", [128, 64])
        s_ps = k.ptmp(ctx, "aps", [128, 2048])
        oav_ps = k.ptmp(ctx, "apo", [64, 1024])

        v.memset(ones2k[0:1, :], 1.0)
        vz = None
        for bb in range(2):
            v.memset(qh[bb][64:128, :], 0.0)
            vz = v.memset(kh[bb][64:128, :], 0.0)
        vz = VE.bump(vz)
        if causal:
            gi = k.s_gp.bump(g.iota(iota_j[0:1, :], pattern=[[1, S]], base=0,
                                    channel_multiplier=0,
                                    allow_small_or_imprecise_dtypes=True))
            v.wait_ge(k.s_gp.sem, gi)
            for r_ in range(4):
                v.tensor_scalar(mtri[:, r_ * 512:(r_ + 1) * 512], k.r512[:, :],
                                float(128 * r_), -BIG, op0=ALU.is_lt, op1=ALU.mult)
            for r_ in range(4):
                v.tensor_scalar(mc1[:, r_ * 512:(r_ + 1) * 512], k.r512[:, :],
                                float(128 * r_), -BIG, op0=ALU.is_gt, op1=ALU.mult)
            k.vs(v.memset(smt[:, 7:8], 0.0))

        def issue_loads(h):
            hb = h % 2
            LDP = k.s_lda if hb == 0 else k.s_ldb
            kvh = h // 4 if kind == "l1" else h
            LDP.bump(sy.dma_start(out=qh[hb][0:64, :],
                                  in_=k.scr["qT"][b][h * 64:(h + 1) * 64, :]), 16)
            LDP.bump(sy.dma_start(out=kh[hb][0:64, :],
                                  in_=k.scr["kT"][b][kvh * 64:(kvh + 1) * 64, :]), 16)
            l3 = LDP.bump(sy.dma_start(
                out=vt[hb][:, :].rearrange("p (kc d) -> p kc d", kc=NT),
                in_=k.scr["vTm"][b].rearrange("(kc p) f -> p kc f", p=128)[
                    :, :, kvh * 64:(kvh + 1) * 64]), 16)
            if kind == "l0f":
                l3 = LDP.bump(sy.dma_start(out=kh[hb][65:66, :],
                                           in_=k.scr["augr"][b, :]), 16)
            return l3

        def aug_rows(h, ldt):
            hb = h % 2
            LDP = k.s_lda if hb == 0 else k.s_ldb
            LA = k.s_la
            v.wait_ge(LDP.sem, ldt)
            v.wait_ge(VE.sem, vz)
            v.tensor_copy(kh[hb][64:65, :], ones2k[0:1, :])
            av = VE.bump(v.memset(qh[hb][64:65, :], 0.0))
            al = LA.v
            if kind == "l0f":
                sy.wait_ge(VE.sem, av)
                al = LA.bump(sy.dma_start(out=qh[hb][65:66, :],
                                          in_=ones2k[0:1, :]), 16)
            elif kind == "l1":
                vsr = k.vs(v.tensor_scalar(srow[0:1, :], ones2k[0:1, :],
                                           k.slop[0:1, h:h + 1], inv_scale,
                                           op0=ALU.mult, op1=ALU.mult))
                s.wait_ge(VE.sem, vsr)
                if h > 0:
                    s.wait_ge(LA.sem, LA.v)
                sca = SC.bump(s.copy(srow_a[0:1, :], srow[0:1, :]))
                sy.wait_ge(SC.sem, sca)
                LA.bump(sy.dma_start(out=kh[hb][65:66, :],
                                     in_=iota_j[0:1, :]), 16)
                al = LA.bump(sy.dma_start(out=qh[hb][65:66, :],
                                          in_=srow_a[0:1, :]), 16)
            return av, al

        ld_t = issue_loads(0)
        aug_t, aug_l = aug_rows(0, ld_t)
        free_at = {}
        for h in range(H):
            hb = h % 2
            if h + 1 < H:
                if h >= 1:
                    sy.wait_ge(PE.sem, free_at[h - 1])
                ld_next = issue_loads(h + 1)

            t.wait_ge((k.s_lda if hb == 0 else k.s_ldb).sem, ld_t)
            t.wait_ge(k.s_la.sem, aug_l)
            t.wait_ge(VE.sem, aug_t)

            t.wait_ge(SC.sem, SC.v)
            expg = {}
            gcnt = 0
            if causal:
                v.memset(mgall[:, :], -BIG)
                k.vs(v.memset(sgall[:, :], 0.0))
            for qt in range(NT):
                nk = (qt // 4 + 1) if causal else 4
                for kc in range(nk):
                    bank = gcnt % 4
                    slot = gcnt % 4
                    col = qt * 4 + kc
                    if gcnt >= 4:
                        t.wait_ge(SC.sem, expg[gcnt - 4])
                    pc = PE.bump(t.matmul(s_ps[:, bank * 512:(bank + 1) * 512],
                                          qh[hb][0:CK, qt * 128:(qt + 1) * 128],
                                          kh[hb][0:CK, kc * 512:(kc + 1) * 512],
                                          start=True, stop=True))
                    v.wait_ge(PE.sem, pc)
                    if causal and kc == qt // 4:
                        vmc = VE.bump(v.tensor_add(
                            s_ps[:, bank * 512:(bank + 1) * 512],
                            s_ps[:, bank * 512:(bank + 1) * 512],
                            mc1[:, (qt % 4) * 512:(qt % 4 + 1) * 512]))
                        v.wait_ge(VE.sem, vmc)
                    if gcnt >= 4:
                        v.wait_ge(SC.sem, expg[gcnt - 4])
                    k.vs(v.reduce_max(mgall[:, col:col + 1],
                                      s_ps[:, bank * 512:(bank + 1) * 512], axis=AX.X))
                    vneg = k.vs(v.tensor_scalar(smt[:, slot:slot + 1],
                                                mgall[:, col:col + 1],
                                                -scale, None, op0=ALU.mult))
                    s.wait_ge(VE.sem, vneg)
                    expg[gcnt] = SC.bump(s.activation(
                        pdum[:, kc * 512:(kc + 1) * 512],
                        s_ps[:, bank * 512:(bank + 1) * 512],
                        AF.Exp, bias=smt[:, slot:slot + 1], scale=scale,
                        accum_out=sgall[:, col:col + 1]))
                    gcnt += 1

            exp1_last = expg[gcnt - 1]
            v.wait_ge(SC.sem, exp1_last)
            mg3 = mgall[:, :].rearrange("p (q g) -> p q g", g=4)
            k.vs(v.tensor_reduce(mall[:, :], mg3, axis=AX.X, op=ALU.max))
            last = None
            for g_ in range(4):
                last = v.scalar_tensor_tensor(
                    tm64[:, :].rearrange("p (q g) -> p q g", g=4)[:, :, g_],
                    mall[:, :], -1.0,
                    mg3[:, :, g_], op0=ALU.mult, op1=ALU.add)
            k.vs(last)
            vc4 = k.vs(v.tensor_scalar(tm64[:, :], tm64[:, :], scale, None,
                                       op0=ALU.mult))
            s.wait_ge(VE.sem, vc4)
            scm = SC.bump(s.activation(tm64[:, :], tm64[:, :], AF.Exp, bias=0.0,
                                       scale=1.0))
            v.wait_ge(SC.sem, scm)
            k.vs(v.tensor_mul(tm64[:, :], tm64[:, :], sgall[:, :]))
            k.vs(v.tensor_reduce(sall[:, :], tm64[:, :].rearrange("p (q g) -> p q g", g=4),
                                 axis=AX.X, op=ALU.add))

            s.wait_ge(VE.sem, VE.v)
            sln = SC.bump(s.activation(lns[:, :], sall[:, :], AF.Ln, bias=0.0,
                                       scale=1.0))
            v.wait_ge(SC.sem, sln)
            k.vs(v.tensor_scalar(pdum[:, 0:NT], lns[:, :], inv_scale, None,
                                 op0=ALU.mult))
            vw = k.vs(v.tensor_add(pdum[:, 0:NT], pdum[:, 0:NT], mall[:, :]))
            if kind == "l0f":
                vw = k.vs(v.tensor_sub(pdum[:, 0:NT], pdum[:, 0:NT],
                                       k.sb["lngsc_sb"][:, b * NT:(b + 1) * NT]))
            t.wait_ge(VE.sem, vw)
            t.wait_ge(SC.sem, SC.v)
            ptr = PE.bump(t.transpose(s_ps[0:16, 0:128], pdum[:, 0:NT],
                                      k.ident[:, :]))
            s.wait_ge(PE.sem, ptr)
            if h > 0:
                s.wait_ge(k.s_lm.sem, prev_negm_st)
            vng = SC.bump(s.activation(nm16[:, :], s_ps[0:16, 0:128], AF.Copy,
                                       bias=0.0, scale=-1.0))
            sy.wait_ge(SC.sem, vng)
            prev_negm_st = k.s_lm.bump(sy.dma_start(out=k.scr["negm"][:, :],
                                                    in_=nm16[:, :]), 16)
            ldm = k.s_lm.bump(sy.dma_start(
                out=qh[hb][64:65, :],
                in_=k.scr["negm"].rearrange("a j -> (a j)")[None, :]), 16)

            t.wait_ge(k.s_lm.sem, ldm)
            uexp = {}
            for qt4 in range(4):
                ub = uT[qt4 % 2]
                nkk = 16 if not causal else (4 * qt4 + 4)
                for kc in range(nkk):
                    bank = kc % 4
                    if qt4 == 0 and kc == 0:
                        t.wait_ge(SC.sem, max(exp1_last, vng))
                    if kc >= 4:
                        t.wait_ge(SC.sem, uexp[(qt4, kc - 4)])
                    elif qt4 > 0:
                        t.wait_ge(SC.sem, uexp[(qt4 - 1, prev_nkk - 4 + kc)])
                    pc = PE.bump(t.matmul(s_ps[:, bank * 512:(bank + 1) * 512],
                                          kh[hb][0:CK, kc * 128:(kc + 1) * 128],
                                          qh[hb][0:CK, qt4 * 512:(qt4 + 1) * 512],
                                          start=True, stop=True))
                    diag = causal and (kc >= 4 * qt4)
                    if diag:
                        v.wait_ge(PE.sem, pc)
                        r = kc - 4 * qt4
                        vst = VE.bump(v.scalar_tensor_tensor(
                            dstage[:, r * 512:(r + 1) * 512],
                            s_ps[:, bank * 512:(bank + 1) * 512],
                            scale, mtri[:, r * 512:(r + 1) * 512],
                            op0=ALU.mult, op1=ALU.add))
                        s.wait_ge(VE.sem, vst)
                        uexp[(qt4, kc)] = SC.bump(s.activation(
                            ub[:, kc * 512:(kc + 1) * 512],
                            dstage[:, r * 512:(r + 1) * 512],
                            AF.Exp, bias=0.0, scale=1.0))
                    else:
                        s.wait_ge(PE.sem, pc)
                        uexp[(qt4, kc)] = SC.bump(s.activation(
                            ub[:, kc * 512:(kc + 1) * 512],
                            s_ps[:, bank * 512:(bank + 1) * 512],
                            AF.Exp, bias=0.0, scale=scale))
                prev_nkk = nkk
                t.wait_ge(SC.sem, uexp[(qt4, nkk - 1)])
                if h > 0 or qt4 >= 2:
                    t.wait_ge(SC.sem, oev[qt4 % 2])
                obank = oav_ps[:, (qt4 % 2) * 512:(qt4 % 2) * 512 + 512]
                last = None
                for kc in range(nkk):
                    last = t.matmul(obank,
                                    vt[hb][:, kc * 64:(kc + 1) * 64],
                                    ub[:, kc * 512:(kc + 1) * 512],
                                    start=(kc == 0), stop=(kc == nkk - 1))
                pav = PE.bump(last)
                s.wait_ge(PE.sem, pav)
                if h == 0 and qt4 == 0:
                    oev = {}
                oev[qt4 % 2] = SC.bump(s.copy(
                    ofm[64 * (h % 2):64 * (h % 2) + 64, qt4 * 512:(qt4 + 1) * 512],
                    obank))
            free_at[h] = PE.v
            if h % 2 == 1:
                g.wait_ge(SC.sem, SC.v)
                stf = ST.bump(g.dma_start(
                    out=outscr[b][(h // 2) * 128:(h // 2) * 128 + 128, :],
                    in_=ofm[:, :]), 16)
                s.wait_ge(ST.sem, stf)
            if h + 1 < H:
                ld_t = ld_next
                aug_t, aug_l = aug_rows(h + 1, ld_t)
        g.wait_ge(ST.sem, ST.v)


def ln_tile(k, v, s, buf, xc, st1, sqd):
    VE, SC = k.s_ve, k.s_sc
    k.vs(v.reduce_sum(st1[:, 0:1], buf[:, :], axis=AX.X))
    vc = k.vs(v.tensor_scalar(st1[:, 1:2], st1[:, 0:1], -1.0 / D, None, op0=ALU.mult))
    s.wait_ge(VE.sem, vc)
    sc0 = SC.bump(s.activation(sqd[:, :], buf[:, :], AF.Square, bias=0.0,
                               scale=1.0, accum_out=st1[:, 2:3]))
    v.wait_ge(SC.sem, sc0)
    v.tensor_scalar(st1[:, 3:4], st1[:, 2:3], 1.0 / D, None, op0=ALU.mult)
    k.vs(v.tensor_mul(st1[:, 4:5], st1[:, 1:2], st1[:, 1:2]))
    k.vs(v.tensor_sub(st1[:, 3:4], st1[:, 3:4], st1[:, 4:5]))
    vc = k.vs(v.tensor_scalar(st1[:, 3:4], st1[:, 3:4], EPS, None, op0=ALU.add))
    s.wait_ge(VE.sem, vc)
    k.ss(s.activation(st1[:, 7:8], st1[:, 3:4], AF.Ln, bias=0.0, scale=1.0))
    sc1 = k.s_sc.bump(s.activation(st1[:, 4:5], st1[:, 7:8], AF.Exp, bias=0.0,
                                   scale=-0.5))
    v.wait_ge(SC.sem, sc1)
    k.vs(v.tensor_mul(st1[:, 5:6], st1[:, 4:5], st1[:, 4:5]))
    k.vs(v.tensor_mul(st1[:, 5:6], st1[:, 5:6], st1[:, 3:4]))
    k.vs(v.tensor_scalar(st1[:, 5:6], st1[:, 5:6], -0.5, 1.5, op0=ALU.mult, op1=ALU.add))
    k.vs(v.tensor_mul(st1[:, 6:7], st1[:, 4:5], st1[:, 5:6]))
    k.vs(v.tensor_scalar(xc[:, :], buf[:, :], st1[:, 1:2], None, op0=ALU.add))
    return k.s_ve.bump(v.tensor_scalar(xc[:, :], xc[:, :], st1[:, 6:7], None,
                                       op0=ALU.mult))


def emit_combine_l0(k: K, b: int):
    nc = k.nc
    LD, PE, VE, SC, ST = k.s_ld, k.s_pe, k.s_ve, k.s_sc, k.s_st
    v, s, t, g, sy = k.ve, k.sc, k.te, k.gp, k.sy
    with ExitStack() as ctx:
        a0_sb = k.tmp(ctx, "cba", [128, NC * 512])
        f0_sb = k.tmp(ctx, "cbf", [128, NC * 512])
        x_sb = k.tmp(ctx, "cbx", [128, NC * 512])
        x1stg = k.tmp(ctx, "cbs", [128, NC * 512])
        lnt = k.tmp(ctx, "cbl", [128, D])
        xnt = k.tmp(ctx, "cbn", [128, D])
        sqd = k.tmp(ctx, "cbq", [128, D])
        st1 = k.tmp(ctx, "cbst", [128, 12])
        xn1stg = k.tmp(ctx, "cbg", [128, D])
        ps_t = k.ptmp(ctx, "cbp", [128, 1024])
        ps_t2 = k.ptmp(ctx, "cbp2", [128, 1024])

        for ts4 in range(4):
            sl = slice(ts4 * 512, (ts4 + 1) * 512)
            l1 = LD.bump(sy.dma_start(
                out=a0_sb[:, :].rearrange("p (c t) -> p c t", c=NC),
                in_=k.scr["qT"][b].rearrange("(c p) t -> p c t", p=128)[:, :, sl]), 16)
            l2 = LD.bump(sy.dma_start(
                out=f0_sb[:, :].rearrange("p (c t) -> p c t", c=NC),
                in_=k.scr["kT"][b].rearrange("(c p) t -> p c t", p=128)[:, :, sl]), 16)
            l3 = LD.bump(sy.dma_start(
                out=x_sb[:, :].rearrange("p (c t) -> p c t", c=NC),
                in_=k.scr["xT"][b].rearrange("(c p) t -> p c t", p=128)[:, :, sl]), 16)
            v.wait_ge(LD.sem, max(l1, l2, l3))
            lastv = None
            for c in range(NC):
                csl = slice(c * 512, (c + 1) * 512)
                lastv = v.tensor_add(x1stg[:, csl], x_sb[:, csl], a0_sb[:, csl])
            k.vs(lastv)
            vx1 = None
            for c in range(NC):
                csl = slice(c * 512, (c + 1) * 512)
                vx1 = v.tensor_add(x1stg[:, csl], x1stg[:, csl], f0_sb[:, csl])
            VE.bump(vx1)
            vx1c = k.vflush()
            g.wait_ge(VE.sem, vx1c)
            st1c = ST.bump(g.dma_start(
                out=k.scr["x1T"][b].rearrange("(c p) t -> p c t", p=128)[:, :, sl],
                in_=x1stg[:, :].rearrange("p (c t) -> p c t", c=NC)), 16)
            for tc in range(4):
                t.wait_ge(VE.sem, vx1c)
                if not (ts4 == 0 and tc == 0):
                    t.wait_ge(SC.sem, sc_ev)
                last = None
                for c in range(NC):
                    last = t.transpose(ps_t[:, c * 128:(c + 1) * 128],
                                       x1stg[:, c * 512 + tc * 128:c * 512 + tc * 128 + 128],
                                       k.ident[:, :])
                ptc = PE.bump(last)
                s.wait_ge(PE.sem, ptc)
                last = None
                for c in range(NC):
                    last = s.copy(lnt[:, c * 128:(c + 1) * 128],
                                  ps_t[:, c * 128:(c + 1) * 128])
                sc_ev = SC.bump(last)
                v.wait_ge(SC.sem, sc_ev)
                vxn = ln_tile(k, v, s, lnt, xnt, st1, sqd)
                t.wait_ge(VE.sem, vxn)
                if not (ts4 == 0 and tc == 0):
                    t.wait_ge(SC.sem, sc_ev2)
                last = None
                for c in range(NC):
                    last = t.transpose(ps_t2[:, c * 128:(c + 1) * 128],
                                       xnt[:, c * 128:(c + 1) * 128], k.ident[:, :])
                ptc2 = PE.bump(last)
                s.wait_ge(PE.sem, ptc2)
                if not (ts4 == 0 and tc == 0):
                    s.wait_ge(ST.sem, st_xn)
                last = None
                for c in range(NC):
                    last = s.copy(xn1stg[:, c * 128:(c + 1) * 128],
                                  ps_t2[:, c * 128:(c + 1) * 128])
                sc_ev2 = SC.bump(last)
                g.wait_ge(SC.sem, sc_ev2)
                st_xn = ST.bump(g.dma_start(
                    out=k.scr["xn1T"][b].rearrange("(c p) t -> p c t", p=128)[
                        :, :, ts4 * 512 + tc * 128:ts4 * 512 + tc * 128 + 128],
                    in_=xn1stg[:, :].rearrange("p (c t) -> p c t", c=NC)), 16)
            sy.wait_ge(PE.sem, PE.v)
            sy.wait_ge(ST.sem, st1c)
        g.wait_ge(ST.sem, ST.v)


def emit_moe(k: K):
    nc = k.nc
    LD, PE, VE, SC, ST = k.s_ld, k.s_pe, k.s_ve, k.s_sc, k.s_st
    v, s, t, g, sy = k.ve, k.sc, k.te, k.gp, k.sy
    with ExitStack() as ctx:
        xn_sb = [k.tmp(ctx, "mx", [128, NC * 512]) for _ in range(2)]
        wg_sb = k.tmp(ctx, "mwg", [128, NC * 8])
        lgs = k.tmp(ctx, "mlg", [8, 512])
        lgtm = k.tmp(ctx, "mlt", [128, 8])
        cw = k.tmp(ctx, "mcw", [128, 16])
        coeff = k.tmp(ctx, "mcf", [128, 32 * 8])
        crow = k.tmp(ctx, "mcr", [1, 512])
        cbc = k.tmp(ctx, "mcb", [128, 512])
        w1_sb = k.tmp(ctx, "mw1", [128, NC * 2048])
        w2_sb = k.tmp(ctx, "mw2", [128, 16 * D])
        b1_sb = k.tmp(ctx, "mb1", [128, 16])
        b2_sb = k.tmp(ctx, "mb2", [128, 8])
        h_sb = k.tmp(ctx, "mh", [128, 16 * 512])
        mstg = [k.tmp(ctx, "mst", [128, 512]) for _ in range(2)]
        ps_g = k.ptmp(ctx, "mpg", [128, 512])
        ps_t = k.ptmp(ctx, "mpt", [128, 128])
        ps_h = k.ptmp(ctx, "mph", [128, 1024])
        ps_e = k.ptmp(ctx, "mpe", [128, 1024])

        ldg = LD.bump(sy.dma_start(
            out=wg_sb[:, :].rearrange("p (c o) -> p c o", c=NC),
            in_=k.ins["w1g"].rearrange("(c p) o -> p c o", p=128)), 16)

        for tb in range(8):
            b, ts4 = tb // 4, tb % 4
            sl = slice(ts4 * 512, (ts4 + 1) * 512)
            if tb > 0:
                sy.wait_ge(PE.sem, PE.v)
            lx = LD.bump(sy.dma_start(
                out=xn_sb[0][:, :].rearrange("p (c t) -> p c t", c=NC),
                in_=k.scr["xn1T"][b].rearrange("(c p) t -> p c t", p=128)[:, :, sl]), 16)
            t.wait_ge(LD.sem, max(ldg, lx))
            if tb > 0:
                t.wait_ge(SC.sem, lg_ev)
            last = None
            for c in range(NC):
                last = t.matmul(ps_g[0:8, :], wg_sb[:, c * 8:(c + 1) * 8],
                                xn_sb[0][:, c * 512:(c + 1) * 512],
                                start=(c == 0), stop=(c == NC - 1))
            pg = PE.bump(last)
            s.wait_ge(PE.sem, pg)
            lg_ev = SC.bump(s.copy(lgs[:, :], ps_g[0:8, :]))
            v.wait_ge(SC.sem, lg_ev)
            vb = k.vs(v.tensor_scalar(lgs[:, :], lgs[:, :], k.bias["b1g"][:, 0:1],
                                      None, op0=ALU.add))
            for tc in range(4):
                t.wait_ge(VE.sem, vb)
                if not (tb == 0 and tc == 0):
                    t.wait_ge(SC.sem, tr_ev)
                ptc = PE.bump(t.transpose(ps_t[0:128, 0:8],
                                          lgs[:, tc * 128:(tc + 1) * 128],
                                          k.ident[0:8, 0:8]))
                s.wait_ge(PE.sem, ptc)
                tr_ev = SC.bump(s.copy(lgtm[:, :], ps_t[0:128, 0:8]))
                v.wait_ge(SC.sem, tr_ev)
                k.vs(v.reduce_max(cw[:, 0:1], lgtm[:, :], axis=AX.X))
                k.vs(v.tensor_scalar(cw[:, 8:16], lgtm[:, :], cw[:, 0:1],
                                     None, op0=ALU.is_ge))
                k.vs(v.reduce_sum(cw[:, 1:2], cw[:, 8:16], axis=AX.X))
                k.vs(v.scalar_tensor_tensor(cbc[:, 0:8], cw[:, 8:16], -BIG,
                                            lgtm[:, :], op0=ALU.mult, op1=ALU.add))
                k.vs(v.reduce_max(cw[:, 2:3], cbc[:, 0:8], axis=AX.X))
                k.vs(v.tensor_scalar(cw[:, 3:4], cw[:, 1:2], 2.0, None, op0=ALU.is_lt))
                k.vs(v.tensor_scalar(cbc[:, 8:16], lgtm[:, :], cw[:, 2:3], None,
                                     op0=ALU.is_ge))
                k.vs(v.tensor_sub(cbc[:, 8:16], cbc[:, 8:16], cw[:, 8:16]))
                k.vs(v.scalar_tensor_tensor(cbc[:, 8:16], cbc[:, 8:16], cw[:, 3:4],
                                            cw[:, 8:16], op0=ALU.mult, op1=ALU.add))
                vng = k.vs(v.tensor_scalar(cw[:, 4:5], cw[:, 0:1], -1.0, None,
                                           op0=ALU.mult))
                s.wait_ge(VE.sem, vng)
                swx = SC.bump(s.activation(cbc[:, 16:24], lgtm[:, :], AF.Exp,
                                           bias=cw[:, 4:5], scale=1.0))
                v.wait_ge(SC.sem, swx)
                k.vs(v.tensor_mul(cbc[:, 16:24], cbc[:, 16:24], cbc[:, 8:16]))
                k.vs(v.reduce_sum(cw[:, 5:6], cbc[:, 16:24], axis=AX.X))
                k.vs(v.reciprocal(cw[:, 6:7], cw[:, 5:6]))
                k.vs(v.tensor_scalar(coeff[:, (tb * 4 + tc) * 8:(tb * 4 + tc) * 8 + 8],
                                     cbc[:, 16:24], cw[:, 6:7], None, op0=ALU.mult))

        vcoeff_done = VE.v
        t.wait_ge(VE.sem, vcoeff_done)
        t.wait_ge(SC.sem, SC.v)
        for e in range(8):
            sy.wait_ge(PE.sem, PE.v)
            lw1 = LD.bump(sy.dma_start(
                out=w1_sb[:, :].rearrange("p (c o) -> p c o", c=NC),
                in_=k.ins["w1e1"][e].rearrange("(c p) o -> p c o", p=128)), 16)
            lw2 = LD.bump(sy.dma_start(
                out=w2_sb[:, :].rearrange("p (c o) -> p c o", c=16),
                in_=k.ins["w1e2"][e].rearrange("(c p) o -> p c o", p=128)), 16)
            lb1 = LD.bump(sy.dma_start(out=b1_sb[:, :], in_=k.ins["b1e1"][e]), 16)
            lb2 = LD.bump(sy.dma_start(out=b2_sb[:, :], in_=k.ins["b1e2"][e]), 16)
            for tb in range(8):
                b, ts4 = tb // 4, tb % 4
                sl = slice(ts4 * 512, (ts4 + 1) * 512)
                xb = xn_sb[tb % 2]
                if e > 0 or tb >= 2:
                    sy.wait_ge(PE.sem, PE.v)
                lx = LD.bump(sy.dma_start(
                    out=xb[:, :].rearrange("p (c t) -> p c t", c=NC),
                    in_=k.scr["xn1T"][b].rearrange("(c p) t -> p c t", p=128)[:, :, sl]), 16)
                for tc in range(4):
                    if not (e == 0 and tb == 0 and tc == 0):
                        t.wait_ge(SC.sem, cr_ev)
                    ptc = PE.bump(t.transpose(
                        ps_t[0:1, 0:128],
                        coeff[:, (tb * 4 + tc) * 8 + e:(tb * 4 + tc) * 8 + e + 1],
                        k.ident[:, :]))
                    s.wait_ge(PE.sem, ptc)
                    cr_ev = SC.bump(s.copy(crow[0:1, tc * 128:(tc + 1) * 128],
                                           ps_t[0:1, 0:128]))
                t.wait_ge(SC.sem, cr_ev)
                if not (e == 0 and tb == 0):
                    t.wait_ge(VE.sem, cb_ev)
                pbc = PE.bump(t.matmul(ps_g[:, :], k.ones_row[:, :], crow[0:1, :],
                                       start=True, stop=True))
                v.wait_ge(PE.sem, pbc)
                cb_ev = k.vs(v.tensor_copy(cbc[:, :], ps_g[:, :]))
                t.wait_ge(LD.sem, max(lw1, lb1, lx))
                h_prev = h_ev if not (e == 0 and tb == 0) else None
                h_ev = {}
                for ffc in range(16):
                    bank = ffc % 2
                    if ffc >= 2:
                        t.wait_ge(SC.sem, h_ev[ffc - 2])
                    elif h_prev is not None:
                        t.wait_ge(SC.sem, h_prev[14 + ffc])
                    last = None
                    for c in range(NC):
                        last = t.matmul(ps_h[:, bank * 512:bank * 512 + 512],
                                        w1_sb[:, c * 2048 + ffc * 128:
                                              c * 2048 + ffc * 128 + 128],
                                        xb[:, c * 512:(c + 1) * 512],
                                        start=(c == 0), stop=(c == NC - 1))
                    ph = PE.bump(last)
                    s.wait_ge(PE.sem, ph)
                    if ffc == 0 and h_prev is not None:
                        s.wait_ge(PE.sem, eo_pe_last)
                    h_ev[ffc] = SC.bump(s.activation(
                        h_sb[:, ffc * 512:(ffc + 1) * 512],
                        ps_h[:, bank * 512:bank * 512 + 512],
                        AF.Relu, bias=b1_sb[:, ffc:ffc + 1], scale=1.0))
                t.wait_ge(LD.sem, max(lw2, lb2))
                t.wait_ge(SC.sem, h_ev[15])
                for dc in range(8):
                    bank = dc % 2
                    if dc >= 2:
                        t.wait_ge(VE.sem, m_ev[dc - 2])
                    elif not (e == 0 and tb == 0):
                        t.wait_ge(VE.sem, m_ev_prev[6 + dc])
                    last = None
                    for ffc in range(16):
                        last = t.matmul(ps_e[:, bank * 512:bank * 512 + 512],
                                        w2_sb[:, ffc * D + dc * 128:
                                              ffc * D + dc * 128 + 128],
                                        h_sb[:, ffc * 512:(ffc + 1) * 512],
                                        start=(ffc == 0), stop=(ffc == 15))
                    pe_ = PE.bump(last)
                    if dc == 0:
                        m_ev = {}
                    v.wait_ge(PE.sem, pe_)
                    if e > 0 or tb > 0 or dc >= 2:
                        STP2 = k.s_sta if dc % 2 == 0 else k.s_stb
                        v.wait_ge(STP2.sem, st_prev.get(dc, 0) if dc < 2 else st_d[dc - 2])
                    if dc == 0:
                        st_d = {}
                    VE.bump(v.scalar_tensor_tensor(
                        mstg[dc % 2][:, :], ps_e[:, bank * 512:bank * 512 + 512],
                        b2_sb[:, dc:dc + 1], cbc[:, :], op0=ALU.add, op1=ALU.mult))
                    m_ev[dc] = k.vflush()
                    g.wait_ge(VE.sem, m_ev[dc])
                    STP = k.s_sta if dc % 2 == 0 else k.s_stb
                    st_d[dc] = STP.bump(g.dma_start(
                        out=k.scr["mixT"][b][dc * 128:(dc + 1) * 128, sl],
                        in_=mstg[dc % 2][:, :],
                        accum_op=(ALU.add if e > 0 else ALU.bypass)), 16)
                eo_pe_last = PE.v
                m_ev_prev = dict(m_ev)
                st_prev = {0: st_d[6], 1: st_d[7]}
        g.wait_ge(ST.sem, ST.v)
        g.wait_ge(k.s_sta.sem, k.s_sta.v)
        g.wait_ge(k.s_stb.sem, k.s_stb.v)


def emit_final(k: K):
    nc = k.nc
    LD, PE, VE, SC, ST = k.s_ld, k.s_pe, k.s_ve, k.s_sc, k.s_st
    v, s, t, g, sy = k.ve, k.sc, k.te, k.gp, k.sy
    with ExitStack() as ctx:
        mix_sb = k.tmp(ctx, "fmx", [128, NC * 512])
        xn_sb = k.tmp(ctx, "fxn", [128, NC * 512])
        x1_sb = k.tmp(ctx, "fx1", [128, NC * 512])
        a1_sb = k.tmp(ctx, "fa1", [128, NC * 512])
        lnt = k.tmp(ctx, "fln", [128, D])
        xnt = k.tmp(ctx, "fxt", [128, D])
        sqd = k.tmp(ctx, "fsq", [128, D])
        st1 = k.tmp(ctx, "fst", [128, 12])
        ynT = k.tmp(ctx, "fyt", [128, D])
        outT = k.tmp(ctx, "fot", [128, D])
        outtm = k.tmp(ctx, "fom", [128, D])
        ps_1 = k.ptmp(ctx, "fp1", [128, 1024])
        ps_2 = k.ptmp(ctx, "fp2", [128, 1024])
        ps_3 = k.ptmp(ctx, "fp3", [128, 1024])

        for b in range(NB):
            for ts4 in range(4):
                sl = slice(ts4 * 512, (ts4 + 1) * 512)
                if not (b == 0 and ts4 == 0):
                    sy.wait_ge(VE.sem, VE.v)
                    sy.wait_ge(PE.sem, PE.v)
                l1_ = LD.bump(sy.dma_start(
                    out=mix_sb[:, :].rearrange("p (c t) -> p c t", c=NC),
                    in_=k.scr["mixT"][b].rearrange("(c p) t -> p c t", p=128)[:, :, sl]), 16)
                l2 = LD.bump(sy.dma_start(
                    out=xn_sb[:, :].rearrange("p (c t) -> p c t", c=NC),
                    in_=k.scr["xn1T"][b].rearrange("(c p) t -> p c t", p=128)[:, :, sl]), 16)
                l3 = LD.bump(sy.dma_start(
                    out=x1_sb[:, :].rearrange("p (c t) -> p c t", c=NC),
                    in_=k.scr["x1T"][b].rearrange("(c p) t -> p c t", p=128)[:, :, sl]), 16)
                l4 = LD.bump(sy.dma_start(
                    out=a1_sb[:, :].rearrange("p (c t) -> p c t", c=NC),
                    in_=k.scr["a1T"][b].rearrange("(c p) t -> p c t", p=128)[:, :, sl]), 16)
                v.wait_ge(LD.sem, max(l1_, l2, l3, l4))
                vy = VE.bump(v.tensor_add(mix_sb[:, :], mix_sb[:, :], xn_sb[:, :]))
                for tc in range(4):
                    csl = lambda c: slice(c * 512 + tc * 128, c * 512 + tc * 128 + 128)
                    t.wait_ge(VE.sem, vy)
                    if not (b == 0 and ts4 == 0 and tc == 0):
                        t.wait_ge(SC.sem, ev1)
                    last = None
                    for c in range(NC):
                        last = t.transpose(ps_1[:, c * 128:(c + 1) * 128],
                                           mix_sb[:, csl(c)], k.ident[:, :])
                    pt1 = PE.bump(last)
                    s.wait_ge(PE.sem, pt1)
                    last = None
                    for c in range(NC):
                        last = s.copy(lnt[:, c * 128:(c + 1) * 128],
                                      ps_1[:, c * 128:(c + 1) * 128])
                    ev1 = SC.bump(last)
                    v.wait_ge(SC.sem, ev1)
                    vxn = ln_tile(k, v, s, lnt, xnt, st1, sqd)
                    t.wait_ge(VE.sem, vxn)
                    if not (b == 0 and ts4 == 0 and tc == 0):
                        t.wait_ge(SC.sem, ev2)
                    last = None
                    for c in range(NC):
                        last = t.transpose(ps_2[:, c * 128:(c + 1) * 128],
                                           xnt[:, c * 128:(c + 1) * 128], k.ident[:, :])
                    pt2 = PE.bump(last)
                    s.wait_ge(PE.sem, pt2)
                    last = None
                    for c in range(NC):
                        last = s.copy(ynT[:, c * 128:(c + 1) * 128],
                                      ps_2[:, c * 128:(c + 1) * 128])
                    ev2 = SC.bump(last)
                    v.wait_ge(SC.sem, ev2)
                    lastv = None
                    for c in range(NC):
                        lastv = v.tensor_scalar(outT[:, c * 128:(c + 1) * 128],
                                        ynT[:, c * 128:(c + 1) * 128],
                                        k.bias["gamma"][:, c:c + 1], None, op0=ALU.mult)
                    k.vs(lastv)
                    for c in range(NC):
                        lastv = v.scalar_tensor_tensor(outT[:, c * 128:(c + 1) * 128],
                                               outT[:, c * 128:(c + 1) * 128],
                                               k.bias["beta"][:, c:c + 1],
                                               x1_sb[:, csl(c)],
                                               op0=ALU.add, op1=ALU.add)
                    k.vs(lastv)
                    vo = None
                    for c in range(NC):
                        vo = v.tensor_add(outT[:, c * 128:(c + 1) * 128],
                                          outT[:, c * 128:(c + 1) * 128],
                                          a1_sb[:, csl(c)])
                    vou = VE.bump(vo)
                    t.wait_ge(VE.sem, vou)
                    if not (b == 0 and ts4 == 0 and tc == 0):
                        t.wait_ge(SC.sem, ev3)
                    last = None
                    for c in range(NC):
                        last = t.transpose(ps_3[:, c * 128:(c + 1) * 128],
                                           outT[:, c * 128:(c + 1) * 128],
                                           k.ident[:, :])
                    pt3 = PE.bump(last)
                    s.wait_ge(PE.sem, pt3)
                    if not (b == 0 and ts4 == 0 and tc == 0):
                        s.wait_ge(ST.sem, sto)
                    last = None
                    for c in range(NC):
                        last = s.copy(outtm[:, c * 128:(c + 1) * 128],
                                      ps_3[:, c * 128:(c + 1) * 128])
                    ev3 = SC.bump(last)
                    g.wait_ge(SC.sem, ev3)
                    t0 = ts4 * 512 + tc * 128
                    sto = ST.bump(g.dma_start(
                        out=k.out[b, t0:t0 + 128, :], in_=outtm[:, :]), 16)
        g.wait_ge(ST.sem, ST.v)


def build_program(dbg=False):
    nc = bass.Bass("TRN2", target_bir_lowering=False)
    with ExitStack() as ctx:
        k = K(nc, ctx, dbg)
        declare_io(k)
        k.s_ld = k.sem("s_ld")
        k.s_st = k.sem("s_st")
        k.s_pe = k.sem("s_pe")
        k.s_ve = k.sem("s_ve")
        k.s_sc = k.sem("s_sc")
        k.s_gp = k.sem("s_gp")
        k.s_ld2 = k.sem("s_ld2")
        k.s_lda = k.sem("s_lda")
        k.s_ldb = k.sem("s_ldb")
        k.s_la = k.sem("s_la")
        k.s_lm = k.sem("s_lm")
        k.s_sta = k.sem("s_sta")
        k.s_stb = k.sem("s_stb")
        k.s_pe2 = k.sem("s_pe2")
        k.s_sc2 = k.sem("s_sc2")
        k.s_ve2 = k.sem("s_ve2")
        k.s_st2 = k.sem("s_st2")
        k.sbuf("bst_sb", [1, 32])
        k.sbuf("gsccol_sb", [128, 2 * NT])
        k.sbuf("lngsc_sb", [128, 2 * NT])
        k.sbuf("flush_sb", [1, 8])

        emit_init(k)
        k.barrier()
        for b in range(NB):
            with nc.sbuf_tensor(f"z_sb_{b}", [1, S], F32) as zt:
                k.sb["z_sb"] = zt
                emit_p0(k, b)
                k.barrier()
                emit_bisect(k, b)
                k.barrier()
        for b in range(NB):
            emit_proj(k, b, k.scr["xnT"][b], [
                dict(w="w0aq", bias="b0aq", nout=D, dst=k.scr["qT"][b], dst_tm=None),
                dict(w="w0ak", bias="b0ak", nout=D, dst=k.scr["kT"][b], dst_tm=None),
                dict(w="w0av", bias="b0av", nout=D, dst=None,
                     dst_tm=k.scr["vTm"][b], col0=0),
            ])
            k.barrier()
            emit_attn(k, b, "l0a")
            k.barrier()
            emit_proj(k, b, k.scr["xnT"][b], [
                dict(w="w0fq", bias=None, nout=D, dst=k.scr["qT"][b], dst_tm=None),
                dict(w="w0fk", bias=None, nout=D, dst=k.scr["kT"][b], dst_tm=None),
                dict(w="w0fv", bias=None, nout=D, dst=None,
                     dst_tm=k.scr["vTm"][b], col0=0),
            ])
            k.barrier()
            emit_attn(k, b, "l0f")
            k.barrier()
            emit_proj(k, b, k.scr["oaT"][b], [
                dict(w="w0ao", bias="b0ao", nout=D, dst=k.scr["qT"][b], dst_tm=None)])
            k.barrier()
            emit_proj(k, b, k.scr["ofT"][b], [
                dict(w="w0fo", bias=None, nout=D, dst=k.scr["kT"][b], dst_tm=None)])
            k.barrier()
            emit_combine_l0(k, b)
            k.barrier()
            emit_proj(k, b, k.scr["xn1T"][b], [
                dict(w="w1aq", bias="b1aq", nout=D, dst=k.scr["qT"][b], dst_tm=None),
                dict(w="w1ak", bias="b1ak", nout=256, dst=k.scr["kT"][b], dst_tm=None),
                dict(w="w1av", bias="b1av", nout=256, dst=None,
                     dst_tm=k.scr["vTm"][b], col0=0),
            ])
            k.barrier()
            emit_attn(k, b, "l1")
            k.barrier()
            emit_proj(k, b, k.scr["oaT"][b], [
                dict(w="w1ao", bias="b1ao", nout=D, dst=k.scr["a1T"][b], dst_tm=None)])
            k.barrier()
        emit_moe(k)
        k.barrier()
        emit_final(k)
    return nc


def finalize(k: K):
    nc = k.nc
    with ExitStack() as ctx:
        zt = ctx.enter_context(nc.sbuf_tensor("fin_z", [128, D], F32))
        vc = k.s_ve.bump(k.ve.memset(zt[:, :], 0.0))
        k.gp.wait_ge(k.s_ve.sem, vc)
        ST = k.s_st
        for b in range(NB):
            for tt in range(NT):
                ST.bump(k.gp.dma_start(out=k.out[b, tt * 128:(tt + 1) * 128, :],
                                       in_=zt[:, :]), 16)
        k.gp.wait_ge(ST.sem, ST.v)


def prep_inputs(full):
    T = lambda a: np.ascontiguousarray(np.asarray(a).T)
    C = lambda a: np.ascontiguousarray(np.asarray(a), dtype=np.float32)
    bias_cols = lambda a, nc_: np.ascontiguousarray(np.asarray(a).reshape(nc_, 128).T)
    base = {
        "w0aq": T(full["L0_attn_Wq"]), "w0ak": T(full["L0_attn_Wk"]),
        "w0av": T(full["L0_attn_Wv"]), "w0ao": T(full["L0_attn_Wo"]),
        "b0aq": bias_cols(full["L0_attn_bq"], NC), "b0ak": bias_cols(full["L0_attn_bk"], NC),
        "b0av": bias_cols(full["L0_attn_bv"], NC), "b0ao": bias_cols(full["L0_attn_bo"], NC),
        "w0fg": np.ascontiguousarray(np.repeat(bias_cols(full["L0_ffn_Wg"][:, 0], NC), 2, axis=1)),
        "w0fq": C(full["L0_ffn_Wq"]), "w0fk": C(full["L0_ffn_Wk"]),
        "w0fv": C(full["L0_ffn_Wv"]), "w0fo": C(full["L0_ffn_Wo"]),
        "w1aq": C(full["L1_attn_Wq"]), "w1ak": C(full["L1_attn_Wk"]),
        "w1av": C(full["L1_attn_Wv"]), "w1ao": C(full["L1_attn_Wo"]),
        "b1aq": bias_cols(full["L1_attn_bq"], NC), "b1ao": bias_cols(full["L1_attn_bo"], NC),
        "b1ak": bias_cols(full["L1_attn_bk"], 2), "b1av": bias_cols(full["L1_attn_bv"], 2),
        "slopes": C(full["L1_attn_slopes"]).reshape(1, H),
        "w1g": C(full["L1_ffn_Wg"]), "b1g": C(full["L1_ffn_bg"]).reshape(8, 1),
        "w1e1": C(full["L1_ffn_W1"]),
        "b1e1": np.ascontiguousarray(np.asarray(full["L1_ffn_b1"]).reshape(8, 16, 128).transpose(0, 2, 1)),
        "w1e2": C(full["L1_ffn_W2"]),
        "b1e2": np.ascontiguousarray(np.asarray(full["L1_ffn_b2"]).reshape(8, 8, 128).transpose(0, 2, 1)),
        "gamma": bias_cols(full["L1_ffn_gamma"], NC),
        "beta": bias_cols(full["L1_ffn_beta"], NC),
    }
    x = np.asarray(full["L0_attn_x"], dtype=np.float32)
    maps = []
    for c in range(8):
        m = dict(base)
        m["x"] = np.ascontiguousarray(x[2 * c:2 * c + 2])
        maps.append(m)
    return maps


def kernel(**inputs):
    maps = prep_inputs(inputs)
    nc = build_program(dbg=False)
    res = run_bass_kernel_spmd(nc, maps, core_ids=list(range(8)))
    return np.concatenate([r["out"] for r in res.results], axis=0)


# version 13
# speedup vs baseline: 1.0064x; 1.0064x over previous
import numpy as np
from contextlib import ExitStack

import concourse.bass as bass
import concourse.mybir as mybir
from concourse.bass_utils import run_bass_kernel_spmd

F32 = mybir.dt.float32
AF = mybir.ActivationFunctionType
ALU = mybir.AluOpType
AX = mybir.AxisListType

NB = 2
S = 2048
D = 1024
NC = D // 128
NT = S // 128
H = 16
DH = 64
BIG = 1.0e30
NEGAUG = -8.0e30
EPS = 1e-5
N_BISECT = 45


class Ctr:

    def __init__(self, sem):
        self.sem = sem
        self.v = 0

    def bump(self, inst, n=1):
        inst.then_inc(self.sem, n)
        self.v += n
        return self.v


class K:

    def __init__(self, nc, ctx, dbg):
        self.nc = nc
        self.ctx = ctx
        self.dbg = dbg
        self.sy = nc.sync
        self.ve = nc.vector
        self.sc = nc.scalar
        self.te = nc.tensor
        self.gp = nc.gpsimd
        self.ins = {}
        self.scr = {}
        self.sb = {}

    _uid = 0

    def sbuf(self, name, shape, dtype=F32):
        t = self.ctx.enter_context(self.nc.sbuf_tensor(name, list(shape), dtype))
        self.sb[name] = t
        return t

    def tmp(self, ctx, name, shape, dtype=F32):
        K._uid += 1
        return ctx.enter_context(self.nc.sbuf_tensor(f"{name}_{K._uid}", list(shape), dtype))

    def ptmp(self, ctx, name, shape, dtype=F32):
        K._uid += 1
        return ctx.enter_context(self.nc.psum_tensor(f"{name}_{K._uid}", list(shape), dtype))

    def psum(self, name, shape, dtype=F32):
        return self.ctx.enter_context(self.nc.psum_tensor(name, list(shape), dtype))

    _fl = 0

    def vflush(self):
        K._fl = (K._fl + 1) % 4
        self.ve.wait_ge(self.s_ve.sem, self.flush_c0)
        c = self.s_ve.bump(self.ve.tensor_copy(
            self.sb["flush_sb"][0:1, 4 + K._fl:5 + K._fl],
            self.sb["flush_sb"][0:1, 0:1]))
        return c

    def vs(self, inst):
        c = self.s_ve.bump(inst)
        self.ve.wait_ge(self.s_ve.sem, c)
        return c

    def ss(self, inst):
        c = self.s_sc.bump(inst)
        self.sc.wait_ge(self.s_sc.sem, c)
        return c

    def sem(self, name):
        return Ctr(self.ctx.enter_context(self.nc.semaphore(name)))

    def barrier(self):
        self.nc.all_engine_barrier()


def declare_io(k: K):
    nc = k.nc
    def inp(name, shape):
        k.ins[name] = nc.dram_tensor(name, list(shape), F32, kind="ExternalInput").ap()
    inp("x", [NB, S, D])
    for w in ("w0aq", "w0ak", "w0av", "w0ao"):
        inp(w, [D, D])
    for b in ("b0aq", "b0ak", "b0av", "b0ao"):
        inp(b, [128, NC])
    inp("w0fg", [128, 2 * NC])
    for w in ("w0fq", "w0fk", "w0fv", "w0fo"):
        inp(w, [D, D])
    inp("w1aq", [D, D]); inp("w1ao", [D, D])
    inp("w1ak", [D, 256]); inp("w1av", [D, 256])
    inp("b1aq", [128, NC]); inp("b1ao", [128, NC])
    inp("b1ak", [128, 2]); inp("b1av", [128, 2])
    inp("slopes", [1, H])
    inp("w1g", [D, 8]); inp("b1g", [8, 1])
    inp("w1e1", [8, D, 2 * D]); inp("b1e1", [8, 128, 16])
    inp("w1e2", [8, 2 * D, D]); inp("b1e2", [8, 128, 8])
    inp("gamma", [128, NC]); inp("beta", [128, NC])

    k.out = nc.dram_tensor("out", [NB, S, D], F32, kind="ExternalOutput").ap()

    kind = "ExternalOutput" if k.dbg else "Internal"
    def scr(name, shape):
        k.scr[name] = nc.dram_tensor(name, list(shape), F32, kind=kind).ap()
    scr("xT", [NB, D, S])
    scr("xnT", [NB, D, S])
    scr("qT", [NB, D, S])
    scr("kT", [NB, D, S])
    scr("vTm", [NB, S, D])
    scr("oaT", [NB, D, S])
    scr("ofT", [NB, D, S])
    scr("x1T", [NB, D, S])
    scr("xn1T", [NB, D, S])
    scr("a1T", [NB, D, S])
    scr("mixT", [NB, D, S])
    scr("augr", [NB, S])
    scr("negm", [16, 128])
    if k.dbg:
        scr("attn_dbg", [NB, 4, S])
    if k.dbg:
        scr("zdbg", [NB, 8, S])


def emit_init(k: K):
    ident = k.sbuf("ident", [128, 128])
    ones_row = k.sbuf("ones_row", [1, 128])
    onescol = k.sbuf("onescol", [128, 1])
    r512 = k.sbuf("r512", [128, 512])
    slop = k.sbuf("slop", [1, H])
    bias_sb = {}
    for nm in ("b0aq", "b0ak", "b0av", "b0ao", "b1aq", "b1ao", "gamma", "beta"):
        bias_sb[nm] = k.sbuf(nm + "_sb", [128, NC])
    for nm in ("b1ak", "b1av"):
        bias_sb[nm] = k.sbuf(nm + "_sb", [128, 2])
    bias_sb["b1g"] = k.sbuf("b1g_sb", [8, 1])
    bias_sb["w0fg"] = k.sbuf("w0fg_sb", [128, 2 * NC])
    k.bias = bias_sb

    LD, GPc = k.s_ld, k.s_gp
    for nm in ("b0aq", "b0ak", "b0av", "b0ao", "b1aq", "b1ao", "gamma", "beta",
               "b1ak", "b1av", "b1g", "w0fg"):
        c = LD.bump(k.sy.dma_start(out=bias_sb[nm][:, :], in_=k.ins[nm]), 16)
    c = LD.bump(k.sy.dma_start(out=slop[:, :], in_=k.ins["slopes"]), 16)
    ld_done = LD.v

    g = k.gp
    g.iota(ident[:, :], pattern=[[1, 128]], base=0, channel_multiplier=-1,
           allow_small_or_imprecise_dtypes=True)
    gc = GPc.bump(g.iota(r512[:, :], pattern=[[1, 512]], base=0, channel_multiplier=-1,
                         allow_small_or_imprecise_dtypes=True))
    v = k.ve
    k.flush_c0 = 0
    k.flush_c0 = k.vs(v.memset(k.sb["flush_sb"][0:1, :], 0.0))
    v.wait_ge(GPc.sem, gc)
    v.tensor_single_scalar(ident[:, :], ident[:, :], 0.0, op=ALU.is_equal)
    v.memset(ones_row[:, :], 1.0)
    v.memset(onescol[:, :], 1.0)
    v.wait_ge(LD.sem, LD.v)
    k.ident = ident
    k.ones_row = ones_row
    k.onescol = onescol
    k.r512 = r512
    k.slop = slop


def emit_p0(k: K, b: int):
    nc = k.nc
    LD, PE, VE, SC, ST = k.s_ld, k.s_pe, k.s_ve, k.s_sc, k.s_st
    with ExitStack() as ctx:
        xt = [k.tmp(ctx, "p0x", [128, D]) for i in range(2)]
        stg_xn = k.tmp(ctx, "p0sxn", [128, D])
        stg_x = k.tmp(ctx, "p0sx", [128, D])
        xc = k.tmp(ctx, "p0xc", [128, D])
        sqd = k.tmp(ctx, "p0sqd", [128, D])
        st1 = k.tmp(ctx, "p0st", [128, 12])
        ps_t = k.ptmp(ctx, "p0pst", [128, 1024])
        ps_t2 = k.ptmp(ctx, "p0pst2", [128, 1024])
        ps_z = k.ptmp(ctx, "p0psz", [128, 128])

        ld0 = [0, 0]
        pe_done = [0] * NT
        sc_evac = [0] * NT
        v, s, t, g = k.ve, k.sc, k.te, k.gp
        for tt in range(NT):
            buf = xt[tt % 2]
            if tt >= 2:
                k.sy.wait_ge(PE.sem, pe_done[tt - 2])
            LDP = k.s_lda if tt % 2 == 0 else k.s_ldb
            ld0[tt % 2] = LDP.bump(k.sy.dma_start(
                out=buf[:, :], in_=k.ins["x"][b, tt * 128:(tt + 1) * 128, :]), 16)

            v.wait_ge(LDP.sem, ld0[tt % 2])
            k.vs(v.reduce_sum(st1[:, 0:1], buf[:, :], axis=AX.X))
            vc = k.vs(v.tensor_scalar(st1[:, 1:2], st1[:, 0:1], -1.0 / D, None,
                                      op0=ALU.mult))
            s.wait_ge(VE.sem, vc)
            sc0 = SC.bump(s.activation(sqd[:, :], buf[:, :], AF.Square, bias=0.0,
                                       scale=1.0, accum_out=st1[:, 2:3]))
            v.wait_ge(SC.sem, sc0)
            v.tensor_scalar(st1[:, 3:4], st1[:, 2:3], 1.0 / D, None, op0=ALU.mult)
            k.vs(v.tensor_mul(st1[:, 4:5], st1[:, 1:2], st1[:, 1:2]))
            k.vs(v.tensor_sub(st1[:, 3:4], st1[:, 3:4], st1[:, 4:5]))
            vc = k.vs(v.tensor_scalar(st1[:, 3:4], st1[:, 3:4], EPS, None,
                                      op0=ALU.add))
            s.wait_ge(VE.sem, vc)
            k.ss(s.activation(st1[:, 7:8], st1[:, 3:4], AF.Ln, bias=0.0, scale=1.0))
            sc1 = SC.bump(s.activation(st1[:, 4:5], st1[:, 7:8], AF.Exp, bias=0.0,
                                       scale=-0.5))
            v.wait_ge(SC.sem, sc1)
            k.vs(v.tensor_mul(st1[:, 5:6], st1[:, 4:5], st1[:, 4:5]))
            k.vs(v.tensor_mul(st1[:, 5:6], st1[:, 5:6], st1[:, 3:4]))
            k.vs(v.tensor_scalar(st1[:, 5:6], st1[:, 5:6], -0.5, 1.5, op0=ALU.mult,
                                 op1=ALU.add))
            k.vs(v.tensor_mul(st1[:, 6:7], st1[:, 4:5], st1[:, 5:6]))
            if tt >= 1:
                v.wait_ge(PE.sem, pe_done[tt - 1])
            k.vs(v.tensor_scalar(xc[:, :], buf[:, :], st1[:, 1:2], None, op0=ALU.add))
            vxn = VE.bump(v.tensor_scalar(xc[:, :], xc[:, :], st1[:, 6:7], None,
                                          op0=ALU.mult))

            t.wait_ge(VE.sem, vxn)
            if tt >= 1:
                t.wait_ge(SC.sem, sc_evac[tt - 1])
            for c in range(NC):
                t.transpose(ps_t[:, c * 128:c * 128 + 128],
                            xc[:, c * 128:(c + 1) * 128], k.ident[:, :])
            last = None
            for c in range(NC):
                last = t.transpose(ps_t2[:, c * 128:c * 128 + 128],
                                   buf[:, c * 128:(c + 1) * 128], k.ident[:, :])
            pe_done[tt] = PE.bump(last)

            s.wait_ge(PE.sem, pe_done[tt])
            for c in range(NC):
                s.copy(stg_xn[:, c * 128:(c + 1) * 128], ps_t[:, c * 128:c * 128 + 128])
            last = None
            for c in range(NC):
                last = s.copy(stg_x[:, c * 128:(c + 1) * 128], ps_t2[:, c * 128:c * 128 + 128])
            sc_evac[tt] = SC.bump(last)

            t.wait_ge(SC.sem, sc_evac[tt])
            for c in range(NC):
                last = t.matmul(ps_z[0:2, 0:128], k.bias["w0fg"][:, 2 * c:2 * c + 2],
                                stg_xn[:, c * 128:(c + 1) * 128],
                                start=(c == 0), stop=(c == NC - 1))
            pez = PE.bump(last)
            s.wait_ge(PE.sem, pez)
            s.copy(k.sb["z_sb"][0:1, tt * 128:(tt + 1) * 128], ps_z[0:1, 0:128])

            g.wait_ge(SC.sem, sc_evac[tt])
            ST.bump(g.dma_start(
                out=k.scr["xnT"][b].rearrange("(c p) t -> p c t", p=128)[:, :, tt * 128:(tt + 1) * 128],
                in_=stg_xn[:, :].rearrange("p (c t) -> p c t", c=NC)), 16)
            ST.bump(g.dma_start(
                out=k.scr["xT"][b].rearrange("(c p) t -> p c t", p=128)[:, :, tt * 128:(tt + 1) * 128],
                in_=stg_x[:, :].rearrange("p (c t) -> p c t", c=NC)), 16)
            if tt < NT - 1:
                k.sc.wait_ge(ST.sem, ST.v)
        k.gp.wait_ge(ST.sem, ST.v)


def emit_bisect(k: K, b: int):
    nc = k.nc
    VE, SC, PE, ST, LD = k.s_ve, k.s_sc, k.s_pe, k.s_st, k.s_ld
    v, s, t, g = k.ve, k.sc, k.te, k.gp
    z = k.sb["z_sb"]
    st0 = k.sb["bst_sb"]
    zr = z[0:1, :]
    st = lambda i: st0[0:1, 16 * b + i:16 * b + i + 1]
    with ExitStack() as ctx:
        cmp_row = k.tmp(ctx, "bcmp", [1, S])
        aug = k.tmp(ctx, "baug", [1, S])
        aug_a = k.tmp(ctx, "bauga", [1, S])
        gsc = k.tmp(ctx, "bgsc", [1, S])
        psg = k.ptmp(ctx, "bpsg", [128, 128])

        k.vs(v.reduce_max(st(0), zr, axis=AX.X))
        k.vs(v.tensor_reduce(st(1), zr, axis=AX.X, op=ALU.min))
        v.tensor_scalar(st(2), st(1), -1.0, None, op0=ALU.add)
        k.vs(v.tensor_scalar(st(3), st(0), 1.0, None, op0=ALU.add))
        for it in range(N_BISECT):
            k.vs(v.tensor_add(st(4), st(2), st(3)))
            k.vs(v.tensor_scalar(st(4), st(4), 0.5, None, op0=ALU.mult))
            k.vs(v.tensor_scalar(cmp_row[0:1, :], zr, st(4), 0.0,
                                 op0=ALU.is_ge, op1=ALU.add, accum_out=st(5)))
            k.vs(v.tensor_scalar(st(6), st(5), 1024.0, None, op0=ALU.is_ge))
            k.vs(v.tensor_sub(st(7), st(4), st(2)))
            k.vs(v.tensor_mul(st(7), st(7), st(6)))
            k.vs(v.tensor_scalar(st(8), st(6), -1.0, 1.0, op0=ALU.mult, op1=ALU.add))
            k.vs(v.tensor_sub(st(9), st(4), st(3)))
            k.vs(v.tensor_mul(st(9), st(9), st(8)))
            k.vs(v.tensor_add(st(2), st(2), st(7)))
            k.vs(v.tensor_add(st(3), st(3), st(9)))
        vau = k.vs(v.tensor_scalar(aug[0:1, :], zr, st(2), NEGAUG,
                                   op0=ALU.is_lt, op1=ALU.mult))
        s.wait_ge(VE.sem, vau)
        saug = SC.bump(s.copy(aug_a[0:1, :], aug[0:1, :]))
        g.wait_ge(SC.sem, saug)
        sta = ST.bump(g.dma_start(out=k.scr["augr"][b, :], in_=aug_a[0:1, :]), 16)
        s.wait_ge(VE.sem, vau)
        sc0 = SC.bump(s.activation(cmp_row[0:1, :], zr, AF.Exp, bias=0.0, scale=-1.0))
        v.wait_ge(SC.sem, sc0)
        k.vs(v.tensor_scalar(cmp_row[0:1, :], cmp_row[0:1, :], 1.0, None, op0=ALU.add))
        k.vs(v.reciprocal(gsc[0:1, :], cmp_row[0:1, :]))
        k.vs(v.tensor_scalar(cmp_row[0:1, :], zr, st(2), None, op0=ALU.is_ge))
        if k.dbg:
            g.wait_ge(VE.sem, VE.v)
            ST.bump(g.dma_start(out=k.scr["zdbg"][b, 4, :], in_=gsc[0:1, :]), 16)
            ST.bump(g.dma_start(out=k.scr["zdbg"][b, 5, :], in_=cmp_row[0:1, :]), 16)
            g.wait_ge(ST.sem, ST.v)
        vg = k.vs(v.tensor_mul(gsc[0:1, :], gsc[0:1, :], cmp_row[0:1, :]))
        t.wait_ge(VE.sem, vg)
        for tt in range(NT):
            pc = PE.bump(t.transpose(psg[:, tt % 2:tt % 2 + 1],
                        gsc[0:1, tt * 128:(tt + 1) * 128], k.ident[0:1, 0:1]))
            s.wait_ge(PE.sem, pc)
            scv = SC.bump(s.copy(k.sb["gsccol_sb"][:, b * NT + tt:b * NT + tt + 1],
                                 psg[:, tt % 2:tt % 2 + 1]))
            if tt < NT - 1:
                t.wait_ge(SC.sem, scv)
        v.wait_ge(SC.sem, SC.v)
        vgm = k.vs(v.tensor_scalar(k.sb["lngsc_sb"][:, b * NT:(b + 1) * NT],
                                   k.sb["gsccol_sb"][:, b * NT:(b + 1) * NT],
                                   1e-30, None, op0=ALU.max))
        s.wait_ge(VE.sem, vgm)
        sgl = SC.bump(s.activation(k.sb["lngsc_sb"][:, b * NT:(b + 1) * NT],
                                   k.sb["lngsc_sb"][:, b * NT:(b + 1) * NT],
                                   AF.Ln, bias=0.0, scale=1.0))
        v.wait_ge(SC.sem, sgl)
        k.vs(v.tensor_scalar(k.sb["lngsc_sb"][:, b * NT:(b + 1) * NT],
                             k.sb["lngsc_sb"][:, b * NT:(b + 1) * NT],
                             8.0, None, op0=ALU.mult))
        if k.dbg:
            g.wait_ge(VE.sem, VE.v)
            g.wait_ge(SC.sem, SC.v)
            ST.bump(g.dma_start(out=k.scr["zdbg"][b, 0, :], in_=zr), 16)
            ST.bump(g.dma_start(out=k.scr["zdbg"][b, 1, :], in_=aug[0:1, :]), 16)
            ST.bump(g.dma_start(out=k.scr["zdbg"][b, 2, :], in_=gsc[0:1, :]), 16)
            ST.bump(g.dma_start(out=k.scr["zdbg"][b, 3, 0:16],
                                in_=st0[0:1, 16 * b:16 * b + 16]), 16)
        g.wait_ge(ST.sem, ST.v)


def emit_proj(k: K, b: int, xin, specs):
    nc = k.nc
    LD, PE, VE, SC, ST = k.s_ld, k.s_pe, k.s_ve, k.s_sc, k.s_st
    v, s, t, g, sy = k.ve, k.sc, k.te, k.gp, k.sy
    with ExitStack() as ctx:
        xn_sb = k.tmp(ctx, "prx", [128, NC * S])
        w_sb = k.tmp(ctx, "prw", [128, NC * D])
        stg = [k.tmp(ctx, "prs", [128, 512]) for _ in range(2)]
        vstg = k.tmp(ctx, "prv", [128, 512])
        vt_stage = k.tmp(ctx, "prvt", [128, 4 * D])
        ps = k.ptmp(ctx, "prps", [128, 1024])
        ps_t = k.ptmp(ctx, "prpst", [128, 512])

        ldx = LD.bump(sy.dma_start(
            out=xn_sb[:, :].rearrange("p (c t) -> p c t", c=NC),
            in_=xin.rearrange("(c p) t -> p c t", p=128)), 16)

        for si, spec in enumerate(specs):
            noc = spec["nout"] // 128
            if si > 0:
                sy.wait_ge(PE.sem, PE.v)
            ldw = LD.bump(sy.dma_start(
                out=w_sb[:, 0:NC * spec["nout"]].rearrange("p (c o) -> p c o", c=NC),
                in_=k.ins[spec["w"]].rearrange("(c p) o -> p c o", p=128)), 16)
            bias = k.bias[spec["bias"]] if spec["bias"] else None
            t.wait_ge(LD.sem, max(ldx, ldw))

            if spec.get("dst_tm") is None:
                evd = {}
                std = {}
                it = 0
                for oc in range(noc):
                    for ts4 in range(4):
                        bank = it % 2
                        if it >= 2:
                            t.wait_ge(VE.sem if bias else SC.sem, evd[it - 2])
                        last = None
                        for c in range(NC):
                            last = t.matmul(
                                ps[:, bank * 512:bank * 512 + 512],
                                w_sb[:, c * spec["nout"] + oc * 128:
                                     c * spec["nout"] + oc * 128 + 128],
                                xn_sb[:, c * S + ts4 * 512:c * S + ts4 * 512 + 512],
                                start=(c == 0), stop=(c == NC - 1))
                        pc = PE.bump(last)
                        if it >= 2:
                            STP2 = k.s_sta if it % 2 == 0 else k.s_stb
                            (v if bias else s).wait_ge(STP2.sem, std[it - 2])
                        if bias:
                            v.wait_ge(PE.sem, pc)
                            evd[it] = VE.bump(v.tensor_scalar(
                                stg[bank][:, :], ps[:, bank * 512:bank * 512 + 512],
                                bias[:, oc:oc + 1], None, op0=ALU.add))
                            evd[it] = k.vflush()
                            g.wait_ge(VE.sem, evd[it])
                        else:
                            s.wait_ge(PE.sem, pc)
                            evd[it] = SC.bump(s.copy(
                                stg[bank][:, :], ps[:, bank * 512:bank * 512 + 512]))
                            g.wait_ge(SC.sem, evd[it])
                        STP = k.s_sta if it % 2 == 0 else k.s_stb
                        std[it] = STP.bump(g.dma_start(
                            out=spec["dst"][oc * 128:(oc + 1) * 128,
                                            ts4 * 512:(ts4 + 1) * 512],
                            in_=stg[bank][:, :]), 16)
                        it += 1
                g.wait_ge(k.s_sta.sem, k.s_sta.v)
                g.wait_ge(k.s_stb.sem, k.s_stb.v)
            else:
                for ts4 in range(4):
                    for oc in range(noc):
                        it = ts4 * noc + oc
                        bank = it % 2
                        if it >= 2:
                            t.wait_ge(VE.sem if bias else SC.sem, evd2[it - 2])
                        last = None
                        for c in range(NC):
                            last = t.matmul(
                                ps[:, bank * 512:bank * 512 + 512],
                                w_sb[:, c * spec["nout"] + oc * 128:
                                     c * spec["nout"] + oc * 128 + 128],
                                xn_sb[:, c * S + ts4 * 512:c * S + ts4 * 512 + 512],
                                start=(c == 0), stop=(c == NC - 1))
                        pc = PE.bump(last)
                        if it == 0:
                            evd2, trd, cpd = {}, {}, {}
                        if bias:
                            v.wait_ge(PE.sem, pc)
                            evd2[it] = VE.bump(v.tensor_scalar(
                                vstg[:, :], ps[:, bank * 512:bank * 512 + 512],
                                bias[:, oc:oc + 1], None, op0=ALU.add))
                            t.wait_ge(VE.sem, evd2[it])
                        else:
                            s.wait_ge(PE.sem, pc)
                            evd2[it] = SC.bump(s.copy(
                                vstg[:, :], ps[:, bank * 512:bank * 512 + 512]))
                            t.wait_ge(SC.sem, evd2[it])
                        if it >= 1:
                            t.wait_ge(SC.sem, cpd[it - 1])
                        last = None
                        for tc in range(4):
                            last = t.transpose(ps_t[:, tc * 128:tc * 128 + 128],
                                               vstg[:, tc * 128:tc * 128 + 128],
                                               k.ident[:, :])
                        trd[it] = PE.bump(last)
                        s.wait_ge(PE.sem, trd[it])
                        last = None
                        for tc in range(4):
                            last = s.copy(
                                vt_stage[:, tc * spec["nout"] + oc * 128:
                                         tc * spec["nout"] + oc * 128 + 128],
                                ps_t[:, tc * 128:tc * 128 + 128])
                        cpd[it] = SC.bump(last)
                    g.wait_ge(SC.sem, cpd[ts4 * noc + noc - 1])
                    stv = ST.bump(g.dma_start(
                        out=spec["dst_tm"].rearrange("(tb tc p) f -> tb p tc f",
                                                     p=128, tc=4)[ts4][:, :,
                                                     spec["col0"]:spec["col0"] + spec["nout"]],
                        in_=vt_stage[:, 0:4 * spec["nout"]].rearrange(
                            "p (tc f) -> p tc f", tc=4)), 16)
                    s.wait_ge(ST.sem, stv)
        g.wait_ge(ST.sem, ST.v)
        g.wait_ge(k.s_sta.sem, k.s_sta.v)
        g.wait_ge(k.s_stb.sem, k.s_stb.v)


def emit_attn(k: K, b: int, kind: str):
    nc = k.nc
    LD, PE, VE, SC, ST = k.s_ld, k.s_pe, k.s_ve, k.s_sc, k.s_st
    v, s, t, g, sy = k.ve, k.sc, k.te, k.gp, k.sy
    scale = {"l0a": 1.0 / 32.0, "l0f": 1.0 / 8.0, "l1": 1.0 / 8.0}[kind]
    inv_scale = 1.0 / scale
    CK = 128
    causal = (kind == "l1")
    outscr = k.scr["ofT"] if kind == "l0f" else k.scr["oaT"]

    with ExitStack() as ctx:
        qh = [k.tmp(ctx, "aq", [128, S]) for _ in range(2)]
        kh = [k.tmp(ctx, "ak", [128, S]) for _ in range(2)]
        vt = [k.tmp(ctx, "av", [128, NT * DH]) for _ in range(2)]
        uT = [k.tmp(ctx, "au", [128, NT * 512]) for _ in range(2)]
        pdum = k.tmp(ctx, "apd", [128, S])
        ofm = k.tmp(ctx, "aof", [128, S])
        mall = k.tmp(ctx, "am", [128, NT])
        sall = k.tmp(ctx, "as", [128, NT])
        lns = k.tmp(ctx, "al", [128, NT])
        smt = k.tmp(ctx, "asm", [128, 8])
        nm16 = k.tmp(ctx, "anm", [16, 128])
        srow = k.tmp(ctx, "asr", [1, S])
        srow_a = k.tmp(ctx, "asra", [1, S])
        ones2k = k.tmp(ctx, "ao2", [1, S])
        if causal:
            iota_j = k.tmp(ctx, "aij", [1, S])
            mtri = k.tmp(ctx, "amt", [128, 4 * 512])
            mc1 = k.tmp(ctx, "amc", [128, 4 * 512])
            dstage = k.tmp(ctx, "ads", [128, 4 * 512])
        mgall = k.tmp(ctx, "amg", [128, 64])
        sgall = k.tmp(ctx, "asg", [128, 64])
        tm64 = k.tmp(ctx, "atm", [128, 64])
        s_ps = k.ptmp(ctx, "aps", [128, 2048])
        oav_ps = k.ptmp(ctx, "apo", [64, 1024])

        v.memset(ones2k[0:1, :], 1.0)
        vz = None
        for bb in range(2):
            v.memset(qh[bb][64:128, :], 0.0)
            vz = v.memset(kh[bb][64:128, :], 0.0)
        vz = VE.bump(vz)
        if causal:
            gi = k.s_gp.bump(g.iota(iota_j[0:1, :], pattern=[[1, S]], base=0,
                                    channel_multiplier=0,
                                    allow_small_or_imprecise_dtypes=True))
            v.wait_ge(k.s_gp.sem, gi)
            for r_ in range(4):
                v.tensor_scalar(mtri[:, r_ * 512:(r_ + 1) * 512], k.r512[:, :],
                                float(128 * r_), -BIG, op0=ALU.is_lt, op1=ALU.mult)
            for r_ in range(4):
                v.tensor_scalar(mc1[:, r_ * 512:(r_ + 1) * 512], k.r512[:, :],
                                float(128 * r_), -BIG, op0=ALU.is_gt, op1=ALU.mult)
            k.vs(v.memset(smt[:, 7:8], 0.0))

        def issue_loads(h):
            hb = h % 2
            LDP = k.s_lda if hb == 0 else k.s_ldb
            kvh = h // 4 if kind == "l1" else h
            LDP.bump(sy.dma_start(out=qh[hb][0:64, :],
                                  in_=k.scr["qT"][b][h * 64:(h + 1) * 64, :]), 16)
            LDP.bump(sy.dma_start(out=kh[hb][0:64, :],
                                  in_=k.scr["kT"][b][kvh * 64:(kvh + 1) * 64, :]), 16)
            l3 = LDP.bump(sy.dma_start(
                out=vt[hb][:, :].rearrange("p (kc d) -> p kc d", kc=NT),
                in_=k.scr["vTm"][b].rearrange("(kc p) f -> p kc f", p=128)[
                    :, :, kvh * 64:(kvh + 1) * 64]), 16)
            if kind == "l0f":
                l3 = LDP.bump(sy.dma_start(out=kh[hb][65:66, :],
                                           in_=k.scr["augr"][b, :]), 16)
            return l3

        def aug_rows(h, ldt):
            hb = h % 2
            LDP = k.s_lda if hb == 0 else k.s_ldb
            LA = k.s_la
            v.wait_ge(LDP.sem, ldt)
            v.wait_ge(VE.sem, vz)
            v.tensor_copy(kh[hb][64:65, :], ones2k[0:1, :])
            av = VE.bump(v.memset(qh[hb][64:65, :], 0.0))
            al = LA.v
            if kind == "l0f":
                sy.wait_ge(VE.sem, av)
                al = LA.bump(sy.dma_start(out=qh[hb][65:66, :],
                                          in_=ones2k[0:1, :]), 16)
            elif kind == "l1":
                vsr = k.vs(v.tensor_scalar(srow[0:1, :], ones2k[0:1, :],
                                           k.slop[0:1, h:h + 1], inv_scale,
                                           op0=ALU.mult, op1=ALU.mult))
                s.wait_ge(VE.sem, vsr)
                if h > 0:
                    s.wait_ge(LA.sem, LA.v)
                sca = SC.bump(s.copy(srow_a[0:1, :], srow[0:1, :]))
                sy.wait_ge(SC.sem, sca)
                LA.bump(sy.dma_start(out=kh[hb][65:66, :],
                                     in_=iota_j[0:1, :]), 16)
                al = LA.bump(sy.dma_start(out=qh[hb][65:66, :],
                                          in_=srow_a[0:1, :]), 16)
            return av, al

        ld_t = issue_loads(0)
        aug_t, aug_l = aug_rows(0, ld_t)
        free_at = {}
        for h in range(H):
            hb = h % 2
            if h + 1 < H:
                if h >= 1:
                    sy.wait_ge(PE.sem, free_at[h - 1])
                ld_next = issue_loads(h + 1)

            t.wait_ge((k.s_lda if hb == 0 else k.s_ldb).sem, ld_t)
            t.wait_ge(k.s_la.sem, aug_l)
            t.wait_ge(VE.sem, aug_t)

            t.wait_ge(SC.sem, SC.v)
            expg = {}
            gcnt = 0
            if causal:
                v.memset(mgall[:, :], -BIG)
                k.vs(v.memset(sgall[:, :], 0.0))
            for qt in range(NT):
                nk = (qt // 4 + 1) if causal else 4
                for kc in range(nk):
                    bank = gcnt % 4
                    slot = gcnt % 4
                    col = qt * 4 + kc
                    if gcnt >= 4:
                        t.wait_ge(SC.sem, expg[gcnt - 4])
                    pc = PE.bump(t.matmul(s_ps[:, bank * 512:(bank + 1) * 512],
                                          qh[hb][0:CK, qt * 128:(qt + 1) * 128],
                                          kh[hb][0:CK, kc * 512:(kc + 1) * 512],
                                          start=True, stop=True))
                    v.wait_ge(PE.sem, pc)
                    if causal and kc == qt // 4:
                        vmc = VE.bump(v.tensor_add(
                            s_ps[:, bank * 512:(bank + 1) * 512],
                            s_ps[:, bank * 512:(bank + 1) * 512],
                            mc1[:, (qt % 4) * 512:(qt % 4 + 1) * 512]))
                        v.wait_ge(VE.sem, vmc)
                    if gcnt >= 4:
                        v.wait_ge(SC.sem, expg[gcnt - 4])
                    k.vs(v.reduce_max(mgall[:, col:col + 1],
                                      s_ps[:, bank * 512:(bank + 1) * 512], axis=AX.X))
                    vneg = k.vs(v.tensor_scalar(smt[:, slot:slot + 1],
                                                mgall[:, col:col + 1],
                                                -scale, None, op0=ALU.mult))
                    s.wait_ge(VE.sem, vneg)
                    expg[gcnt] = SC.bump(s.activation(
                        pdum[:, kc * 512:(kc + 1) * 512],
                        s_ps[:, bank * 512:(bank + 1) * 512],
                        AF.Exp, bias=smt[:, slot:slot + 1], scale=scale,
                        accum_out=sgall[:, col:col + 1]))
                    gcnt += 1

            exp1_last = expg[gcnt - 1]
            v.wait_ge(SC.sem, exp1_last)
            mg3 = mgall[:, :].rearrange("p (q g) -> p q g", g=4)
            k.vs(v.tensor_reduce(mall[:, :], mg3, axis=AX.X, op=ALU.max))
            last = None
            for g_ in range(4):
                last = v.scalar_tensor_tensor(
                    tm64[:, :].rearrange("p (q g) -> p q g", g=4)[:, :, g_],
                    mall[:, :], -1.0,
                    mg3[:, :, g_], op0=ALU.mult, op1=ALU.add)
            k.vs(last)
            vc4 = k.vs(v.tensor_scalar(tm64[:, :], tm64[:, :], scale, None,
                                       op0=ALU.mult))
            s.wait_ge(VE.sem, vc4)
            scm = SC.bump(s.activation(tm64[:, :], tm64[:, :], AF.Exp, bias=0.0,
                                       scale=1.0))
            v.wait_ge(SC.sem, scm)
            k.vs(v.tensor_mul(tm64[:, :], tm64[:, :], sgall[:, :]))
            k.vs(v.tensor_reduce(sall[:, :], tm64[:, :].rearrange("p (q g) -> p q g", g=4),
                                 axis=AX.X, op=ALU.add))

            s.wait_ge(VE.sem, VE.v)
            sln = SC.bump(s.activation(lns[:, :], sall[:, :], AF.Ln, bias=0.0,
                                       scale=1.0))
            v.wait_ge(SC.sem, sln)
            k.vs(v.tensor_scalar(pdum[:, 0:NT], lns[:, :], inv_scale, None,
                                 op0=ALU.mult))
            vw = k.vs(v.tensor_add(pdum[:, 0:NT], pdum[:, 0:NT], mall[:, :]))
            if kind == "l0f":
                vw = k.vs(v.tensor_sub(pdum[:, 0:NT], pdum[:, 0:NT],
                                       k.sb["lngsc_sb"][:, b * NT:(b + 1) * NT]))
            t.wait_ge(VE.sem, vw)
            t.wait_ge(SC.sem, SC.v)
            ptr = PE.bump(t.transpose(s_ps[0:16, 0:128], pdum[:, 0:NT],
                                      k.ident[:, :]))
            s.wait_ge(PE.sem, ptr)
            if h > 0:
                s.wait_ge(k.s_lm.sem, prev_negm_st)
            vng = SC.bump(s.activation(nm16[:, :], s_ps[0:16, 0:128], AF.Copy,
                                       bias=0.0, scale=-1.0))
            sy.wait_ge(SC.sem, vng)
            prev_negm_st = k.s_lm.bump(sy.dma_start(out=k.scr["negm"][:, :],
                                                    in_=nm16[:, :]), 16)
            sy.wait_ge(k.s_lm.sem, prev_negm_st)
            ldm = k.s_lm.bump(sy.dma_start(
                out=qh[hb][64:65, :],
                in_=k.scr["negm"].rearrange("a j -> (a j)")[None, :]), 16)

            t.wait_ge(k.s_lm.sem, ldm)
            uexp = {}
            for qt4 in range(4):
                ub = uT[qt4 % 2]
                nkk = 16 if not causal else (4 * qt4 + 4)
                for kc in range(nkk):
                    bank = kc % 4
                    if qt4 == 0 and kc == 0:
                        t.wait_ge(SC.sem, max(exp1_last, vng))
                    if kc >= 4:
                        t.wait_ge(SC.sem, uexp[(qt4, kc - 4)])
                    elif qt4 > 0:
                        t.wait_ge(SC.sem, uexp[(qt4 - 1, prev_nkk - 4 + kc)])
                    pc = PE.bump(t.matmul(s_ps[:, bank * 512:(bank + 1) * 512],
                                          kh[hb][0:CK, kc * 128:(kc + 1) * 128],
                                          qh[hb][0:CK, qt4 * 512:(qt4 + 1) * 512],
                                          start=True, stop=True))
                    diag = causal and (kc >= 4 * qt4)
                    if diag:
                        v.wait_ge(PE.sem, pc)
                        r = kc - 4 * qt4
                        vst = VE.bump(v.scalar_tensor_tensor(
                            dstage[:, r * 512:(r + 1) * 512],
                            s_ps[:, bank * 512:(bank + 1) * 512],
                            scale, mtri[:, r * 512:(r + 1) * 512],
                            op0=ALU.mult, op1=ALU.add))
                        s.wait_ge(VE.sem, vst)
                        uexp[(qt4, kc)] = SC.bump(s.activation(
                            ub[:, kc * 512:(kc + 1) * 512],
                            dstage[:, r * 512:(r + 1) * 512],
                            AF.Exp, bias=0.0, scale=1.0))
                    else:
                        s.wait_ge(PE.sem, pc)
                        uexp[(qt4, kc)] = SC.bump(s.activation(
                            ub[:, kc * 512:(kc + 1) * 512],
                            s_ps[:, bank * 512:(bank + 1) * 512],
                            AF.Exp, bias=0.0, scale=scale))
                prev_nkk = nkk
                t.wait_ge(SC.sem, uexp[(qt4, nkk - 1)])
                if h > 0 or qt4 >= 2:
                    t.wait_ge(SC.sem, oev[qt4 % 2])
                obank = oav_ps[:, (qt4 % 2) * 512:(qt4 % 2) * 512 + 512]
                last = None
                for kc in range(nkk):
                    last = t.matmul(obank,
                                    vt[hb][:, kc * 64:(kc + 1) * 64],
                                    ub[:, kc * 512:(kc + 1) * 512],
                                    start=(kc == 0), stop=(kc == nkk - 1))
                pav = PE.bump(last)
                s.wait_ge(PE.sem, pav)
                if h == 0 and qt4 == 0:
                    oev = {}
                oev[qt4 % 2] = SC.bump(s.copy(
                    ofm[64 * (h % 2):64 * (h % 2) + 64, qt4 * 512:(qt4 + 1) * 512],
                    obank))
            free_at[h] = PE.v
            if h % 2 == 1:
                g.wait_ge(SC.sem, SC.v)
                stf = ST.bump(g.dma_start(
                    out=outscr[b][(h // 2) * 128:(h // 2) * 128 + 128, :],
                    in_=ofm[:, :]), 16)
                s.wait_ge(ST.sem, stf)
            if h + 1 < H:
                ld_t = ld_next
                aug_t, aug_l = aug_rows(h + 1, ld_t)
        g.wait_ge(ST.sem, ST.v)


def ln_tile(k, v, s, buf, xc, st1, sqd):
    VE, SC = k.s_ve, k.s_sc
    k.vs(v.reduce_sum(st1[:, 0:1], buf[:, :], axis=AX.X))
    vc = k.vs(v.tensor_scalar(st1[:, 1:2], st1[:, 0:1], -1.0 / D, None, op0=ALU.mult))
    s.wait_ge(VE.sem, vc)
    sc0 = SC.bump(s.activation(sqd[:, :], buf[:, :], AF.Square, bias=0.0,
                               scale=1.0, accum_out=st1[:, 2:3]))
    v.wait_ge(SC.sem, sc0)
    v.tensor_scalar(st1[:, 3:4], st1[:, 2:3], 1.0 / D, None, op0=ALU.mult)
    k.vs(v.tensor_mul(st1[:, 4:5], st1[:, 1:2], st1[:, 1:2]))
    k.vs(v.tensor_sub(st1[:, 3:4], st1[:, 3:4], st1[:, 4:5]))
    vc = k.vs(v.tensor_scalar(st1[:, 3:4], st1[:, 3:4], EPS, None, op0=ALU.add))
    s.wait_ge(VE.sem, vc)
    k.ss(s.activation(st1[:, 7:8], st1[:, 3:4], AF.Ln, bias=0.0, scale=1.0))
    sc1 = k.s_sc.bump(s.activation(st1[:, 4:5], st1[:, 7:8], AF.Exp, bias=0.0,
                                   scale=-0.5))
    v.wait_ge(SC.sem, sc1)
    k.vs(v.tensor_mul(st1[:, 5:6], st1[:, 4:5], st1[:, 4:5]))
    k.vs(v.tensor_mul(st1[:, 5:6], st1[:, 5:6], st1[:, 3:4]))
    k.vs(v.tensor_scalar(st1[:, 5:6], st1[:, 5:6], -0.5, 1.5, op0=ALU.mult, op1=ALU.add))
    k.vs(v.tensor_mul(st1[:, 6:7], st1[:, 4:5], st1[:, 5:6]))
    k.vs(v.tensor_scalar(xc[:, :], buf[:, :], st1[:, 1:2], None, op0=ALU.add))
    return k.s_ve.bump(v.tensor_scalar(xc[:, :], xc[:, :], st1[:, 6:7], None,
                                       op0=ALU.mult))


def emit_combine_l0(k: K, b: int):
    nc = k.nc
    LD, PE, VE, SC, ST = k.s_ld, k.s_pe, k.s_ve, k.s_sc, k.s_st
    v, s, t, g, sy = k.ve, k.sc, k.te, k.gp, k.sy
    with ExitStack() as ctx:
        a0_sb = k.tmp(ctx, "cba", [128, NC * 512])
        f0_sb = k.tmp(ctx, "cbf", [128, NC * 512])
        x_sb = k.tmp(ctx, "cbx", [128, NC * 512])
        x1stg = k.tmp(ctx, "cbs", [128, NC * 512])
        lnt = k.tmp(ctx, "cbl", [128, D])
        xnt = k.tmp(ctx, "cbn", [128, D])
        sqd = k.tmp(ctx, "cbq", [128, D])
        st1 = k.tmp(ctx, "cbst", [128, 12])
        xn1stg = k.tmp(ctx, "cbg", [128, D])
        ps_t = k.ptmp(ctx, "cbp", [128, 1024])
        ps_t2 = k.ptmp(ctx, "cbp2", [128, 1024])

        for ts4 in range(4):
            sl = slice(ts4 * 512, (ts4 + 1) * 512)
            l1 = LD.bump(sy.dma_start(
                out=a0_sb[:, :].rearrange("p (c t) -> p c t", c=NC),
                in_=k.scr["qT"][b].rearrange("(c p) t -> p c t", p=128)[:, :, sl]), 16)
            l2 = LD.bump(sy.dma_start(
                out=f0_sb[:, :].rearrange("p (c t) -> p c t", c=NC),
                in_=k.scr["kT"][b].rearrange("(c p) t -> p c t", p=128)[:, :, sl]), 16)
            l3 = LD.bump(sy.dma_start(
                out=x_sb[:, :].rearrange("p (c t) -> p c t", c=NC),
                in_=k.scr["xT"][b].rearrange("(c p) t -> p c t", p=128)[:, :, sl]), 16)
            v.wait_ge(LD.sem, max(l1, l2, l3))
            lastv = None
            for c in range(NC):
                csl = slice(c * 512, (c + 1) * 512)
                lastv = v.tensor_add(x1stg[:, csl], x_sb[:, csl], a0_sb[:, csl])
            k.vs(lastv)
            vx1 = None
            for c in range(NC):
                csl = slice(c * 512, (c + 1) * 512)
                vx1 = v.tensor_add(x1stg[:, csl], x1stg[:, csl], f0_sb[:, csl])
            VE.bump(vx1)
            vx1c = k.vflush()
            g.wait_ge(VE.sem, vx1c)
            st1c = ST.bump(g.dma_start(
                out=k.scr["x1T"][b].rearrange("(c p) t -> p c t", p=128)[:, :, sl],
                in_=x1stg[:, :].rearrange("p (c t) -> p c t", c=NC)), 16)
            for tc in range(4):
                t.wait_ge(VE.sem, vx1c)
                if not (ts4 == 0 and tc == 0):
                    t.wait_ge(SC.sem, sc_ev)
                last = None
                for c in range(NC):
                    last = t.transpose(ps_t[:, c * 128:(c + 1) * 128],
                                       x1stg[:, c * 512 + tc * 128:c * 512 + tc * 128 + 128],
                                       k.ident[:, :])
                ptc = PE.bump(last)
                s.wait_ge(PE.sem, ptc)
                last = None
                for c in range(NC):
                    last = s.copy(lnt[:, c * 128:(c + 1) * 128],
                                  ps_t[:, c * 128:(c + 1) * 128])
                sc_ev = SC.bump(last)
                v.wait_ge(SC.sem, sc_ev)
                vxn = ln_tile(k, v, s, lnt, xnt, st1, sqd)
                t.wait_ge(VE.sem, vxn)
                if not (ts4 == 0 and tc == 0):
                    t.wait_ge(SC.sem, sc_ev2)
                last = None
                for c in range(NC):
                    last = t.transpose(ps_t2[:, c * 128:(c + 1) * 128],
                                       xnt[:, c * 128:(c + 1) * 128], k.ident[:, :])
                ptc2 = PE.bump(last)
                s.wait_ge(PE.sem, ptc2)
                if not (ts4 == 0 and tc == 0):
                    s.wait_ge(ST.sem, st_xn)
                last = None
                for c in range(NC):
                    last = s.copy(xn1stg[:, c * 128:(c + 1) * 128],
                                  ps_t2[:, c * 128:(c + 1) * 128])
                sc_ev2 = SC.bump(last)
                g.wait_ge(SC.sem, sc_ev2)
                st_xn = ST.bump(g.dma_start(
                    out=k.scr["xn1T"][b].rearrange("(c p) t -> p c t", p=128)[
                        :, :, ts4 * 512 + tc * 128:ts4 * 512 + tc * 128 + 128],
                    in_=xn1stg[:, :].rearrange("p (c t) -> p c t", c=NC)), 16)
            sy.wait_ge(PE.sem, PE.v)
            sy.wait_ge(ST.sem, st1c)
        g.wait_ge(ST.sem, ST.v)


def emit_moe(k: K):
    nc = k.nc
    LD, PE, VE, SC, ST = k.s_ld, k.s_pe, k.s_ve, k.s_sc, k.s_st
    v, s, t, g, sy = k.ve, k.sc, k.te, k.gp, k.sy
    with ExitStack() as ctx:
        xn_sb = [k.tmp(ctx, "mx", [128, NC * 512]) for _ in range(2)]
        wg_sb = k.tmp(ctx, "mwg", [128, NC * 8])
        lgs = k.tmp(ctx, "mlg", [8, 512])
        lgtm = k.tmp(ctx, "mlt", [128, 8])
        cw = k.tmp(ctx, "mcw", [128, 16])
        coeff = k.tmp(ctx, "mcf", [128, 32 * 8])
        crow = k.tmp(ctx, "mcr", [1, 512])
        cbc = k.tmp(ctx, "mcb", [128, 512])
        w1_sb = k.tmp(ctx, "mw1", [128, NC * 2048])
        w2_sb = k.tmp(ctx, "mw2", [128, 16 * D])
        b1_sb = k.tmp(ctx, "mb1", [128, 16])
        b2_sb = k.tmp(ctx, "mb2", [128, 8])
        h_sb = k.tmp(ctx, "mh", [128, 16 * 512])
        mstg = [k.tmp(ctx, "mst", [128, 512]) for _ in range(2)]
        ps_g = k.ptmp(ctx, "mpg", [128, 512])
        ps_t = k.ptmp(ctx, "mpt", [128, 128])
        ps_h = k.ptmp(ctx, "mph", [128, 1024])
        ps_e = k.ptmp(ctx, "mpe", [128, 1024])

        ldg = LD.bump(sy.dma_start(
            out=wg_sb[:, :].rearrange("p (c o) -> p c o", c=NC),
            in_=k.ins["w1g"].rearrange("(c p) o -> p c o", p=128)), 16)

        for tb in range(8):
            b, ts4 = tb // 4, tb % 4
            sl = slice(ts4 * 512, (ts4 + 1) * 512)
            if tb > 0:
                sy.wait_ge(PE.sem, PE.v)
            lx = LD.bump(sy.dma_start(
                out=xn_sb[0][:, :].rearrange("p (c t) -> p c t", c=NC),
                in_=k.scr["xn1T"][b].rearrange("(c p) t -> p c t", p=128)[:, :, sl]), 16)
            t.wait_ge(LD.sem, max(ldg, lx))
            if tb > 0:
                t.wait_ge(SC.sem, lg_ev)
            last = None
            for c in range(NC):
                last = t.matmul(ps_g[0:8, :], wg_sb[:, c * 8:(c + 1) * 8],
                                xn_sb[0][:, c * 512:(c + 1) * 512],
                                start=(c == 0), stop=(c == NC - 1))
            pg = PE.bump(last)
            s.wait_ge(PE.sem, pg)
            lg_ev = SC.bump(s.copy(lgs[:, :], ps_g[0:8, :]))
            v.wait_ge(SC.sem, lg_ev)
            vb = k.vs(v.tensor_scalar(lgs[:, :], lgs[:, :], k.bias["b1g"][:, 0:1],
                                      None, op0=ALU.add))
            for tc in range(4):
                t.wait_ge(VE.sem, vb)
                if not (tb == 0 and tc == 0):
                    t.wait_ge(SC.sem, tr_ev)
                ptc = PE.bump(t.transpose(ps_t[0:128, 0:8],
                                          lgs[:, tc * 128:(tc + 1) * 128],
                                          k.ident[0:8, 0:8]))
                s.wait_ge(PE.sem, ptc)
                tr_ev = SC.bump(s.copy(lgtm[:, :], ps_t[0:128, 0:8]))
                v.wait_ge(SC.sem, tr_ev)
                k.vs(v.reduce_max(cw[:, 0:1], lgtm[:, :], axis=AX.X))
                k.vs(v.tensor_scalar(cw[:, 8:16], lgtm[:, :], cw[:, 0:1],
                                     None, op0=ALU.is_ge))
                k.vs(v.reduce_sum(cw[:, 1:2], cw[:, 8:16], axis=AX.X))
                k.vs(v.scalar_tensor_tensor(cbc[:, 0:8], cw[:, 8:16], -BIG,
                                            lgtm[:, :], op0=ALU.mult, op1=ALU.add))
                k.vs(v.reduce_max(cw[:, 2:3], cbc[:, 0:8], axis=AX.X))
                k.vs(v.tensor_scalar(cw[:, 3:4], cw[:, 1:2], 2.0, None, op0=ALU.is_lt))
                k.vs(v.tensor_scalar(cbc[:, 8:16], lgtm[:, :], cw[:, 2:3], None,
                                     op0=ALU.is_ge))
                k.vs(v.tensor_sub(cbc[:, 8:16], cbc[:, 8:16], cw[:, 8:16]))
                k.vs(v.scalar_tensor_tensor(cbc[:, 8:16], cbc[:, 8:16], cw[:, 3:4],
                                            cw[:, 8:16], op0=ALU.mult, op1=ALU.add))
                vng = k.vs(v.tensor_scalar(cw[:, 4:5], cw[:, 0:1], -1.0, None,
                                           op0=ALU.mult))
                s.wait_ge(VE.sem, vng)
                swx = SC.bump(s.activation(cbc[:, 16:24], lgtm[:, :], AF.Exp,
                                           bias=cw[:, 4:5], scale=1.0))
                v.wait_ge(SC.sem, swx)
                k.vs(v.tensor_mul(cbc[:, 16:24], cbc[:, 16:24], cbc[:, 8:16]))
                k.vs(v.reduce_sum(cw[:, 5:6], cbc[:, 16:24], axis=AX.X))
                k.vs(v.reciprocal(cw[:, 6:7], cw[:, 5:6]))
                k.vs(v.tensor_scalar(coeff[:, (tb * 4 + tc) * 8:(tb * 4 + tc) * 8 + 8],
                                     cbc[:, 16:24], cw[:, 6:7], None, op0=ALU.mult))

        vcoeff_done = VE.v
        t.wait_ge(VE.sem, vcoeff_done)
        t.wait_ge(SC.sem, SC.v)
        for e in range(8):
            sy.wait_ge(PE.sem, PE.v)
            lw1 = LD.bump(sy.dma_start(
                out=w1_sb[:, :].rearrange("p (c o) -> p c o", c=NC),
                in_=k.ins["w1e1"][e].rearrange("(c p) o -> p c o", p=128)), 16)
            lw2 = LD.bump(sy.dma_start(
                out=w2_sb[:, :].rearrange("p (c o) -> p c o", c=16),
                in_=k.ins["w1e2"][e].rearrange("(c p) o -> p c o", p=128)), 16)
            lb1 = LD.bump(sy.dma_start(out=b1_sb[:, :], in_=k.ins["b1e1"][e]), 16)
            lb2 = LD.bump(sy.dma_start(out=b2_sb[:, :], in_=k.ins["b1e2"][e]), 16)
            for tb in range(8):
                b, ts4 = tb // 4, tb % 4
                sl = slice(ts4 * 512, (ts4 + 1) * 512)
                xb = xn_sb[tb % 2]
                if e > 0 or tb >= 2:
                    sy.wait_ge(PE.sem, PE.v)
                lx = LD.bump(sy.dma_start(
                    out=xb[:, :].rearrange("p (c t) -> p c t", c=NC),
                    in_=k.scr["xn1T"][b].rearrange("(c p) t -> p c t", p=128)[:, :, sl]), 16)
                for tc in range(4):
                    if not (e == 0 and tb == 0 and tc == 0):
                        t.wait_ge(SC.sem, cr_ev)
                    ptc = PE.bump(t.transpose(
                        ps_t[0:1, 0:128],
                        coeff[:, (tb * 4 + tc) * 8 + e:(tb * 4 + tc) * 8 + e + 1],
                        k.ident[:, :]))
                    s.wait_ge(PE.sem, ptc)
                    cr_ev = SC.bump(s.copy(crow[0:1, tc * 128:(tc + 1) * 128],
                                           ps_t[0:1, 0:128]))
                t.wait_ge(SC.sem, cr_ev)
                if not (e == 0 and tb == 0):
                    t.wait_ge(VE.sem, cb_ev)
                pbc = PE.bump(t.matmul(ps_g[:, :], k.ones_row[:, :], crow[0:1, :],
                                       start=True, stop=True))
                v.wait_ge(PE.sem, pbc)
                cb_ev = k.vs(v.tensor_copy(cbc[:, :], ps_g[:, :]))
                t.wait_ge(LD.sem, max(lw1, lb1, lx))
                h_prev = h_ev if not (e == 0 and tb == 0) else None
                h_ev = {}
                for ffc in range(16):
                    bank = ffc % 2
                    if ffc >= 2:
                        t.wait_ge(SC.sem, h_ev[ffc - 2])
                    elif h_prev is not None:
                        t.wait_ge(SC.sem, h_prev[14 + ffc])
                    last = None
                    for c in range(NC):
                        last = t.matmul(ps_h[:, bank * 512:bank * 512 + 512],
                                        w1_sb[:, c * 2048 + ffc * 128:
                                              c * 2048 + ffc * 128 + 128],
                                        xb[:, c * 512:(c + 1) * 512],
                                        start=(c == 0), stop=(c == NC - 1))
                    ph = PE.bump(last)
                    s.wait_ge(PE.sem, ph)
                    if ffc == 0 and h_prev is not None:
                        s.wait_ge(PE.sem, eo_pe_last)
                    h_ev[ffc] = SC.bump(s.activation(
                        h_sb[:, ffc * 512:(ffc + 1) * 512],
                        ps_h[:, bank * 512:bank * 512 + 512],
                        AF.Relu, bias=b1_sb[:, ffc:ffc + 1], scale=1.0))
                t.wait_ge(LD.sem, max(lw2, lb2))
                t.wait_ge(SC.sem, h_ev[15])
                for dc in range(8):
                    bank = dc % 2
                    if dc >= 2:
                        t.wait_ge(VE.sem, m_ev[dc - 2])
                    elif not (e == 0 and tb == 0):
                        t.wait_ge(VE.sem, m_ev_prev[6 + dc])
                    last = None
                    for ffc in range(16):
                        last = t.matmul(ps_e[:, bank * 512:bank * 512 + 512],
                                        w2_sb[:, ffc * D + dc * 128:
                                              ffc * D + dc * 128 + 128],
                                        h_sb[:, ffc * 512:(ffc + 1) * 512],
                                        start=(ffc == 0), stop=(ffc == 15))
                    pe_ = PE.bump(last)
                    if dc == 0:
                        m_ev = {}
                    v.wait_ge(PE.sem, pe_)
                    if e > 0 or tb > 0 or dc >= 2:
                        STP2 = k.s_sta if dc % 2 == 0 else k.s_stb
                        v.wait_ge(STP2.sem, st_prev.get(dc, 0) if dc < 2 else st_d[dc - 2])
                    if dc == 0:
                        st_d = {}
                    VE.bump(v.scalar_tensor_tensor(
                        mstg[dc % 2][:, :], ps_e[:, bank * 512:bank * 512 + 512],
                        b2_sb[:, dc:dc + 1], cbc[:, :], op0=ALU.add, op1=ALU.mult))
                    m_ev[dc] = k.vflush()
                    g.wait_ge(VE.sem, m_ev[dc])
                    STP = k.s_sta if dc % 2 == 0 else k.s_stb
                    st_d[dc] = STP.bump(g.dma_start(
                        out=k.scr["mixT"][b][dc * 128:(dc + 1) * 128, sl],
                        in_=mstg[dc % 2][:, :],
                        accum_op=(ALU.add if e > 0 else ALU.bypass)), 16)
                eo_pe_last = PE.v
                m_ev_prev = dict(m_ev)
                st_prev = {0: st_d[6], 1: st_d[7]}
        g.wait_ge(ST.sem, ST.v)
        g.wait_ge(k.s_sta.sem, k.s_sta.v)
        g.wait_ge(k.s_stb.sem, k.s_stb.v)


def emit_final(k: K):
    nc = k.nc
    LD, PE, VE, SC, ST = k.s_ld, k.s_pe, k.s_ve, k.s_sc, k.s_st
    v, s, t, g, sy = k.ve, k.sc, k.te, k.gp, k.sy
    with ExitStack() as ctx:
        mix_sb = k.tmp(ctx, "fmx", [128, NC * 512])
        xn_sb = k.tmp(ctx, "fxn", [128, NC * 512])
        x1_sb = k.tmp(ctx, "fx1", [128, NC * 512])
        a1_sb = k.tmp(ctx, "fa1", [128, NC * 512])
        lnt = k.tmp(ctx, "fln", [128, D])
        xnt = k.tmp(ctx, "fxt", [128, D])
        sqd = k.tmp(ctx, "fsq", [128, D])
        st1 = k.tmp(ctx, "fst", [128, 12])
        ynT = k.tmp(ctx, "fyt", [128, D])
        outT = k.tmp(ctx, "fot", [128, D])
        outtm = k.tmp(ctx, "fom", [128, D])
        ps_1 = k.ptmp(ctx, "fp1", [128, 1024])
        ps_2 = k.ptmp(ctx, "fp2", [128, 1024])
        ps_3 = k.ptmp(ctx, "fp3", [128, 1024])

        for b in range(NB):
            for ts4 in range(4):
                sl = slice(ts4 * 512, (ts4 + 1) * 512)
                if not (b == 0 and ts4 == 0):
                    sy.wait_ge(VE.sem, VE.v)
                    sy.wait_ge(PE.sem, PE.v)
                l1_ = LD.bump(sy.dma_start(
                    out=mix_sb[:, :].rearrange("p (c t) -> p c t", c=NC),
                    in_=k.scr["mixT"][b].rearrange("(c p) t -> p c t", p=128)[:, :, sl]), 16)
                l2 = LD.bump(sy.dma_start(
                    out=xn_sb[:, :].rearrange("p (c t) -> p c t", c=NC),
                    in_=k.scr["xn1T"][b].rearrange("(c p) t -> p c t", p=128)[:, :, sl]), 16)
                l3 = LD.bump(sy.dma_start(
                    out=x1_sb[:, :].rearrange("p (c t) -> p c t", c=NC),
                    in_=k.scr["x1T"][b].rearrange("(c p) t -> p c t", p=128)[:, :, sl]), 16)
                l4 = LD.bump(sy.dma_start(
                    out=a1_sb[:, :].rearrange("p (c t) -> p c t", c=NC),
                    in_=k.scr["a1T"][b].rearrange("(c p) t -> p c t", p=128)[:, :, sl]), 16)
                v.wait_ge(LD.sem, max(l1_, l2, l3, l4))
                vy = VE.bump(v.tensor_add(mix_sb[:, :], mix_sb[:, :], xn_sb[:, :]))
                for tc in range(4):
                    csl = lambda c: slice(c * 512 + tc * 128, c * 512 + tc * 128 + 128)
                    t.wait_ge(VE.sem, vy)
                    if not (b == 0 and ts4 == 0 and tc == 0):
                        t.wait_ge(SC.sem, ev1)
                    last = None
                    for c in range(NC):
                        last = t.transpose(ps_1[:, c * 128:(c + 1) * 128],
                                           mix_sb[:, csl(c)], k.ident[:, :])
                    pt1 = PE.bump(last)
                    s.wait_ge(PE.sem, pt1)
                    last = None
                    for c in range(NC):
                        last = s.copy(lnt[:, c * 128:(c + 1) * 128],
                                      ps_1[:, c * 128:(c + 1) * 128])
                    ev1 = SC.bump(last)
                    v.wait_ge(SC.sem, ev1)
                    vxn = ln_tile(k, v, s, lnt, xnt, st1, sqd)
                    t.wait_ge(VE.sem, vxn)
                    if not (b == 0 and ts4 == 0 and tc == 0):
                        t.wait_ge(SC.sem, ev2)
                    last = None
                    for c in range(NC):
                        last = t.transpose(ps_2[:, c * 128:(c + 1) * 128],
                                           xnt[:, c * 128:(c + 1) * 128], k.ident[:, :])
                    pt2 = PE.bump(last)
                    s.wait_ge(PE.sem, pt2)
                    last = None
                    for c in range(NC):
                        last = s.copy(ynT[:, c * 128:(c + 1) * 128],
                                      ps_2[:, c * 128:(c + 1) * 128])
                    ev2 = SC.bump(last)
                    v.wait_ge(SC.sem, ev2)
                    lastv = None
                    for c in range(NC):
                        lastv = v.tensor_scalar(outT[:, c * 128:(c + 1) * 128],
                                        ynT[:, c * 128:(c + 1) * 128],
                                        k.bias["gamma"][:, c:c + 1], None, op0=ALU.mult)
                    k.vs(lastv)
                    for c in range(NC):
                        lastv = v.scalar_tensor_tensor(outT[:, c * 128:(c + 1) * 128],
                                               outT[:, c * 128:(c + 1) * 128],
                                               k.bias["beta"][:, c:c + 1],
                                               x1_sb[:, csl(c)],
                                               op0=ALU.add, op1=ALU.add)
                    k.vs(lastv)
                    vo = None
                    for c in range(NC):
                        vo = v.tensor_add(outT[:, c * 128:(c + 1) * 128],
                                          outT[:, c * 128:(c + 1) * 128],
                                          a1_sb[:, csl(c)])
                    vou = VE.bump(vo)
                    t.wait_ge(VE.sem, vou)
                    if not (b == 0 and ts4 == 0 and tc == 0):
                        t.wait_ge(SC.sem, ev3)
                    last = None
                    for c in range(NC):
                        last = t.transpose(ps_3[:, c * 128:(c + 1) * 128],
                                           outT[:, c * 128:(c + 1) * 128],
                                           k.ident[:, :])
                    pt3 = PE.bump(last)
                    s.wait_ge(PE.sem, pt3)
                    if not (b == 0 and ts4 == 0 and tc == 0):
                        s.wait_ge(ST.sem, sto)
                    last = None
                    for c in range(NC):
                        last = s.copy(outtm[:, c * 128:(c + 1) * 128],
                                      ps_3[:, c * 128:(c + 1) * 128])
                    ev3 = SC.bump(last)
                    g.wait_ge(SC.sem, ev3)
                    t0 = ts4 * 512 + tc * 128
                    sto = ST.bump(g.dma_start(
                        out=k.out[b, t0:t0 + 128, :], in_=outtm[:, :]), 16)
        g.wait_ge(ST.sem, ST.v)


def build_program(dbg=False):
    nc = bass.Bass("TRN2", target_bir_lowering=False)
    with ExitStack() as ctx:
        k = K(nc, ctx, dbg)
        declare_io(k)
        k.s_ld = k.sem("s_ld")
        k.s_st = k.sem("s_st")
        k.s_pe = k.sem("s_pe")
        k.s_ve = k.sem("s_ve")
        k.s_sc = k.sem("s_sc")
        k.s_gp = k.sem("s_gp")
        k.s_ld2 = k.sem("s_ld2")
        k.s_lda = k.sem("s_lda")
        k.s_ldb = k.sem("s_ldb")
        k.s_la = k.sem("s_la")
        k.s_lm = k.sem("s_lm")
        k.s_sta = k.sem("s_sta")
        k.s_stb = k.sem("s_stb")
        k.s_pe2 = k.sem("s_pe2")
        k.s_sc2 = k.sem("s_sc2")
        k.s_ve2 = k.sem("s_ve2")
        k.s_st2 = k.sem("s_st2")
        k.sbuf("bst_sb", [1, 32])
        k.sbuf("gsccol_sb", [128, 2 * NT])
        k.sbuf("lngsc_sb", [128, 2 * NT])
        k.sbuf("flush_sb", [1, 8])

        emit_init(k)
        k.barrier()
        for b in range(NB):
            with nc.sbuf_tensor(f"z_sb_{b}", [1, S], F32) as zt:
                k.sb["z_sb"] = zt
                emit_p0(k, b)
                k.barrier()
                emit_bisect(k, b)
                k.barrier()
        for b in range(NB):
            emit_proj(k, b, k.scr["xnT"][b], [
                dict(w="w0aq", bias="b0aq", nout=D, dst=k.scr["qT"][b], dst_tm=None),
                dict(w="w0ak", bias="b0ak", nout=D, dst=k.scr["kT"][b], dst_tm=None),
                dict(w="w0av", bias="b0av", nout=D, dst=None,
                     dst_tm=k.scr["vTm"][b], col0=0),
            ])
            k.barrier()
            emit_attn(k, b, "l0a")
            k.barrier()
            emit_proj(k, b, k.scr["xnT"][b], [
                dict(w="w0fq", bias=None, nout=D, dst=k.scr["qT"][b], dst_tm=None),
                dict(w="w0fk", bias=None, nout=D, dst=k.scr["kT"][b], dst_tm=None),
                dict(w="w0fv", bias=None, nout=D, dst=None,
                     dst_tm=k.scr["vTm"][b], col0=0),
            ])
            k.barrier()
            emit_attn(k, b, "l0f")
            k.barrier()
            emit_proj(k, b, k.scr["oaT"][b], [
                dict(w="w0ao", bias="b0ao", nout=D, dst=k.scr["qT"][b], dst_tm=None)])
            k.barrier()
            emit_proj(k, b, k.scr["ofT"][b], [
                dict(w="w0fo", bias=None, nout=D, dst=k.scr["kT"][b], dst_tm=None)])
            k.barrier()
            emit_combine_l0(k, b)
            k.barrier()
            emit_proj(k, b, k.scr["xn1T"][b], [
                dict(w="w1aq", bias="b1aq", nout=D, dst=k.scr["qT"][b], dst_tm=None),
                dict(w="w1ak", bias="b1ak", nout=256, dst=k.scr["kT"][b], dst_tm=None),
                dict(w="w1av", bias="b1av", nout=256, dst=None,
                     dst_tm=k.scr["vTm"][b], col0=0),
            ])
            k.barrier()
            emit_attn(k, b, "l1")
            k.barrier()
            emit_proj(k, b, k.scr["oaT"][b], [
                dict(w="w1ao", bias="b1ao", nout=D, dst=k.scr["a1T"][b], dst_tm=None)])
            k.barrier()
        emit_moe(k)
        k.barrier()
        emit_final(k)
    return nc


def finalize(k: K):
    nc = k.nc
    with ExitStack() as ctx:
        zt = ctx.enter_context(nc.sbuf_tensor("fin_z", [128, D], F32))
        vc = k.s_ve.bump(k.ve.memset(zt[:, :], 0.0))
        k.gp.wait_ge(k.s_ve.sem, vc)
        ST = k.s_st
        for b in range(NB):
            for tt in range(NT):
                ST.bump(k.gp.dma_start(out=k.out[b, tt * 128:(tt + 1) * 128, :],
                                       in_=zt[:, :]), 16)
        k.gp.wait_ge(ST.sem, ST.v)


def prep_inputs(full):
    T = lambda a: np.ascontiguousarray(np.asarray(a).T)
    C = lambda a: np.ascontiguousarray(np.asarray(a), dtype=np.float32)
    bias_cols = lambda a, nc_: np.ascontiguousarray(np.asarray(a).reshape(nc_, 128).T)
    base = {
        "w0aq": T(full["L0_attn_Wq"]), "w0ak": T(full["L0_attn_Wk"]),
        "w0av": T(full["L0_attn_Wv"]), "w0ao": T(full["L0_attn_Wo"]),
        "b0aq": bias_cols(full["L0_attn_bq"], NC), "b0ak": bias_cols(full["L0_attn_bk"], NC),
        "b0av": bias_cols(full["L0_attn_bv"], NC), "b0ao": bias_cols(full["L0_attn_bo"], NC),
        "w0fg": np.ascontiguousarray(np.repeat(bias_cols(full["L0_ffn_Wg"][:, 0], NC), 2, axis=1)),
        "w0fq": C(full["L0_ffn_Wq"]), "w0fk": C(full["L0_ffn_Wk"]),
        "w0fv": C(full["L0_ffn_Wv"]), "w0fo": C(full["L0_ffn_Wo"]),
        "w1aq": C(full["L1_attn_Wq"]), "w1ak": C(full["L1_attn_Wk"]),
        "w1av": C(full["L1_attn_Wv"]), "w1ao": C(full["L1_attn_Wo"]),
        "b1aq": bias_cols(full["L1_attn_bq"], NC), "b1ao": bias_cols(full["L1_attn_bo"], NC),
        "b1ak": bias_cols(full["L1_attn_bk"], 2), "b1av": bias_cols(full["L1_attn_bv"], 2),
        "slopes": C(full["L1_attn_slopes"]).reshape(1, H),
        "w1g": C(full["L1_ffn_Wg"]), "b1g": C(full["L1_ffn_bg"]).reshape(8, 1),
        "w1e1": C(full["L1_ffn_W1"]),
        "b1e1": np.ascontiguousarray(np.asarray(full["L1_ffn_b1"]).reshape(8, 16, 128).transpose(0, 2, 1)),
        "w1e2": C(full["L1_ffn_W2"]),
        "b1e2": np.ascontiguousarray(np.asarray(full["L1_ffn_b2"]).reshape(8, 8, 128).transpose(0, 2, 1)),
        "gamma": bias_cols(full["L1_ffn_gamma"], NC),
        "beta": bias_cols(full["L1_ffn_beta"], NC),
    }
    x = np.asarray(full["L0_attn_x"], dtype=np.float32)
    maps = []
    for c in range(8):
        m = dict(base)
        m["x"] = np.ascontiguousarray(x[2 * c:2 * c + 2])
        maps.append(m)
    return maps


def kernel(**inputs):
    maps = prep_inputs(inputs)
    nc = build_program(dbg=False)
    res = run_bass_kernel_spmd(nc, maps, core_ids=list(range(8)))
    return np.concatenate([r["out"] for r in res.results], axis=0)


# version 14
# speedup vs baseline: 1.0297x; 1.0232x over previous
import numpy as np
from contextlib import ExitStack

import concourse.bass as bass
import concourse.mybir as mybir
from concourse.bass_utils import run_bass_kernel_spmd

F32 = mybir.dt.float32
AF = mybir.ActivationFunctionType
ALU = mybir.AluOpType
AX = mybir.AxisListType

NB = 2
S = 2048
D = 1024
NC = D // 128
NT = S // 128
H = 16
DH = 64
BIG = 1.0e30
NEGAUG = -8.0e30
EPS = 1e-5
N_BISECT = 45


class Ctr:

    def __init__(self, sem):
        self.sem = sem
        self.v = 0

    def bump(self, inst, n=1):
        inst.then_inc(self.sem, n)
        self.v += n
        return self.v


class K:

    def __init__(self, nc, ctx, dbg):
        self.nc = nc
        self.ctx = ctx
        self.dbg = dbg
        self.sy = nc.sync
        self.ve = nc.vector
        self.sc = nc.scalar
        self.te = nc.tensor
        self.gp = nc.gpsimd
        self.ins = {}
        self.scr = {}
        self.sb = {}

    _uid = 0

    def sbuf(self, name, shape, dtype=F32):
        t = self.ctx.enter_context(self.nc.sbuf_tensor(name, list(shape), dtype))
        self.sb[name] = t
        return t

    def tmp(self, ctx, name, shape, dtype=F32):
        K._uid += 1
        return ctx.enter_context(self.nc.sbuf_tensor(f"{name}_{K._uid}", list(shape), dtype))

    def ptmp(self, ctx, name, shape, dtype=F32):
        K._uid += 1
        return ctx.enter_context(self.nc.psum_tensor(f"{name}_{K._uid}", list(shape), dtype))

    def psum(self, name, shape, dtype=F32):
        return self.ctx.enter_context(self.nc.psum_tensor(name, list(shape), dtype))

    _fl = 0

    def vflush(self):
        K._fl = (K._fl + 1) % 4
        self.ve.wait_ge(self.s_ve.sem, self.flush_c0)
        c = self.s_ve.bump(self.ve.tensor_copy(
            self.sb["flush_sb"][0:1, 4 + K._fl:5 + K._fl],
            self.sb["flush_sb"][0:1, 0:1]))
        return c

    def vs(self, inst):
        c = self.s_ve.bump(inst)
        self.ve.wait_ge(self.s_ve.sem, c)
        return c

    def ss(self, inst):
        c = self.s_sc.bump(inst)
        self.sc.wait_ge(self.s_sc.sem, c)
        return c

    def sem(self, name):
        return Ctr(self.ctx.enter_context(self.nc.semaphore(name)))

    def barrier(self):
        self.nc.all_engine_barrier()


def declare_io(k: K):
    nc = k.nc
    def inp(name, shape):
        k.ins[name] = nc.dram_tensor(name, list(shape), F32, kind="ExternalInput").ap()
    inp("x", [NB, S, D])
    for w in ("w0aq", "w0ak", "w0av", "w0ao"):
        inp(w, [D, D])
    for b in ("b0aq", "b0ak", "b0av", "b0ao"):
        inp(b, [128, NC])
    inp("w0fg", [128, 2 * NC])
    for w in ("w0fq", "w0fk", "w0fv", "w0fo"):
        inp(w, [D, D])
    inp("w1aq", [D, D]); inp("w1ao", [D, D])
    inp("w1ak", [D, 256]); inp("w1av", [D, 256])
    inp("b1aq", [128, NC]); inp("b1ao", [128, NC])
    inp("b1ak", [128, 2]); inp("b1av", [128, 2])
    inp("slopes", [1, H])
    inp("w1g", [D, 8]); inp("b1g", [8, 1])
    inp("w1e1", [8, D, 2 * D]); inp("b1e1", [8, 128, 16])
    inp("w1e2", [8, 2 * D, D]); inp("b1e2", [8, 128, 8])
    inp("gamma", [128, NC]); inp("beta", [128, NC])

    k.out = nc.dram_tensor("out", [NB, S, D], F32, kind="ExternalOutput").ap()

    kind = "ExternalOutput" if k.dbg else "Internal"
    def scr(name, shape):
        k.scr[name] = nc.dram_tensor(name, list(shape), F32, kind=kind).ap()
    scr("xT", [NB, D, S])
    scr("xnT", [NB, D, S])
    scr("qT", [NB, D, S])
    scr("kT", [NB, D, S])
    scr("vTm", [NB, S, D])
    scr("oaT", [NB, D, S])
    scr("ofT", [NB, D, S])
    scr("x1T", [NB, D, S])
    scr("xn1T", [NB, D, S])
    scr("a1T", [NB, D, S])
    scr("mixT", [NB, D, S])
    scr("augr", [NB, S])
    scr("negm", [16, 128])
    if k.dbg:
        scr("attn_dbg", [NB, 4, S])
    if k.dbg:
        scr("zdbg", [NB, 8, S])


def emit_init(k: K):
    ident = k.sbuf("ident", [128, 128])
    ones_row = k.sbuf("ones_row", [1, 128])
    onescol = k.sbuf("onescol", [128, 1])
    r512 = k.sbuf("r512", [128, 512])
    slop = k.sbuf("slop", [1, H])
    bias_sb = {}
    for nm in ("b0aq", "b0ak", "b0av", "b0ao", "b1aq", "b1ao", "gamma", "beta"):
        bias_sb[nm] = k.sbuf(nm + "_sb", [128, NC])
    for nm in ("b1ak", "b1av"):
        bias_sb[nm] = k.sbuf(nm + "_sb", [128, 2])
    bias_sb["b1g"] = k.sbuf("b1g_sb", [8, 1])
    bias_sb["w0fg"] = k.sbuf("w0fg_sb", [128, 2 * NC])
    k.bias = bias_sb

    LD, GPc = k.s_ld, k.s_gp
    for nm in ("b0aq", "b0ak", "b0av", "b0ao", "b1aq", "b1ao", "gamma", "beta",
               "b1ak", "b1av", "b1g", "w0fg"):
        c = LD.bump(k.sy.dma_start(out=bias_sb[nm][:, :], in_=k.ins[nm]), 16)
    c = LD.bump(k.sy.dma_start(out=slop[:, :], in_=k.ins["slopes"]), 16)
    ld_done = LD.v

    g = k.gp
    g.iota(ident[:, :], pattern=[[1, 128]], base=0, channel_multiplier=-1,
           allow_small_or_imprecise_dtypes=True)
    gc = GPc.bump(g.iota(r512[:, :], pattern=[[1, 512]], base=0, channel_multiplier=-1,
                         allow_small_or_imprecise_dtypes=True))
    v = k.ve
    k.flush_c0 = 0
    k.flush_c0 = k.vs(v.memset(k.sb["flush_sb"][0:1, :], 0.0))
    v.wait_ge(GPc.sem, gc)
    v.tensor_single_scalar(ident[:, :], ident[:, :], 0.0, op=ALU.is_equal)
    v.memset(ones_row[:, :], 1.0)
    v.memset(onescol[:, :], 1.0)
    v.wait_ge(LD.sem, LD.v)
    k.ident = ident
    k.ones_row = ones_row
    k.onescol = onescol
    k.r512 = r512
    k.slop = slop


def emit_p0(k: K, b: int):
    nc = k.nc
    LD, PE, VE, SC, ST = k.s_ld, k.s_pe, k.s_ve, k.s_sc, k.s_st
    with ExitStack() as ctx:
        xt = [k.tmp(ctx, "p0x", [128, D]) for i in range(2)]
        stg_xn = k.tmp(ctx, "p0sxn", [128, D])
        stg_x = k.tmp(ctx, "p0sx", [128, D])
        xc = k.tmp(ctx, "p0xc", [128, D])
        sqd = k.tmp(ctx, "p0sqd", [128, D])
        st1 = k.tmp(ctx, "p0st", [128, 12])
        ps_t = k.ptmp(ctx, "p0pst", [128, 1024])
        ps_t2 = k.ptmp(ctx, "p0pst2", [128, 1024])
        ps_z = k.ptmp(ctx, "p0psz", [128, 128])

        ld0 = [0, 0]
        pe_done = [0] * NT
        sc_evac = [0] * NT
        v, s, t, g = k.ve, k.sc, k.te, k.gp
        for tt in range(NT):
            buf = xt[tt % 2]
            if tt >= 2:
                k.sy.wait_ge(PE.sem, pe_done[tt - 2])
            LDP = k.s_lda if tt % 2 == 0 else k.s_ldb
            ld0[tt % 2] = LDP.bump(k.sy.dma_start(
                out=buf[:, :], in_=k.ins["x"][b, tt * 128:(tt + 1) * 128, :]), 16)

            v.wait_ge(LDP.sem, ld0[tt % 2])
            k.vs(v.reduce_sum(st1[:, 0:1], buf[:, :], axis=AX.X))
            vc = k.vs(v.tensor_scalar(st1[:, 1:2], st1[:, 0:1], -1.0 / D, None,
                                      op0=ALU.mult))
            s.wait_ge(VE.sem, vc)
            sc0 = SC.bump(s.activation(sqd[:, :], buf[:, :], AF.Square, bias=0.0,
                                       scale=1.0, accum_out=st1[:, 2:3]))
            v.wait_ge(SC.sem, sc0)
            v.tensor_scalar(st1[:, 3:4], st1[:, 2:3], 1.0 / D, None, op0=ALU.mult)
            k.vs(v.tensor_mul(st1[:, 4:5], st1[:, 1:2], st1[:, 1:2]))
            k.vs(v.tensor_sub(st1[:, 3:4], st1[:, 3:4], st1[:, 4:5]))
            vc = k.vs(v.tensor_scalar(st1[:, 3:4], st1[:, 3:4], EPS, None,
                                      op0=ALU.add))
            s.wait_ge(VE.sem, vc)
            k.ss(s.activation(st1[:, 7:8], st1[:, 3:4], AF.Ln, bias=0.0, scale=1.0))
            sc1 = SC.bump(s.activation(st1[:, 4:5], st1[:, 7:8], AF.Exp, bias=0.0,
                                       scale=-0.5))
            v.wait_ge(SC.sem, sc1)
            k.vs(v.tensor_mul(st1[:, 5:6], st1[:, 4:5], st1[:, 4:5]))
            k.vs(v.tensor_mul(st1[:, 5:6], st1[:, 5:6], st1[:, 3:4]))
            k.vs(v.tensor_scalar(st1[:, 5:6], st1[:, 5:6], -0.5, 1.5, op0=ALU.mult,
                                 op1=ALU.add))
            k.vs(v.tensor_mul(st1[:, 6:7], st1[:, 4:5], st1[:, 5:6]))
            if tt >= 1:
                v.wait_ge(PE.sem, pe_done[tt - 1])
            k.vs(v.tensor_scalar(xc[:, :], buf[:, :], st1[:, 1:2], None, op0=ALU.add))
            vxn = VE.bump(v.tensor_scalar(xc[:, :], xc[:, :], st1[:, 6:7], None,
                                          op0=ALU.mult))

            t.wait_ge(VE.sem, vxn)
            if tt >= 1:
                t.wait_ge(SC.sem, sc_evac[tt - 1])
            for c in range(NC):
                t.transpose(ps_t[:, c * 128:c * 128 + 128],
                            xc[:, c * 128:(c + 1) * 128], k.ident[:, :])
            last = None
            for c in range(NC):
                last = t.transpose(ps_t2[:, c * 128:c * 128 + 128],
                                   buf[:, c * 128:(c + 1) * 128], k.ident[:, :])
            pe_done[tt] = PE.bump(last)

            s.wait_ge(PE.sem, pe_done[tt])
            for c in range(NC):
                s.copy(stg_xn[:, c * 128:(c + 1) * 128], ps_t[:, c * 128:c * 128 + 128])
            last = None
            for c in range(NC):
                last = s.copy(stg_x[:, c * 128:(c + 1) * 128], ps_t2[:, c * 128:c * 128 + 128])
            sc_evac[tt] = SC.bump(last)

            t.wait_ge(SC.sem, sc_evac[tt])
            for c in range(NC):
                last = t.matmul(ps_z[0:2, 0:128], k.bias["w0fg"][:, 2 * c:2 * c + 2],
                                stg_xn[:, c * 128:(c + 1) * 128],
                                start=(c == 0), stop=(c == NC - 1))
            pez = PE.bump(last)
            s.wait_ge(PE.sem, pez)
            s.copy(k.sb["z_sb"][0:1, tt * 128:(tt + 1) * 128], ps_z[0:1, 0:128])

            g.wait_ge(SC.sem, sc_evac[tt])
            ST.bump(g.dma_start(
                out=k.scr["xnT"][b].rearrange("(c p) t -> p c t", p=128)[:, :, tt * 128:(tt + 1) * 128],
                in_=stg_xn[:, :].rearrange("p (c t) -> p c t", c=NC)), 16)
            ST.bump(g.dma_start(
                out=k.scr["xT"][b].rearrange("(c p) t -> p c t", p=128)[:, :, tt * 128:(tt + 1) * 128],
                in_=stg_x[:, :].rearrange("p (c t) -> p c t", c=NC)), 16)
            if tt < NT - 1:
                k.sc.wait_ge(ST.sem, ST.v)
        k.gp.wait_ge(ST.sem, ST.v)


def emit_bisect(k: K, b: int):
    nc = k.nc
    VE, SC, PE, ST, LD = k.s_ve, k.s_sc, k.s_pe, k.s_st, k.s_ld
    v, s, t, g = k.ve, k.sc, k.te, k.gp
    z = k.sb["z_sb"]
    st0 = k.sb["bst_sb"]
    zr = z[0:1, :]
    st = lambda i: st0[0:1, 16 * b + i:16 * b + i + 1]
    with ExitStack() as ctx:
        cmp_row = k.tmp(ctx, "bcmp", [1, S])
        aug = k.tmp(ctx, "baug", [1, S])
        aug_a = k.tmp(ctx, "bauga", [1, S])
        gsc = k.tmp(ctx, "bgsc", [1, S])
        psg = k.ptmp(ctx, "bpsg", [128, 128])

        k.vs(v.reduce_max(st(0), zr, axis=AX.X))
        k.vs(v.tensor_reduce(st(1), zr, axis=AX.X, op=ALU.min))
        v.tensor_scalar(st(2), st(1), -1.0, None, op0=ALU.add)
        k.vs(v.tensor_scalar(st(3), st(0), 1.0, None, op0=ALU.add))
        for it in range(N_BISECT):
            k.vs(v.tensor_add(st(4), st(2), st(3)))
            k.vs(v.tensor_scalar(st(4), st(4), 0.5, None, op0=ALU.mult))
            k.vs(v.tensor_scalar(cmp_row[0:1, :], zr, st(4), 0.0,
                                 op0=ALU.is_ge, op1=ALU.add, accum_out=st(5)))
            k.vs(v.tensor_scalar(st(6), st(5), 1024.0, None, op0=ALU.is_ge))
            k.vs(v.tensor_sub(st(7), st(4), st(2)))
            k.vs(v.tensor_mul(st(7), st(7), st(6)))
            k.vs(v.tensor_scalar(st(8), st(6), -1.0, 1.0, op0=ALU.mult, op1=ALU.add))
            k.vs(v.tensor_sub(st(9), st(4), st(3)))
            k.vs(v.tensor_mul(st(9), st(9), st(8)))
            k.vs(v.tensor_add(st(2), st(2), st(7)))
            k.vs(v.tensor_add(st(3), st(3), st(9)))
        vau = k.vs(v.tensor_scalar(aug[0:1, :], zr, st(2), NEGAUG,
                                   op0=ALU.is_lt, op1=ALU.mult))
        s.wait_ge(VE.sem, vau)
        saug = SC.bump(s.copy(aug_a[0:1, :], aug[0:1, :]))
        g.wait_ge(SC.sem, saug)
        sta = ST.bump(g.dma_start(out=k.scr["augr"][b, :], in_=aug_a[0:1, :]), 16)
        s.wait_ge(VE.sem, vau)
        sc0 = SC.bump(s.activation(cmp_row[0:1, :], zr, AF.Exp, bias=0.0, scale=-1.0))
        v.wait_ge(SC.sem, sc0)
        k.vs(v.tensor_scalar(cmp_row[0:1, :], cmp_row[0:1, :], 1.0, None, op0=ALU.add))
        k.vs(v.reciprocal(gsc[0:1, :], cmp_row[0:1, :]))
        k.vs(v.tensor_scalar(cmp_row[0:1, :], zr, st(2), None, op0=ALU.is_ge))
        if k.dbg:
            g.wait_ge(VE.sem, VE.v)
            ST.bump(g.dma_start(out=k.scr["zdbg"][b, 4, :], in_=gsc[0:1, :]), 16)
            ST.bump(g.dma_start(out=k.scr["zdbg"][b, 5, :], in_=cmp_row[0:1, :]), 16)
            g.wait_ge(ST.sem, ST.v)
        vg = k.vs(v.tensor_mul(gsc[0:1, :], gsc[0:1, :], cmp_row[0:1, :]))
        t.wait_ge(VE.sem, vg)
        for tt in range(NT):
            pc = PE.bump(t.transpose(psg[:, tt % 2:tt % 2 + 1],
                        gsc[0:1, tt * 128:(tt + 1) * 128], k.ident[0:1, 0:1]))
            s.wait_ge(PE.sem, pc)
            scv = SC.bump(s.copy(k.sb["gsccol_sb"][:, b * NT + tt:b * NT + tt + 1],
                                 psg[:, tt % 2:tt % 2 + 1]))
            if tt < NT - 1:
                t.wait_ge(SC.sem, scv)
        v.wait_ge(SC.sem, SC.v)
        vgm = k.vs(v.tensor_scalar(k.sb["lngsc_sb"][:, b * NT:(b + 1) * NT],
                                   k.sb["gsccol_sb"][:, b * NT:(b + 1) * NT],
                                   1e-30, None, op0=ALU.max))
        s.wait_ge(VE.sem, vgm)
        sgl = SC.bump(s.activation(k.sb["lngsc_sb"][:, b * NT:(b + 1) * NT],
                                   k.sb["lngsc_sb"][:, b * NT:(b + 1) * NT],
                                   AF.Ln, bias=0.0, scale=1.0))
        v.wait_ge(SC.sem, sgl)
        k.vs(v.tensor_scalar(k.sb["lngsc_sb"][:, b * NT:(b + 1) * NT],
                             k.sb["lngsc_sb"][:, b * NT:(b + 1) * NT],
                             8.0, None, op0=ALU.mult))
        if k.dbg:
            g.wait_ge(VE.sem, VE.v)
            g.wait_ge(SC.sem, SC.v)
            ST.bump(g.dma_start(out=k.scr["zdbg"][b, 0, :], in_=zr), 16)
            ST.bump(g.dma_start(out=k.scr["zdbg"][b, 1, :], in_=aug[0:1, :]), 16)
            ST.bump(g.dma_start(out=k.scr["zdbg"][b, 2, :], in_=gsc[0:1, :]), 16)
            ST.bump(g.dma_start(out=k.scr["zdbg"][b, 3, 0:16],
                                in_=st0[0:1, 16 * b:16 * b + 16]), 16)
        g.wait_ge(ST.sem, ST.v)


def emit_proj(k: K, b: int, xin, specs):
    nc = k.nc
    LD, PE, VE, SC, ST = k.s_ld, k.s_pe, k.s_ve, k.s_sc, k.s_st
    v, s, t, g, sy = k.ve, k.sc, k.te, k.gp, k.sy
    with ExitStack() as ctx:
        xn_sb = k.tmp(ctx, "prx", [128, NC * S])
        w_sb = k.tmp(ctx, "prw", [128, NC * D])
        stg = [k.tmp(ctx, "prs", [128, 512]) for _ in range(2)]
        vstg = k.tmp(ctx, "prv", [128, 512])
        vt_stage = k.tmp(ctx, "prvt", [128, 4 * D])
        ps = k.ptmp(ctx, "prps", [128, 1024])
        ps_t = k.ptmp(ctx, "prpst", [128, 512])

        ldx = LD.bump(sy.dma_start(
            out=xn_sb[:, :].rearrange("p (c t) -> p c t", c=NC),
            in_=xin.rearrange("(c p) t -> p c t", p=128)), 16)

        evd = {}
        std = {}
        git = 0
        for si, spec in enumerate(specs):
            noc = spec["nout"] // 128
            if si > 0:
                sy.wait_ge(PE.sem, PE.v)
            ldw = LD.bump(sy.dma_start(
                out=w_sb[:, 0:NC * spec["nout"]].rearrange("p (c o) -> p c o", c=NC),
                in_=k.ins[spec["w"]].rearrange("(c p) o -> p c o", p=128)), 16)
            bias = k.bias[spec["bias"]] if spec["bias"] else None
            t.wait_ge(LD.sem, max(ldx, ldw))

            if spec.get("dst_tm") is None:
                for oc in range(noc):
                    for ts4 in range(4):
                        bank = git % 2
                        if git >= 2:
                            eng_, cval = evd[git - 2]
                            eng_.wait_ge(cval[0], cval[1])
                        last = None
                        for c in range(NC):
                            last = t.matmul(
                                ps[:, bank * 512:bank * 512 + 512],
                                w_sb[:, c * spec["nout"] + oc * 128:
                                     c * spec["nout"] + oc * 128 + 128],
                                xn_sb[:, c * S + ts4 * 512:c * S + ts4 * 512 + 512],
                                start=(c == 0), stop=(c == NC - 1))
                        pc = PE.bump(last)
                        if git >= 2 and (git - 2) in std:
                            STP2 = k.s_sta if git % 2 == 0 else k.s_stb
                            (v if bias else s).wait_ge(STP2.sem, std[git - 2])
                        if bias:
                            v.wait_ge(PE.sem, pc)
                            VE.bump(v.tensor_scalar(
                                stg[bank][:, :], ps[:, bank * 512:bank * 512 + 512],
                                bias[:, oc:oc + 1], None, op0=ALU.add))
                            fc = k.vflush()
                            evd[git] = (t, (VE.sem, fc))
                            g.wait_ge(VE.sem, fc)
                        else:
                            s.wait_ge(PE.sem, pc)
                            sc_ = SC.bump(s.copy(
                                stg[bank][:, :], ps[:, bank * 512:bank * 512 + 512]))
                            evd[git] = (t, (SC.sem, sc_))
                            g.wait_ge(SC.sem, sc_)
                        STP = k.s_sta if git % 2 == 0 else k.s_stb
                        std[git] = STP.bump(g.dma_start(
                            out=spec["dst"][oc * 128:(oc + 1) * 128,
                                            ts4 * 512:(ts4 + 1) * 512],
                            in_=stg[bank][:, :]), 16)
                        git += 1
                g.wait_ge(k.s_sta.sem, k.s_sta.v)
                g.wait_ge(k.s_stb.sem, k.s_stb.v)
            else:
                for ts4 in range(4):
                    for oc in range(noc):
                        it = ts4 * noc + oc
                        bank = git % 2
                        if git >= 2:
                            eng_, cval = evd[git - 2]
                            eng_.wait_ge(cval[0], cval[1])
                        last = None
                        for c in range(NC):
                            last = t.matmul(
                                ps[:, bank * 512:bank * 512 + 512],
                                w_sb[:, c * spec["nout"] + oc * 128:
                                     c * spec["nout"] + oc * 128 + 128],
                                xn_sb[:, c * S + ts4 * 512:c * S + ts4 * 512 + 512],
                                start=(c == 0), stop=(c == NC - 1))
                        pc = PE.bump(last)
                        if it == 0:
                            trd, cpd = {}, {}
                        if bias:
                            v.wait_ge(PE.sem, pc)
                            ev_ = VE.bump(v.tensor_scalar(
                                vstg[:, :], ps[:, bank * 512:bank * 512 + 512],
                                bias[:, oc:oc + 1], None, op0=ALU.add))
                            evd[git] = (t, (VE.sem, ev_))
                            t.wait_ge(VE.sem, ev_)
                        else:
                            s.wait_ge(PE.sem, pc)
                            ev_ = SC.bump(s.copy(
                                vstg[:, :], ps[:, bank * 512:bank * 512 + 512]))
                            evd[git] = (t, (SC.sem, ev_))
                            t.wait_ge(SC.sem, ev_)
                        if it >= 1:
                            t.wait_ge(SC.sem, cpd[it - 1])
                        last = None
                        for tc in range(4):
                            last = t.transpose(ps_t[:, tc * 128:tc * 128 + 128],
                                               vstg[:, tc * 128:tc * 128 + 128],
                                               k.ident[:, :])
                        trd[it] = PE.bump(last)
                        s.wait_ge(PE.sem, trd[it])
                        last = None
                        for tc in range(4):
                            last = s.copy(
                                vt_stage[:, tc * spec["nout"] + oc * 128:
                                         tc * spec["nout"] + oc * 128 + 128],
                                ps_t[:, tc * 128:tc * 128 + 128])
                        cpd[it] = SC.bump(last)
                        git += 1
                    g.wait_ge(SC.sem, cpd[ts4 * noc + noc - 1])
                    stv = ST.bump(g.dma_start(
                        out=spec["dst_tm"].rearrange("(tb tc p) f -> tb p tc f",
                                                     p=128, tc=4)[ts4][:, :,
                                                     spec["col0"]:spec["col0"] + spec["nout"]],
                        in_=vt_stage[:, 0:4 * spec["nout"]].rearrange(
                            "p (tc f) -> p tc f", tc=4)), 16)
                    s.wait_ge(ST.sem, stv)
        g.wait_ge(ST.sem, ST.v)
        g.wait_ge(k.s_sta.sem, k.s_sta.v)
        g.wait_ge(k.s_stb.sem, k.s_stb.v)


def emit_attn(k: K, b: int, kind: str):
    nc = k.nc
    LD, PE, VE, SC, ST = k.s_ld, k.s_pe, k.s_ve, k.s_sc, k.s_st
    v, s, t, g, sy = k.ve, k.sc, k.te, k.gp, k.sy
    scale = {"l0a": 1.0 / 32.0, "l0f": 1.0 / 8.0, "l1": 1.0 / 8.0}[kind]
    inv_scale = 1.0 / scale
    CK = 128
    causal = (kind == "l1")
    outscr = k.scr["ofT"] if kind == "l0f" else k.scr["oaT"]

    with ExitStack() as ctx:
        qh = [k.tmp(ctx, "aq", [128, S]) for _ in range(2)]
        kh = [k.tmp(ctx, "ak", [128, S]) for _ in range(2)]
        vt = [k.tmp(ctx, "av", [128, NT * DH]) for _ in range(2)]
        uT = [k.tmp(ctx, "au", [128, NT * 512]) for _ in range(2)]
        pdum = k.tmp(ctx, "apd", [128, S])
        ofm = k.tmp(ctx, "aof", [128, S])
        mall = k.tmp(ctx, "am", [128, NT])
        sall = k.tmp(ctx, "as", [128, NT])
        lns = k.tmp(ctx, "al", [128, NT])
        smt = k.tmp(ctx, "asm", [128, 8])
        nm16 = k.tmp(ctx, "anm", [16, 128])
        srow = k.tmp(ctx, "asr", [1, S])
        srow_a = k.tmp(ctx, "asra", [1, S])
        ones2k = k.tmp(ctx, "ao2", [1, S])
        if causal:
            iota_j = k.tmp(ctx, "aij", [1, S])
            mtri = k.tmp(ctx, "amt", [128, 4 * 512])
            mc1 = k.tmp(ctx, "amc", [128, 4 * 512])
            dstage = k.tmp(ctx, "ads", [128, 4 * 512])
        mgall = k.tmp(ctx, "amg", [128, 64])
        sgall = k.tmp(ctx, "asg", [128, 64])
        tm64 = k.tmp(ctx, "atm", [128, 64])
        s_ps = k.ptmp(ctx, "aps", [128, 2048])
        oav_ps = k.ptmp(ctx, "apo", [64, 1024])

        v.memset(ones2k[0:1, :], 1.0)
        vz = None
        for bb in range(2):
            v.memset(qh[bb][64:128, :], 0.0)
            vz = v.memset(kh[bb][64:128, :], 0.0)
        vz = VE.bump(vz)
        if causal:
            gi = k.s_gp.bump(g.iota(iota_j[0:1, :], pattern=[[1, S]], base=0,
                                    channel_multiplier=0,
                                    allow_small_or_imprecise_dtypes=True))
            v.wait_ge(k.s_gp.sem, gi)
            for r_ in range(4):
                v.tensor_scalar(mtri[:, r_ * 512:(r_ + 1) * 512], k.r512[:, :],
                                float(128 * r_), -BIG, op0=ALU.is_lt, op1=ALU.mult)
            for r_ in range(4):
                v.tensor_scalar(mc1[:, r_ * 512:(r_ + 1) * 512], k.r512[:, :],
                                float(128 * r_), -BIG, op0=ALU.is_gt, op1=ALU.mult)
            k.vs(v.memset(smt[:, 7:8], 0.0))

        def issue_loads(h):
            hb = h % 2
            LDP = k.s_lda if hb == 0 else k.s_ldb
            kvh = h // 4 if kind == "l1" else h
            LDP.bump(sy.dma_start(out=qh[hb][0:64, :],
                                  in_=k.scr["qT"][b][h * 64:(h + 1) * 64, :]), 16)
            LDP.bump(sy.dma_start(out=kh[hb][0:64, :],
                                  in_=k.scr["kT"][b][kvh * 64:(kvh + 1) * 64, :]), 16)
            l3 = LDP.bump(sy.dma_start(
                out=vt[hb][:, :].rearrange("p (kc d) -> p kc d", kc=NT),
                in_=k.scr["vTm"][b].rearrange("(kc p) f -> p kc f", p=128)[
                    :, :, kvh * 64:(kvh + 1) * 64]), 16)
            if kind == "l0f":
                l3 = LDP.bump(sy.dma_start(out=kh[hb][65:66, :],
                                           in_=k.scr["augr"][b, :]), 16)
            return l3

        def aug_rows(h, ldt):
            hb = h % 2
            LDP = k.s_lda if hb == 0 else k.s_ldb
            LA = k.s_la
            v.wait_ge(LDP.sem, ldt)
            v.wait_ge(VE.sem, vz)
            v.tensor_copy(kh[hb][64:65, :], ones2k[0:1, :])
            av = VE.bump(v.memset(qh[hb][64:65, :], 0.0))
            al = LA.v
            if kind == "l0f":
                sy.wait_ge(VE.sem, av)
                al = LA.bump(sy.dma_start(out=qh[hb][65:66, :],
                                          in_=ones2k[0:1, :]), 16)
            elif kind == "l1":
                vsr = k.vs(v.tensor_scalar(srow[0:1, :], ones2k[0:1, :],
                                           k.slop[0:1, h:h + 1], inv_scale,
                                           op0=ALU.mult, op1=ALU.mult))
                s.wait_ge(VE.sem, vsr)
                if h > 0:
                    s.wait_ge(LA.sem, LA.v)
                sca = SC.bump(s.copy(srow_a[0:1, :], srow[0:1, :]))
                sy.wait_ge(SC.sem, sca)
                LA.bump(sy.dma_start(out=kh[hb][65:66, :],
                                     in_=iota_j[0:1, :]), 16)
                al = LA.bump(sy.dma_start(out=qh[hb][65:66, :],
                                          in_=srow_a[0:1, :]), 16)
            return av, al

        ld_t = issue_loads(0)
        aug_t, aug_l = aug_rows(0, ld_t)
        free_at = {}
        for h in range(H):
            hb = h % 2
            if h + 1 < H:
                if h >= 1:
                    sy.wait_ge(PE.sem, free_at[h - 1])
                ld_next = issue_loads(h + 1)

            t.wait_ge((k.s_lda if hb == 0 else k.s_ldb).sem, ld_t)
            t.wait_ge(k.s_la.sem, aug_l)
            t.wait_ge(VE.sem, aug_t)

            t.wait_ge(SC.sem, SC.v)
            expg = {}
            gcnt = 0
            if causal:
                v.memset(mgall[:, :], -BIG)
                k.vs(v.memset(sgall[:, :], 0.0))
            for qt in range(NT):
                nk = (qt // 4 + 1) if causal else 4
                for kc in range(nk):
                    bank = gcnt % 4
                    slot = gcnt % 4
                    col = qt * 4 + kc
                    if gcnt >= 4:
                        t.wait_ge(SC.sem, expg[gcnt - 4])
                    pc = PE.bump(t.matmul(s_ps[:, bank * 512:(bank + 1) * 512],
                                          qh[hb][0:CK, qt * 128:(qt + 1) * 128],
                                          kh[hb][0:CK, kc * 512:(kc + 1) * 512],
                                          start=True, stop=True))
                    v.wait_ge(PE.sem, pc)
                    if causal and kc == qt // 4:
                        vmc = VE.bump(v.tensor_add(
                            s_ps[:, bank * 512:(bank + 1) * 512],
                            s_ps[:, bank * 512:(bank + 1) * 512],
                            mc1[:, (qt % 4) * 512:(qt % 4 + 1) * 512]))
                        v.wait_ge(VE.sem, vmc)
                    if gcnt >= 4:
                        v.wait_ge(SC.sem, expg[gcnt - 4])
                    k.vs(v.reduce_max(mgall[:, col:col + 1],
                                      s_ps[:, bank * 512:(bank + 1) * 512], axis=AX.X))
                    vneg = k.vs(v.tensor_scalar(smt[:, slot:slot + 1],
                                                mgall[:, col:col + 1],
                                                -scale, None, op0=ALU.mult))
                    s.wait_ge(VE.sem, vneg)
                    expg[gcnt] = SC.bump(s.activation(
                        pdum[:, kc * 512:(kc + 1) * 512],
                        s_ps[:, bank * 512:(bank + 1) * 512],
                        AF.Exp, bias=smt[:, slot:slot + 1], scale=scale,
                        accum_out=sgall[:, col:col + 1]))
                    gcnt += 1

            exp1_last = expg[gcnt - 1]
            v.wait_ge(SC.sem, exp1_last)
            mg3 = mgall[:, :].rearrange("p (q g) -> p q g", g=4)
            k.vs(v.tensor_reduce(mall[:, :], mg3, axis=AX.X, op=ALU.max))
            last = None
            for g_ in range(4):
                last = v.scalar_tensor_tensor(
                    tm64[:, :].rearrange("p (q g) -> p q g", g=4)[:, :, g_],
                    mall[:, :], -1.0,
                    mg3[:, :, g_], op0=ALU.mult, op1=ALU.add)
            k.vs(last)
            vc4 = k.vs(v.tensor_scalar(tm64[:, :], tm64[:, :], scale, None,
                                       op0=ALU.mult))
            s.wait_ge(VE.sem, vc4)
            scm = SC.bump(s.activation(tm64[:, :], tm64[:, :], AF.Exp, bias=0.0,
                                       scale=1.0))
            v.wait_ge(SC.sem, scm)
            k.vs(v.tensor_mul(tm64[:, :], tm64[:, :], sgall[:, :]))
            k.vs(v.tensor_reduce(sall[:, :], tm64[:, :].rearrange("p (q g) -> p q g", g=4),
                                 axis=AX.X, op=ALU.add))

            s.wait_ge(VE.sem, VE.v)
            sln = SC.bump(s.activation(lns[:, :], sall[:, :], AF.Ln, bias=0.0,
                                       scale=1.0))
            v.wait_ge(SC.sem, sln)
            k.vs(v.tensor_scalar(pdum[:, 0:NT], lns[:, :], inv_scale, None,
                                 op0=ALU.mult))
            vw = k.vs(v.tensor_add(pdum[:, 0:NT], pdum[:, 0:NT], mall[:, :]))
            if kind == "l0f":
                vw = k.vs(v.tensor_sub(pdum[:, 0:NT], pdum[:, 0:NT],
                                       k.sb["lngsc_sb"][:, b * NT:(b + 1) * NT]))
            t.wait_ge(VE.sem, vw)
            t.wait_ge(SC.sem, SC.v)
            ptr = PE.bump(t.transpose(s_ps[0:16, 0:128], pdum[:, 0:NT],
                                      k.ident[:, :]))
            s.wait_ge(PE.sem, ptr)
            if h > 0:
                s.wait_ge(k.s_lm.sem, prev_negm_st)
            vng = SC.bump(s.activation(nm16[:, :], s_ps[0:16, 0:128], AF.Copy,
                                       bias=0.0, scale=-1.0))
            sy.wait_ge(SC.sem, vng)
            prev_negm_st = k.s_lm.bump(sy.dma_start(out=k.scr["negm"][:, :],
                                                    in_=nm16[:, :]), 16)
            sy.wait_ge(k.s_lm.sem, prev_negm_st)
            ldm = k.s_lm.bump(sy.dma_start(
                out=qh[hb][64:65, :],
                in_=k.scr["negm"].rearrange("a j -> (a j)")[None, :]), 16)

            t.wait_ge(k.s_lm.sem, ldm)
            uexp = {}
            for qt4 in range(4):
                ub = uT[qt4 % 2]
                nkk = 16 if not causal else (4 * qt4 + 4)
                for kc in range(nkk):
                    bank = kc % 4
                    if qt4 == 0 and kc == 0:
                        t.wait_ge(SC.sem, max(exp1_last, vng))
                    if kc >= 4:
                        t.wait_ge(SC.sem, uexp[(qt4, kc - 4)])
                    elif qt4 > 0:
                        t.wait_ge(SC.sem, uexp[(qt4 - 1, prev_nkk - 4 + kc)])
                    pc = PE.bump(t.matmul(s_ps[:, bank * 512:(bank + 1) * 512],
                                          kh[hb][0:CK, kc * 128:(kc + 1) * 128],
                                          qh[hb][0:CK, qt4 * 512:(qt4 + 1) * 512],
                                          start=True, stop=True))
                    diag = causal and (kc >= 4 * qt4)
                    if diag:
                        v.wait_ge(PE.sem, pc)
                        r = kc - 4 * qt4
                        vst = VE.bump(v.scalar_tensor_tensor(
                            dstage[:, r * 512:(r + 1) * 512],
                            s_ps[:, bank * 512:(bank + 1) * 512],
                            scale, mtri[:, r * 512:(r + 1) * 512],
                            op0=ALU.mult, op1=ALU.add))
                        s.wait_ge(VE.sem, vst)
                        uexp[(qt4, kc)] = SC.bump(s.activation(
                            ub[:, kc * 512:(kc + 1) * 512],
                            dstage[:, r * 512:(r + 1) * 512],
                            AF.Exp, bias=0.0, scale=1.0))
                    else:
                        s.wait_ge(PE.sem, pc)
                        uexp[(qt4, kc)] = SC.bump(s.activation(
                            ub[:, kc * 512:(kc + 1) * 512],
                            s_ps[:, bank * 512:(bank + 1) * 512],
                            AF.Exp, bias=0.0, scale=scale))
                prev_nkk = nkk
                t.wait_ge(SC.sem, uexp[(qt4, nkk - 1)])
                if h > 0 or qt4 >= 2:
                    t.wait_ge(SC.sem, oev[qt4 % 2])
                obank = oav_ps[:, (qt4 % 2) * 512:(qt4 % 2) * 512 + 512]
                last = None
                for kc in range(nkk):
                    last = t.matmul(obank,
                                    vt[hb][:, kc * 64:(kc + 1) * 64],
                                    ub[:, kc * 512:(kc + 1) * 512],
                                    start=(kc == 0), stop=(kc == nkk - 1))
                pav = PE.bump(last)
                s.wait_ge(PE.sem, pav)
                if h == 0 and qt4 == 0:
                    oev = {}
                oev[qt4 % 2] = SC.bump(s.copy(
                    ofm[64 * (h % 2):64 * (h % 2) + 64, qt4 * 512:(qt4 + 1) * 512],
                    obank))
            free_at[h] = PE.v
            if h % 2 == 1:
                g.wait_ge(SC.sem, SC.v)
                stf = ST.bump(g.dma_start(
                    out=outscr[b][(h // 2) * 128:(h // 2) * 128 + 128, :],
                    in_=ofm[:, :]), 16)
                s.wait_ge(ST.sem, stf)
            if h + 1 < H:
                ld_t = ld_next
                aug_t, aug_l = aug_rows(h + 1, ld_t)
        g.wait_ge(ST.sem, ST.v)


def ln_tile(k, v, s, buf, xc, st1, sqd):
    VE, SC = k.s_ve, k.s_sc
    k.vs(v.reduce_sum(st1[:, 0:1], buf[:, :], axis=AX.X))
    vc = k.vs(v.tensor_scalar(st1[:, 1:2], st1[:, 0:1], -1.0 / D, None, op0=ALU.mult))
    s.wait_ge(VE.sem, vc)
    sc0 = SC.bump(s.activation(sqd[:, :], buf[:, :], AF.Square, bias=0.0,
                               scale=1.0, accum_out=st1[:, 2:3]))
    v.wait_ge(SC.sem, sc0)
    v.tensor_scalar(st1[:, 3:4], st1[:, 2:3], 1.0 / D, None, op0=ALU.mult)
    k.vs(v.tensor_mul(st1[:, 4:5], st1[:, 1:2], st1[:, 1:2]))
    k.vs(v.tensor_sub(st1[:, 3:4], st1[:, 3:4], st1[:, 4:5]))
    vc = k.vs(v.tensor_scalar(st1[:, 3:4], st1[:, 3:4], EPS, None, op0=ALU.add))
    s.wait_ge(VE.sem, vc)
    k.ss(s.activation(st1[:, 7:8], st1[:, 3:4], AF.Ln, bias=0.0, scale=1.0))
    sc1 = k.s_sc.bump(s.activation(st1[:, 4:5], st1[:, 7:8], AF.Exp, bias=0.0,
                                   scale=-0.5))
    v.wait_ge(SC.sem, sc1)
    k.vs(v.tensor_mul(st1[:, 5:6], st1[:, 4:5], st1[:, 4:5]))
    k.vs(v.tensor_mul(st1[:, 5:6], st1[:, 5:6], st1[:, 3:4]))
    k.vs(v.tensor_scalar(st1[:, 5:6], st1[:, 5:6], -0.5, 1.5, op0=ALU.mult, op1=ALU.add))
    k.vs(v.tensor_mul(st1[:, 6:7], st1[:, 4:5], st1[:, 5:6]))
    k.vs(v.tensor_scalar(xc[:, :], buf[:, :], st1[:, 1:2], None, op0=ALU.add))
    return k.s_ve.bump(v.tensor_scalar(xc[:, :], xc[:, :], st1[:, 6:7], None,
                                       op0=ALU.mult))


def emit_combine_l0(k: K, b: int):
    nc = k.nc
    LD, PE, VE, SC, ST = k.s_ld, k.s_pe, k.s_ve, k.s_sc, k.s_st
    v, s, t, g, sy = k.ve, k.sc, k.te, k.gp, k.sy
    with ExitStack() as ctx:
        a0_sb = k.tmp(ctx, "cba", [128, NC * 512])
        f0_sb = k.tmp(ctx, "cbf", [128, NC * 512])
        x_sb = k.tmp(ctx, "cbx", [128, NC * 512])
        x1stg = k.tmp(ctx, "cbs", [128, NC * 512])
        lnt = k.tmp(ctx, "cbl", [128, D])
        xnt = k.tmp(ctx, "cbn", [128, D])
        sqd = k.tmp(ctx, "cbq", [128, D])
        st1 = k.tmp(ctx, "cbst", [128, 12])
        xn1stg = k.tmp(ctx, "cbg", [128, D])
        ps_t = k.ptmp(ctx, "cbp", [128, 1024])
        ps_t2 = k.ptmp(ctx, "cbp2", [128, 1024])

        for ts4 in range(4):
            sl = slice(ts4 * 512, (ts4 + 1) * 512)
            l1 = LD.bump(sy.dma_start(
                out=a0_sb[:, :].rearrange("p (c t) -> p c t", c=NC),
                in_=k.scr["qT"][b].rearrange("(c p) t -> p c t", p=128)[:, :, sl]), 16)
            l2 = LD.bump(sy.dma_start(
                out=f0_sb[:, :].rearrange("p (c t) -> p c t", c=NC),
                in_=k.scr["kT"][b].rearrange("(c p) t -> p c t", p=128)[:, :, sl]), 16)
            l3 = LD.bump(sy.dma_start(
                out=x_sb[:, :].rearrange("p (c t) -> p c t", c=NC),
                in_=k.scr["xT"][b].rearrange("(c p) t -> p c t", p=128)[:, :, sl]), 16)
            v.wait_ge(LD.sem, max(l1, l2, l3))
            lastv = None
            for c in range(NC):
                csl = slice(c * 512, (c + 1) * 512)
                lastv = v.tensor_add(x1stg[:, csl], x_sb[:, csl], a0_sb[:, csl])
            k.vs(lastv)
            vx1 = None
            for c in range(NC):
                csl = slice(c * 512, (c + 1) * 512)
                vx1 = v.tensor_add(x1stg[:, csl], x1stg[:, csl], f0_sb[:, csl])
            VE.bump(vx1)
            vx1c = k.vflush()
            g.wait_ge(VE.sem, vx1c)
            st1c = ST.bump(g.dma_start(
                out=k.scr["x1T"][b].rearrange("(c p) t -> p c t", p=128)[:, :, sl],
                in_=x1stg[:, :].rearrange("p (c t) -> p c t", c=NC)), 16)
            for tc in range(4):
                t.wait_ge(VE.sem, vx1c)
                if not (ts4 == 0 and tc == 0):
                    t.wait_ge(SC.sem, sc_ev)
                last = None
                for c in range(NC):
                    last = t.transpose(ps_t[:, c * 128:(c + 1) * 128],
                                       x1stg[:, c * 512 + tc * 128:c * 512 + tc * 128 + 128],
                                       k.ident[:, :])
                ptc = PE.bump(last)
                s.wait_ge(PE.sem, ptc)
                last = None
                for c in range(NC):
                    last = s.copy(lnt[:, c * 128:(c + 1) * 128],
                                  ps_t[:, c * 128:(c + 1) * 128])
                sc_ev = SC.bump(last)
                v.wait_ge(SC.sem, sc_ev)
                vxn = ln_tile(k, v, s, lnt, xnt, st1, sqd)
                t.wait_ge(VE.sem, vxn)
                if not (ts4 == 0 and tc == 0):
                    t.wait_ge(SC.sem, sc_ev2)
                last = None
                for c in range(NC):
                    last = t.transpose(ps_t2[:, c * 128:(c + 1) * 128],
                                       xnt[:, c * 128:(c + 1) * 128], k.ident[:, :])
                ptc2 = PE.bump(last)
                s.wait_ge(PE.sem, ptc2)
                if not (ts4 == 0 and tc == 0):
                    s.wait_ge(ST.sem, st_xn)
                last = None
                for c in range(NC):
                    last = s.copy(xn1stg[:, c * 128:(c + 1) * 128],
                                  ps_t2[:, c * 128:(c + 1) * 128])
                sc_ev2 = SC.bump(last)
                g.wait_ge(SC.sem, sc_ev2)
                st_xn = ST.bump(g.dma_start(
                    out=k.scr["xn1T"][b].rearrange("(c p) t -> p c t", p=128)[
                        :, :, ts4 * 512 + tc * 128:ts4 * 512 + tc * 128 + 128],
                    in_=xn1stg[:, :].rearrange("p (c t) -> p c t", c=NC)), 16)
            sy.wait_ge(PE.sem, PE.v)
            sy.wait_ge(ST.sem, st1c)
        g.wait_ge(ST.sem, ST.v)


def emit_moe(k: K):
    nc = k.nc
    LD, PE, VE, SC, ST = k.s_ld, k.s_pe, k.s_ve, k.s_sc, k.s_st
    v, s, t, g, sy = k.ve, k.sc, k.te, k.gp, k.sy
    with ExitStack() as ctx:
        xn_sb = [k.tmp(ctx, "mx", [128, NC * 512]) for _ in range(2)]
        wg_sb = k.tmp(ctx, "mwg", [128, NC * 8])
        lgs = k.tmp(ctx, "mlg", [8, 512])
        lgtm = k.tmp(ctx, "mlt", [128, 8])
        cw = k.tmp(ctx, "mcw", [128, 16])
        coeff = k.tmp(ctx, "mcf", [128, 32 * 8])
        crow = k.tmp(ctx, "mcr", [1, 512])
        cbc = k.tmp(ctx, "mcb", [128, 512])
        w1_sb = k.tmp(ctx, "mw1", [128, NC * 2048])
        w2_sb = k.tmp(ctx, "mw2", [128, 16 * D])
        b1_sb = k.tmp(ctx, "mb1", [128, 16])
        b2_sb = k.tmp(ctx, "mb2", [128, 8])
        h_sb = k.tmp(ctx, "mh", [128, 16 * 512])
        mstg = [k.tmp(ctx, "mst", [128, 512]) for _ in range(2)]
        ps_g = k.ptmp(ctx, "mpg", [128, 512])
        ps_t = k.ptmp(ctx, "mpt", [128, 128])
        ps_h = k.ptmp(ctx, "mph", [128, 1024])
        ps_e = k.ptmp(ctx, "mpe", [128, 1024])

        ldg = LD.bump(sy.dma_start(
            out=wg_sb[:, :].rearrange("p (c o) -> p c o", c=NC),
            in_=k.ins["w1g"].rearrange("(c p) o -> p c o", p=128)), 16)

        for tb in range(8):
            b, ts4 = tb // 4, tb % 4
            sl = slice(ts4 * 512, (ts4 + 1) * 512)
            if tb > 0:
                sy.wait_ge(PE.sem, PE.v)
            lx = LD.bump(sy.dma_start(
                out=xn_sb[0][:, :].rearrange("p (c t) -> p c t", c=NC),
                in_=k.scr["xn1T"][b].rearrange("(c p) t -> p c t", p=128)[:, :, sl]), 16)
            t.wait_ge(LD.sem, max(ldg, lx))
            if tb > 0:
                t.wait_ge(SC.sem, lg_ev)
            last = None
            for c in range(NC):
                last = t.matmul(ps_g[0:8, :], wg_sb[:, c * 8:(c + 1) * 8],
                                xn_sb[0][:, c * 512:(c + 1) * 512],
                                start=(c == 0), stop=(c == NC - 1))
            pg = PE.bump(last)
            s.wait_ge(PE.sem, pg)
            lg_ev = SC.bump(s.copy(lgs[:, :], ps_g[0:8, :]))
            v.wait_ge(SC.sem, lg_ev)
            vb = k.vs(v.tensor_scalar(lgs[:, :], lgs[:, :], k.bias["b1g"][:, 0:1],
                                      None, op0=ALU.add))
            for tc in range(4):
                t.wait_ge(VE.sem, vb)
                if not (tb == 0 and tc == 0):
                    t.wait_ge(SC.sem, tr_ev)
                ptc = PE.bump(t.transpose(ps_t[0:128, 0:8],
                                          lgs[:, tc * 128:(tc + 1) * 128],
                                          k.ident[0:8, 0:8]))
                s.wait_ge(PE.sem, ptc)
                tr_ev = SC.bump(s.copy(lgtm[:, :], ps_t[0:128, 0:8]))
                v.wait_ge(SC.sem, tr_ev)
                k.vs(v.reduce_max(cw[:, 0:1], lgtm[:, :], axis=AX.X))
                k.vs(v.tensor_scalar(cw[:, 8:16], lgtm[:, :], cw[:, 0:1],
                                     None, op0=ALU.is_ge))
                k.vs(v.reduce_sum(cw[:, 1:2], cw[:, 8:16], axis=AX.X))
                k.vs(v.scalar_tensor_tensor(cbc[:, 0:8], cw[:, 8:16], -BIG,
                                            lgtm[:, :], op0=ALU.mult, op1=ALU.add))
                k.vs(v.reduce_max(cw[:, 2:3], cbc[:, 0:8], axis=AX.X))
                k.vs(v.tensor_scalar(cw[:, 3:4], cw[:, 1:2], 2.0, None, op0=ALU.is_lt))
                k.vs(v.tensor_scalar(cbc[:, 8:16], lgtm[:, :], cw[:, 2:3], None,
                                     op0=ALU.is_ge))
                k.vs(v.tensor_sub(cbc[:, 8:16], cbc[:, 8:16], cw[:, 8:16]))
                k.vs(v.scalar_tensor_tensor(cbc[:, 8:16], cbc[:, 8:16], cw[:, 3:4],
                                            cw[:, 8:16], op0=ALU.mult, op1=ALU.add))
                vng = k.vs(v.tensor_scalar(cw[:, 4:5], cw[:, 0:1], -1.0, None,
                                           op0=ALU.mult))
                s.wait_ge(VE.sem, vng)
                swx = SC.bump(s.activation(cbc[:, 16:24], lgtm[:, :], AF.Exp,
                                           bias=cw[:, 4:5], scale=1.0))
                v.wait_ge(SC.sem, swx)
                k.vs(v.tensor_mul(cbc[:, 16:24], cbc[:, 16:24], cbc[:, 8:16]))
                k.vs(v.reduce_sum(cw[:, 5:6], cbc[:, 16:24], axis=AX.X))
                k.vs(v.reciprocal(cw[:, 6:7], cw[:, 5:6]))
                k.vs(v.tensor_scalar(coeff[:, (tb * 4 + tc) * 8:(tb * 4 + tc) * 8 + 8],
                                     cbc[:, 16:24], cw[:, 6:7], None, op0=ALU.mult))

        vcoeff_done = VE.v
        t.wait_ge(VE.sem, vcoeff_done)
        t.wait_ge(SC.sem, SC.v)
        for e in range(8):
            sy.wait_ge(PE.sem, PE.v)
            lw1 = LD.bump(sy.dma_start(
                out=w1_sb[:, :].rearrange("p (c o) -> p c o", c=NC),
                in_=k.ins["w1e1"][e].rearrange("(c p) o -> p c o", p=128)), 16)
            lw2 = LD.bump(sy.dma_start(
                out=w2_sb[:, :].rearrange("p (c o) -> p c o", c=16),
                in_=k.ins["w1e2"][e].rearrange("(c p) o -> p c o", p=128)), 16)
            lb1 = LD.bump(sy.dma_start(out=b1_sb[:, :], in_=k.ins["b1e1"][e]), 16)
            lb2 = LD.bump(sy.dma_start(out=b2_sb[:, :], in_=k.ins["b1e2"][e]), 16)
            for tb in range(8):
                b, ts4 = tb // 4, tb % 4
                sl = slice(ts4 * 512, (ts4 + 1) * 512)
                xb = xn_sb[tb % 2]
                if e > 0 or tb >= 2:
                    sy.wait_ge(PE.sem, PE.v)
                lx = LD.bump(sy.dma_start(
                    out=xb[:, :].rearrange("p (c t) -> p c t", c=NC),
                    in_=k.scr["xn1T"][b].rearrange("(c p) t -> p c t", p=128)[:, :, sl]), 16)
                for tc in range(4):
                    if not (e == 0 and tb == 0 and tc == 0):
                        t.wait_ge(SC.sem, cr_ev)
                    ptc = PE.bump(t.transpose(
                        ps_t[0:1, 0:128],
                        coeff[:, (tb * 4 + tc) * 8 + e:(tb * 4 + tc) * 8 + e + 1],
                        k.ident[:, :]))
                    s.wait_ge(PE.sem, ptc)
                    cr_ev = SC.bump(s.copy(crow[0:1, tc * 128:(tc + 1) * 128],
                                           ps_t[0:1, 0:128]))
                t.wait_ge(SC.sem, cr_ev)
                if not (e == 0 and tb == 0):
                    t.wait_ge(VE.sem, cb_ev)
                pbc = PE.bump(t.matmul(ps_g[:, :], k.ones_row[:, :], crow[0:1, :],
                                       start=True, stop=True))
                v.wait_ge(PE.sem, pbc)
                cb_ev = k.vs(v.tensor_copy(cbc[:, :], ps_g[:, :]))
                t.wait_ge(LD.sem, max(lw1, lb1, lx))
                h_prev = h_ev if not (e == 0 and tb == 0) else None
                h_ev = {}
                for ffc in range(16):
                    bank = ffc % 2
                    if ffc >= 2:
                        t.wait_ge(SC.sem, h_ev[ffc - 2])
                    elif h_prev is not None:
                        t.wait_ge(SC.sem, h_prev[14 + ffc])
                    last = None
                    for c in range(NC):
                        last = t.matmul(ps_h[:, bank * 512:bank * 512 + 512],
                                        w1_sb[:, c * 2048 + ffc * 128:
                                              c * 2048 + ffc * 128 + 128],
                                        xb[:, c * 512:(c + 1) * 512],
                                        start=(c == 0), stop=(c == NC - 1))
                    ph = PE.bump(last)
                    s.wait_ge(PE.sem, ph)
                    if ffc == 0 and h_prev is not None:
                        s.wait_ge(PE.sem, eo_pe_last)
                    h_ev[ffc] = SC.bump(s.activation(
                        h_sb[:, ffc * 512:(ffc + 1) * 512],
                        ps_h[:, bank * 512:bank * 512 + 512],
                        AF.Relu, bias=b1_sb[:, ffc:ffc + 1], scale=1.0))
                t.wait_ge(LD.sem, max(lw2, lb2))
                t.wait_ge(SC.sem, h_ev[15])
                for dc in range(8):
                    bank = dc % 2
                    if dc >= 2:
                        t.wait_ge(VE.sem, m_ev[dc - 2])
                    elif not (e == 0 and tb == 0):
                        t.wait_ge(VE.sem, m_ev_prev[6 + dc])
                    last = None
                    for ffc in range(16):
                        last = t.matmul(ps_e[:, bank * 512:bank * 512 + 512],
                                        w2_sb[:, ffc * D + dc * 128:
                                              ffc * D + dc * 128 + 128],
                                        h_sb[:, ffc * 512:(ffc + 1) * 512],
                                        start=(ffc == 0), stop=(ffc == 15))
                    pe_ = PE.bump(last)
                    if dc == 0:
                        m_ev = {}
                    v.wait_ge(PE.sem, pe_)
                    if e > 0 or tb > 0 or dc >= 2:
                        STP2 = k.s_sta if dc % 2 == 0 else k.s_stb
                        v.wait_ge(STP2.sem, st_prev.get(dc, 0) if dc < 2 else st_d[dc - 2])
                    if dc == 0:
                        st_d = {}
                    VE.bump(v.scalar_tensor_tensor(
                        mstg[dc % 2][:, :], ps_e[:, bank * 512:bank * 512 + 512],
                        b2_sb[:, dc:dc + 1], cbc[:, :], op0=ALU.add, op1=ALU.mult))
                    m_ev[dc] = k.vflush()
                    g.wait_ge(VE.sem, m_ev[dc])
                    STP = k.s_sta if dc % 2 == 0 else k.s_stb
                    st_d[dc] = STP.bump(g.dma_start(
                        out=k.scr["mixT"][b][dc * 128:(dc + 1) * 128, sl],
                        in_=mstg[dc % 2][:, :],
                        accum_op=(ALU.add if e > 0 else ALU.bypass)), 16)
                eo_pe_last = PE.v
                m_ev_prev = dict(m_ev)
                st_prev = {0: st_d[6], 1: st_d[7]}
        g.wait_ge(ST.sem, ST.v)
        g.wait_ge(k.s_sta.sem, k.s_sta.v)
        g.wait_ge(k.s_stb.sem, k.s_stb.v)


def emit_final(k: K):
    nc = k.nc
    LD, PE, VE, SC, ST = k.s_ld, k.s_pe, k.s_ve, k.s_sc, k.s_st
    v, s, t, g, sy = k.ve, k.sc, k.te, k.gp, k.sy
    with ExitStack() as ctx:
        mix_sb = k.tmp(ctx, "fmx", [128, NC * 512])
        xn_sb = k.tmp(ctx, "fxn", [128, NC * 512])
        x1_sb = k.tmp(ctx, "fx1", [128, NC * 512])
        a1_sb = k.tmp(ctx, "fa1", [128, NC * 512])
        lnt = k.tmp(ctx, "fln", [128, D])
        xnt = k.tmp(ctx, "fxt", [128, D])
        sqd = k.tmp(ctx, "fsq", [128, D])
        st1 = k.tmp(ctx, "fst", [128, 12])
        ynT = k.tmp(ctx, "fyt", [128, D])
        outT = k.tmp(ctx, "fot", [128, D])
        outtm = k.tmp(ctx, "fom", [128, D])
        ps_1 = k.ptmp(ctx, "fp1", [128, 1024])
        ps_2 = k.ptmp(ctx, "fp2", [128, 1024])
        ps_3 = k.ptmp(ctx, "fp3", [128, 1024])

        for b in range(NB):
            for ts4 in range(4):
                sl = slice(ts4 * 512, (ts4 + 1) * 512)
                if not (b == 0 and ts4 == 0):
                    sy.wait_ge(VE.sem, VE.v)
                    sy.wait_ge(PE.sem, PE.v)
                l1_ = LD.bump(sy.dma_start(
                    out=mix_sb[:, :].rearrange("p (c t) -> p c t", c=NC),
                    in_=k.scr["mixT"][b].rearrange("(c p) t -> p c t", p=128)[:, :, sl]), 16)
                l2 = LD.bump(sy.dma_start(
                    out=xn_sb[:, :].rearrange("p (c t) -> p c t", c=NC),
                    in_=k.scr["xn1T"][b].rearrange("(c p) t -> p c t", p=128)[:, :, sl]), 16)
                l3 = LD.bump(sy.dma_start(
                    out=x1_sb[:, :].rearrange("p (c t) -> p c t", c=NC),
                    in_=k.scr["x1T"][b].rearrange("(c p) t -> p c t", p=128)[:, :, sl]), 16)
                l4 = LD.bump(sy.dma_start(
                    out=a1_sb[:, :].rearrange("p (c t) -> p c t", c=NC),
                    in_=k.scr["a1T"][b].rearrange("(c p) t -> p c t", p=128)[:, :, sl]), 16)
                v.wait_ge(LD.sem, max(l1_, l2, l3, l4))
                vy = VE.bump(v.tensor_add(mix_sb[:, :], mix_sb[:, :], xn_sb[:, :]))
                for tc in range(4):
                    csl = lambda c: slice(c * 512 + tc * 128, c * 512 + tc * 128 + 128)
                    t.wait_ge(VE.sem, vy)
                    if not (b == 0 and ts4 == 0 and tc == 0):
                        t.wait_ge(SC.sem, ev1)
                    last = None
                    for c in range(NC):
                        last = t.transpose(ps_1[:, c * 128:(c + 1) * 128],
                                           mix_sb[:, csl(c)], k.ident[:, :])
                    pt1 = PE.bump(last)
                    s.wait_ge(PE.sem, pt1)
                    last = None
                    for c in range(NC):
                        last = s.copy(lnt[:, c * 128:(c + 1) * 128],
                                      ps_1[:, c * 128:(c + 1) * 128])
                    ev1 = SC.bump(last)
                    v.wait_ge(SC.sem, ev1)
                    vxn = ln_tile(k, v, s, lnt, xnt, st1, sqd)
                    t.wait_ge(VE.sem, vxn)
                    if not (b == 0 and ts4 == 0 and tc == 0):
                        t.wait_ge(SC.sem, ev2)
                    last = None
                    for c in range(NC):
                        last = t.transpose(ps_2[:, c * 128:(c + 1) * 128],
                                           xnt[:, c * 128:(c + 1) * 128], k.ident[:, :])
                    pt2 = PE.bump(last)
                    s.wait_ge(PE.sem, pt2)
                    last = None
                    for c in range(NC):
                        last = s.copy(ynT[:, c * 128:(c + 1) * 128],
                                      ps_2[:, c * 128:(c + 1) * 128])
                    ev2 = SC.bump(last)
                    v.wait_ge(SC.sem, ev2)
                    lastv = None
                    for c in range(NC):
                        lastv = v.tensor_scalar(outT[:, c * 128:(c + 1) * 128],
                                        ynT[:, c * 128:(c + 1) * 128],
                                        k.bias["gamma"][:, c:c + 1], None, op0=ALU.mult)
                    k.vs(lastv)
                    for c in range(NC):
                        lastv = v.scalar_tensor_tensor(outT[:, c * 128:(c + 1) * 128],
                                               outT[:, c * 128:(c + 1) * 128],
                                               k.bias["beta"][:, c:c + 1],
                                               x1_sb[:, csl(c)],
                                               op0=ALU.add, op1=ALU.add)
                    k.vs(lastv)
                    vo = None
                    for c in range(NC):
                        vo = v.tensor_add(outT[:, c * 128:(c + 1) * 128],
                                          outT[:, c * 128:(c + 1) * 128],
                                          a1_sb[:, csl(c)])
                    vou = VE.bump(vo)
                    t.wait_ge(VE.sem, vou)
                    if not (b == 0 and ts4 == 0 and tc == 0):
                        t.wait_ge(SC.sem, ev3)
                    last = None
                    for c in range(NC):
                        last = t.transpose(ps_3[:, c * 128:(c + 1) * 128],
                                           outT[:, c * 128:(c + 1) * 128],
                                           k.ident[:, :])
                    pt3 = PE.bump(last)
                    s.wait_ge(PE.sem, pt3)
                    if not (b == 0 and ts4 == 0 and tc == 0):
                        s.wait_ge(ST.sem, sto)
                    last = None
                    for c in range(NC):
                        last = s.copy(outtm[:, c * 128:(c + 1) * 128],
                                      ps_3[:, c * 128:(c + 1) * 128])
                    ev3 = SC.bump(last)
                    g.wait_ge(SC.sem, ev3)
                    t0 = ts4 * 512 + tc * 128
                    sto = ST.bump(g.dma_start(
                        out=k.out[b, t0:t0 + 128, :], in_=outtm[:, :]), 16)
        g.wait_ge(ST.sem, ST.v)


def build_program(dbg=False):
    nc = bass.Bass("TRN2", target_bir_lowering=False)
    with ExitStack() as ctx:
        k = K(nc, ctx, dbg)
        declare_io(k)
        k.s_ld = k.sem("s_ld")
        k.s_st = k.sem("s_st")
        k.s_pe = k.sem("s_pe")
        k.s_ve = k.sem("s_ve")
        k.s_sc = k.sem("s_sc")
        k.s_gp = k.sem("s_gp")
        k.s_ld2 = k.sem("s_ld2")
        k.s_lda = k.sem("s_lda")
        k.s_ldb = k.sem("s_ldb")
        k.s_la = k.sem("s_la")
        k.s_lm = k.sem("s_lm")
        k.s_sta = k.sem("s_sta")
        k.s_stb = k.sem("s_stb")
        k.s_pe2 = k.sem("s_pe2")
        k.s_sc2 = k.sem("s_sc2")
        k.s_ve2 = k.sem("s_ve2")
        k.s_st2 = k.sem("s_st2")
        k.sbuf("bst_sb", [1, 32])
        k.sbuf("gsccol_sb", [128, 2 * NT])
        k.sbuf("lngsc_sb", [128, 2 * NT])
        k.sbuf("flush_sb", [1, 8])

        emit_init(k)
        k.barrier()
        for b in range(NB):
            with nc.sbuf_tensor(f"z_sb_{b}", [1, S], F32) as zt:
                k.sb["z_sb"] = zt
                emit_p0(k, b)
                k.barrier()
                emit_bisect(k, b)
                k.barrier()
        for b in range(NB):
            emit_proj(k, b, k.scr["xnT"][b], [
                dict(w="w0aq", bias="b0aq", nout=D, dst=k.scr["qT"][b], dst_tm=None),
                dict(w="w0ak", bias="b0ak", nout=D, dst=k.scr["kT"][b], dst_tm=None),
                dict(w="w0av", bias="b0av", nout=D, dst=None,
                     dst_tm=k.scr["vTm"][b], col0=0),
            ])
            k.barrier()
            emit_attn(k, b, "l0a")
            k.barrier()
            emit_proj(k, b, k.scr["xnT"][b], [
                dict(w="w0fq", bias=None, nout=D, dst=k.scr["qT"][b], dst_tm=None),
                dict(w="w0fk", bias=None, nout=D, dst=k.scr["kT"][b], dst_tm=None),
                dict(w="w0fv", bias=None, nout=D, dst=None,
                     dst_tm=k.scr["vTm"][b], col0=0),
            ])
            k.barrier()
            emit_attn(k, b, "l0f")
            k.barrier()
            emit_proj(k, b, k.scr["oaT"][b], [
                dict(w="w0ao", bias="b0ao", nout=D, dst=k.scr["qT"][b], dst_tm=None)])
            k.barrier()
            emit_proj(k, b, k.scr["ofT"][b], [
                dict(w="w0fo", bias=None, nout=D, dst=k.scr["kT"][b], dst_tm=None)])
            k.barrier()
            emit_combine_l0(k, b)
            k.barrier()
            emit_proj(k, b, k.scr["xn1T"][b], [
                dict(w="w1aq", bias="b1aq", nout=D, dst=k.scr["qT"][b], dst_tm=None),
                dict(w="w1ak", bias="b1ak", nout=256, dst=k.scr["kT"][b], dst_tm=None),
                dict(w="w1av", bias="b1av", nout=256, dst=None,
                     dst_tm=k.scr["vTm"][b], col0=0),
            ])
            k.barrier()
            emit_attn(k, b, "l1")
            k.barrier()
            emit_proj(k, b, k.scr["oaT"][b], [
                dict(w="w1ao", bias="b1ao", nout=D, dst=k.scr["a1T"][b], dst_tm=None)])
            k.barrier()
        emit_moe(k)
        k.barrier()
        emit_final(k)
    return nc


def finalize(k: K):
    nc = k.nc
    with ExitStack() as ctx:
        zt = ctx.enter_context(nc.sbuf_tensor("fin_z", [128, D], F32))
        vc = k.s_ve.bump(k.ve.memset(zt[:, :], 0.0))
        k.gp.wait_ge(k.s_ve.sem, vc)
        ST = k.s_st
        for b in range(NB):
            for tt in range(NT):
                ST.bump(k.gp.dma_start(out=k.out[b, tt * 128:(tt + 1) * 128, :],
                                       in_=zt[:, :]), 16)
        k.gp.wait_ge(ST.sem, ST.v)


def prep_inputs(full):
    T = lambda a: np.ascontiguousarray(np.asarray(a).T)
    C = lambda a: np.ascontiguousarray(np.asarray(a), dtype=np.float32)
    bias_cols = lambda a, nc_: np.ascontiguousarray(np.asarray(a).reshape(nc_, 128).T)
    base = {
        "w0aq": T(full["L0_attn_Wq"]), "w0ak": T(full["L0_attn_Wk"]),
        "w0av": T(full["L0_attn_Wv"]), "w0ao": T(full["L0_attn_Wo"]),
        "b0aq": bias_cols(full["L0_attn_bq"], NC), "b0ak": bias_cols(full["L0_attn_bk"], NC),
        "b0av": bias_cols(full["L0_attn_bv"], NC), "b0ao": bias_cols(full["L0_attn_bo"], NC),
        "w0fg": np.ascontiguousarray(np.repeat(bias_cols(full["L0_ffn_Wg"][:, 0], NC), 2, axis=1)),
        "w0fq": C(full["L0_ffn_Wq"]), "w0fk": C(full["L0_ffn_Wk"]),
        "w0fv": C(full["L0_ffn_Wv"]), "w0fo": C(full["L0_ffn_Wo"]),
        "w1aq": C(full["L1_attn_Wq"]), "w1ak": C(full["L1_attn_Wk"]),
        "w1av": C(full["L1_attn_Wv"]), "w1ao": C(full["L1_attn_Wo"]),
        "b1aq": bias_cols(full["L1_attn_bq"], NC), "b1ao": bias_cols(full["L1_attn_bo"], NC),
        "b1ak": bias_cols(full["L1_attn_bk"], 2), "b1av": bias_cols(full["L1_attn_bv"], 2),
        "slopes": C(full["L1_attn_slopes"]).reshape(1, H),
        "w1g": C(full["L1_ffn_Wg"]), "b1g": C(full["L1_ffn_bg"]).reshape(8, 1),
        "w1e1": C(full["L1_ffn_W1"]),
        "b1e1": np.ascontiguousarray(np.asarray(full["L1_ffn_b1"]).reshape(8, 16, 128).transpose(0, 2, 1)),
        "w1e2": C(full["L1_ffn_W2"]),
        "b1e2": np.ascontiguousarray(np.asarray(full["L1_ffn_b2"]).reshape(8, 8, 128).transpose(0, 2, 1)),
        "gamma": bias_cols(full["L1_ffn_gamma"], NC),
        "beta": bias_cols(full["L1_ffn_beta"], NC),
    }
    x = np.asarray(full["L0_attn_x"], dtype=np.float32)
    maps = []
    for c in range(8):
        m = dict(base)
        m["x"] = np.ascontiguousarray(x[2 * c:2 * c + 2])
        maps.append(m)
    return maps


def kernel(**inputs):
    maps = prep_inputs(inputs)
    nc = build_program(dbg=False)
    res = run_bass_kernel_spmd(nc, maps, core_ids=list(range(8)))
    return np.concatenate([r["out"] for r in res.results], axis=0)


# version 15
# speedup vs baseline: 1.0512x; 1.0209x over previous
import numpy as np
from contextlib import ExitStack

import concourse.bass as bass
import concourse.mybir as mybir
from concourse.bass_utils import run_bass_kernel_spmd

F32 = mybir.dt.float32
AF = mybir.ActivationFunctionType
ALU = mybir.AluOpType
AX = mybir.AxisListType

NB = 2
S = 2048
D = 1024
NC = D // 128
NT = S // 128
H = 16
DH = 64
BIG = 1.0e30
NEGAUG = -8.0e30
EPS = 1e-5
N_BISECT = 45


class Ctr:

    def __init__(self, sem):
        self.sem = sem
        self.v = 0

    def bump(self, inst, n=1):
        inst.then_inc(self.sem, n)
        self.v += n
        return self.v


class K:

    def __init__(self, nc, ctx, dbg):
        self.nc = nc
        self.ctx = ctx
        self.dbg = dbg
        self.sy = nc.sync
        self.ve = nc.vector
        self.sc = nc.scalar
        self.te = nc.tensor
        self.gp = nc.gpsimd
        self.ins = {}
        self.scr = {}
        self.sb = {}

    _uid = 0

    def sbuf(self, name, shape, dtype=F32):
        t = self.ctx.enter_context(self.nc.sbuf_tensor(name, list(shape), dtype))
        self.sb[name] = t
        return t

    def tmp(self, ctx, name, shape, dtype=F32):
        K._uid += 1
        return ctx.enter_context(self.nc.sbuf_tensor(f"{name}_{K._uid}", list(shape), dtype))

    def ptmp(self, ctx, name, shape, dtype=F32):
        K._uid += 1
        return ctx.enter_context(self.nc.psum_tensor(f"{name}_{K._uid}", list(shape), dtype))

    def psum(self, name, shape, dtype=F32):
        return self.ctx.enter_context(self.nc.psum_tensor(name, list(shape), dtype))

    _fl = 0

    def vflush(self):
        K._fl = (K._fl + 1) % 4
        self.ve.wait_ge(self.s_ve.sem, self.flush_c0)
        c = self.s_ve.bump(self.ve.tensor_copy(
            self.sb["flush_sb"][0:1, 4 + K._fl:5 + K._fl],
            self.sb["flush_sb"][0:1, 0:1]))
        return c

    def vs(self, inst):
        c = self.s_ve.bump(inst)
        self.ve.wait_ge(self.s_ve.sem, c)
        return c

    def ss(self, inst):
        c = self.s_sc.bump(inst)
        self.sc.wait_ge(self.s_sc.sem, c)
        return c

    def sem(self, name):
        return Ctr(self.ctx.enter_context(self.nc.semaphore(name)))

    def barrier(self):
        self.nc.all_engine_barrier()


def declare_io(k: K):
    nc = k.nc
    def inp(name, shape):
        k.ins[name] = nc.dram_tensor(name, list(shape), F32, kind="ExternalInput").ap()
    inp("x", [NB, S, D])
    for w in ("w0aq", "w0ak", "w0av", "w0ao"):
        inp(w, [D, D])
    for b in ("b0aq", "b0ak", "b0av", "b0ao"):
        inp(b, [128, NC])
    inp("w0fg", [128, 2 * NC])
    for w in ("w0fq", "w0fk", "w0fv", "w0fo"):
        inp(w, [D, D])
    inp("w1aq", [D, D]); inp("w1ao", [D, D])
    inp("w1ak", [D, 256]); inp("w1av", [D, 256])
    inp("b1aq", [128, NC]); inp("b1ao", [128, NC])
    inp("b1ak", [128, 2]); inp("b1av", [128, 2])
    inp("slopes", [1, H])
    inp("w1g", [D, 8]); inp("b1g", [8, 1])
    inp("w1e1", [8, D, 2 * D]); inp("b1e1", [8, 128, 16])
    inp("w1e2", [8, 2 * D, D]); inp("b1e2", [8, 128, 8])
    inp("gamma", [128, NC]); inp("beta", [128, NC])

    k.out = nc.dram_tensor("out", [NB, S, D], F32, kind="ExternalOutput").ap()

    kind = "ExternalOutput" if k.dbg else "Internal"
    def scr(name, shape):
        k.scr[name] = nc.dram_tensor(name, list(shape), F32, kind=kind).ap()
    scr("xT", [NB, D, S])
    scr("xnT", [NB, D, S])
    scr("qT", [NB, D, S])
    scr("kT", [NB, D, S])
    scr("vTm", [NB, S, D])
    scr("oaT", [NB, D, S])
    scr("ofT", [NB, D, S])
    scr("x1T", [NB, D, S])
    scr("xn1T", [NB, D, S])
    scr("a1T", [NB, D, S])
    scr("mixT", [NB, D, S])
    scr("augr", [NB, S])
    scr("negm", [16, 128])
    if k.dbg:
        scr("attn_dbg", [NB, 4, S])
    if k.dbg:
        scr("zdbg", [NB, 8, S])


def emit_init(k: K):
    ident = k.sbuf("ident", [128, 128])
    ones_row = k.sbuf("ones_row", [1, 128])
    onescol = k.sbuf("onescol", [128, 1])
    r512 = k.sbuf("r512", [128, 512])
    slop = k.sbuf("slop", [1, H])
    bias_sb = {}
    for nm in ("b0aq", "b0ak", "b0av", "b0ao", "b1aq", "b1ao", "gamma", "beta"):
        bias_sb[nm] = k.sbuf(nm + "_sb", [128, NC])
    for nm in ("b1ak", "b1av"):
        bias_sb[nm] = k.sbuf(nm + "_sb", [128, 2])
    bias_sb["b1g"] = k.sbuf("b1g_sb", [8, 1])
    bias_sb["w0fg"] = k.sbuf("w0fg_sb", [128, 2 * NC])
    k.bias = bias_sb

    LD, GPc = k.s_ld, k.s_gp
    for nm in ("b0aq", "b0ak", "b0av", "b0ao", "b1aq", "b1ao", "gamma", "beta",
               "b1ak", "b1av", "b1g", "w0fg"):
        c = LD.bump(k.sy.dma_start(out=bias_sb[nm][:, :], in_=k.ins[nm]), 16)
    c = LD.bump(k.sy.dma_start(out=slop[:, :], in_=k.ins["slopes"]), 16)
    ld_done = LD.v

    g = k.gp
    g.iota(ident[:, :], pattern=[[1, 128]], base=0, channel_multiplier=-1,
           allow_small_or_imprecise_dtypes=True)
    gc = GPc.bump(g.iota(r512[:, :], pattern=[[1, 512]], base=0, channel_multiplier=-1,
                         allow_small_or_imprecise_dtypes=True))
    v = k.ve
    k.flush_c0 = 0
    k.flush_c0 = k.vs(v.memset(k.sb["flush_sb"][0:1, :], 0.0))
    v.wait_ge(GPc.sem, gc)
    v.tensor_single_scalar(ident[:, :], ident[:, :], 0.0, op=ALU.is_equal)
    v.memset(ones_row[:, :], 1.0)
    v.memset(onescol[:, :], 1.0)
    v.wait_ge(LD.sem, LD.v)
    k.ident = ident
    k.ones_row = ones_row
    k.onescol = onescol
    k.r512 = r512
    k.slop = slop


def emit_p0(k: K, b: int):
    nc = k.nc
    LD, PE, VE, SC, ST = k.s_ld, k.s_pe, k.s_ve, k.s_sc, k.s_st
    with ExitStack() as ctx:
        xt = [k.tmp(ctx, "p0x", [128, D]) for i in range(2)]
        stg_xn = k.tmp(ctx, "p0sxn", [128, D])
        stg_x = k.tmp(ctx, "p0sx", [128, D])
        xc = k.tmp(ctx, "p0xc", [128, D])
        sqd = k.tmp(ctx, "p0sqd", [128, D])
        st1 = k.tmp(ctx, "p0st", [128, 12])
        ps_t = k.ptmp(ctx, "p0pst", [128, 1024])
        ps_t2 = k.ptmp(ctx, "p0pst2", [128, 1024])
        ps_z = k.ptmp(ctx, "p0psz", [128, 128])

        ld0 = [0, 0]
        pe_done = [0] * NT
        sc_evac = [0] * NT
        v, s, t, g = k.ve, k.sc, k.te, k.gp
        for tt in range(NT):
            buf = xt[tt % 2]
            if tt >= 2:
                k.sy.wait_ge(PE.sem, pe_done[tt - 2])
            LDP = k.s_lda if tt % 2 == 0 else k.s_ldb
            ld0[tt % 2] = LDP.bump(k.sy.dma_start(
                out=buf[:, :], in_=k.ins["x"][b, tt * 128:(tt + 1) * 128, :]), 16)

            v.wait_ge(LDP.sem, ld0[tt % 2])
            k.vs(v.reduce_sum(st1[:, 0:1], buf[:, :], axis=AX.X))
            vc = k.vs(v.tensor_scalar(st1[:, 1:2], st1[:, 0:1], -1.0 / D, None,
                                      op0=ALU.mult))
            s.wait_ge(VE.sem, vc)
            sc0 = SC.bump(s.activation(sqd[:, :], buf[:, :], AF.Square, bias=0.0,
                                       scale=1.0, accum_out=st1[:, 2:3]))
            v.wait_ge(SC.sem, sc0)
            v.tensor_scalar(st1[:, 3:4], st1[:, 2:3], 1.0 / D, None, op0=ALU.mult)
            k.vs(v.tensor_mul(st1[:, 4:5], st1[:, 1:2], st1[:, 1:2]))
            k.vs(v.tensor_sub(st1[:, 3:4], st1[:, 3:4], st1[:, 4:5]))
            vc = k.vs(v.tensor_scalar(st1[:, 3:4], st1[:, 3:4], EPS, None,
                                      op0=ALU.add))
            s.wait_ge(VE.sem, vc)
            k.ss(s.activation(st1[:, 7:8], st1[:, 3:4], AF.Ln, bias=0.0, scale=1.0))
            sc1 = SC.bump(s.activation(st1[:, 4:5], st1[:, 7:8], AF.Exp, bias=0.0,
                                       scale=-0.5))
            v.wait_ge(SC.sem, sc1)
            k.vs(v.tensor_mul(st1[:, 5:6], st1[:, 4:5], st1[:, 4:5]))
            k.vs(v.tensor_mul(st1[:, 5:6], st1[:, 5:6], st1[:, 3:4]))
            k.vs(v.tensor_scalar(st1[:, 5:6], st1[:, 5:6], -0.5, 1.5, op0=ALU.mult,
                                 op1=ALU.add))
            k.vs(v.tensor_mul(st1[:, 6:7], st1[:, 4:5], st1[:, 5:6]))
            if tt >= 1:
                v.wait_ge(PE.sem, pe_done[tt - 1])
            k.vs(v.tensor_scalar(xc[:, :], buf[:, :], st1[:, 1:2], None, op0=ALU.add))
            vxn = VE.bump(v.tensor_scalar(xc[:, :], xc[:, :], st1[:, 6:7], None,
                                          op0=ALU.mult))

            t.wait_ge(VE.sem, vxn)
            if tt >= 1:
                t.wait_ge(SC.sem, sc_evac[tt - 1])
            for c in range(NC):
                t.transpose(ps_t[:, c * 128:c * 128 + 128],
                            xc[:, c * 128:(c + 1) * 128], k.ident[:, :])
            last = None
            for c in range(NC):
                last = t.transpose(ps_t2[:, c * 128:c * 128 + 128],
                                   buf[:, c * 128:(c + 1) * 128], k.ident[:, :])
            pe_done[tt] = PE.bump(last)

            s.wait_ge(PE.sem, pe_done[tt])
            for c in range(NC):
                s.copy(stg_xn[:, c * 128:(c + 1) * 128], ps_t[:, c * 128:c * 128 + 128])
            last = None
            for c in range(NC):
                last = s.copy(stg_x[:, c * 128:(c + 1) * 128], ps_t2[:, c * 128:c * 128 + 128])
            sc_evac[tt] = SC.bump(last)

            t.wait_ge(SC.sem, sc_evac[tt])
            for c in range(NC):
                last = t.matmul(ps_z[0:2, 0:128], k.bias["w0fg"][:, 2 * c:2 * c + 2],
                                stg_xn[:, c * 128:(c + 1) * 128],
                                start=(c == 0), stop=(c == NC - 1))
            pez = PE.bump(last)
            s.wait_ge(PE.sem, pez)
            s.copy(k.sb["z_sb"][0:1, tt * 128:(tt + 1) * 128], ps_z[0:1, 0:128])

            g.wait_ge(SC.sem, sc_evac[tt])
            ST.bump(g.dma_start(
                out=k.scr["xnT"][b].rearrange("(c p) t -> p c t", p=128)[:, :, tt * 128:(tt + 1) * 128],
                in_=stg_xn[:, :].rearrange("p (c t) -> p c t", c=NC)), 16)
            ST.bump(g.dma_start(
                out=k.scr["xT"][b].rearrange("(c p) t -> p c t", p=128)[:, :, tt * 128:(tt + 1) * 128],
                in_=stg_x[:, :].rearrange("p (c t) -> p c t", c=NC)), 16)
            if tt < NT - 1:
                k.sc.wait_ge(ST.sem, ST.v)
        k.gp.wait_ge(ST.sem, ST.v)


def emit_bisect(k: K, b: int):
    nc = k.nc
    VE, SC, PE, ST, LD = k.s_ve, k.s_sc, k.s_pe, k.s_st, k.s_ld
    v, s, t, g = k.ve, k.sc, k.te, k.gp
    z = k.sb["z_sb"]
    st0 = k.sb["bst_sb"]
    zr = z[0:1, :]
    st = lambda i: st0[0:1, 16 * b + i:16 * b + i + 1]
    with ExitStack() as ctx:
        cmp_row = k.tmp(ctx, "bcmp", [1, S])
        aug = k.tmp(ctx, "baug", [1, S])
        aug_a = k.tmp(ctx, "bauga", [1, S])
        gsc = k.tmp(ctx, "bgsc", [1, S])
        psg = k.ptmp(ctx, "bpsg", [128, 128])

        k.vs(v.reduce_max(st(0), zr, axis=AX.X))
        k.vs(v.tensor_reduce(st(1), zr, axis=AX.X, op=ALU.min))
        v.tensor_scalar(st(2), st(1), -1.0, None, op0=ALU.add)
        k.vs(v.tensor_scalar(st(3), st(0), 1.0, None, op0=ALU.add))
        for it in range(N_BISECT):
            k.vs(v.tensor_add(st(4), st(2), st(3)))
            k.vs(v.tensor_scalar(st(4), st(4), 0.5, None, op0=ALU.mult))
            k.vs(v.tensor_scalar(cmp_row[0:1, :], zr, st(4), 0.0,
                                 op0=ALU.is_ge, op1=ALU.add, accum_out=st(5)))
            k.vs(v.tensor_scalar(st(6), st(5), 1024.0, None, op0=ALU.is_ge))
            k.vs(v.tensor_sub(st(7), st(4), st(2)))
            k.vs(v.tensor_mul(st(7), st(7), st(6)))
            k.vs(v.tensor_scalar(st(8), st(6), -1.0, 1.0, op0=ALU.mult, op1=ALU.add))
            k.vs(v.tensor_sub(st(9), st(4), st(3)))
            k.vs(v.tensor_mul(st(9), st(9), st(8)))
            k.vs(v.tensor_add(st(2), st(2), st(7)))
            k.vs(v.tensor_add(st(3), st(3), st(9)))
        vau = k.vs(v.tensor_scalar(aug[0:1, :], zr, st(2), NEGAUG,
                                   op0=ALU.is_lt, op1=ALU.mult))
        s.wait_ge(VE.sem, vau)
        saug = SC.bump(s.copy(aug_a[0:1, :], aug[0:1, :]))
        g.wait_ge(SC.sem, saug)
        sta = ST.bump(g.dma_start(out=k.scr["augr"][b, :], in_=aug_a[0:1, :]), 16)
        s.wait_ge(VE.sem, vau)
        sc0 = SC.bump(s.activation(cmp_row[0:1, :], zr, AF.Exp, bias=0.0, scale=-1.0))
        v.wait_ge(SC.sem, sc0)
        k.vs(v.tensor_scalar(cmp_row[0:1, :], cmp_row[0:1, :], 1.0, None, op0=ALU.add))
        k.vs(v.reciprocal(gsc[0:1, :], cmp_row[0:1, :]))
        k.vs(v.tensor_scalar(cmp_row[0:1, :], zr, st(2), None, op0=ALU.is_ge))
        if k.dbg:
            g.wait_ge(VE.sem, VE.v)
            ST.bump(g.dma_start(out=k.scr["zdbg"][b, 4, :], in_=gsc[0:1, :]), 16)
            ST.bump(g.dma_start(out=k.scr["zdbg"][b, 5, :], in_=cmp_row[0:1, :]), 16)
            g.wait_ge(ST.sem, ST.v)
        vg = k.vs(v.tensor_mul(gsc[0:1, :], gsc[0:1, :], cmp_row[0:1, :]))
        t.wait_ge(VE.sem, vg)
        for tt in range(NT):
            pc = PE.bump(t.transpose(psg[:, tt % 2:tt % 2 + 1],
                        gsc[0:1, tt * 128:(tt + 1) * 128], k.ident[0:1, 0:1]))
            s.wait_ge(PE.sem, pc)
            scv = SC.bump(s.copy(k.sb["gsccol_sb"][:, b * NT + tt:b * NT + tt + 1],
                                 psg[:, tt % 2:tt % 2 + 1]))
            if tt < NT - 1:
                t.wait_ge(SC.sem, scv)
        v.wait_ge(SC.sem, SC.v)
        vgm = k.vs(v.tensor_scalar(k.sb["lngsc_sb"][:, b * NT:(b + 1) * NT],
                                   k.sb["gsccol_sb"][:, b * NT:(b + 1) * NT],
                                   1e-30, None, op0=ALU.max))
        s.wait_ge(VE.sem, vgm)
        sgl = SC.bump(s.activation(k.sb["lngsc_sb"][:, b * NT:(b + 1) * NT],
                                   k.sb["lngsc_sb"][:, b * NT:(b + 1) * NT],
                                   AF.Ln, bias=0.0, scale=1.0))
        v.wait_ge(SC.sem, sgl)
        k.vs(v.tensor_scalar(k.sb["lngsc_sb"][:, b * NT:(b + 1) * NT],
                             k.sb["lngsc_sb"][:, b * NT:(b + 1) * NT],
                             8.0, None, op0=ALU.mult))
        if k.dbg:
            g.wait_ge(VE.sem, VE.v)
            g.wait_ge(SC.sem, SC.v)
            ST.bump(g.dma_start(out=k.scr["zdbg"][b, 0, :], in_=zr), 16)
            ST.bump(g.dma_start(out=k.scr["zdbg"][b, 1, :], in_=aug[0:1, :]), 16)
            ST.bump(g.dma_start(out=k.scr["zdbg"][b, 2, :], in_=gsc[0:1, :]), 16)
            ST.bump(g.dma_start(out=k.scr["zdbg"][b, 3, 0:16],
                                in_=st0[0:1, 16 * b:16 * b + 16]), 16)
        g.wait_ge(ST.sem, ST.v)


def emit_proj(k: K, b: int, xin, specs):
    nc = k.nc
    LD, PE, VE, SC, ST = k.s_ld, k.s_pe, k.s_ve, k.s_sc, k.s_st
    v, s, t, g, sy = k.ve, k.sc, k.te, k.gp, k.sy
    with ExitStack() as ctx:
        xn_sb = k.tmp(ctx, "prx", [128, NC * S])
        w_sb = k.tmp(ctx, "prw", [128, NC * D])
        stg = [k.tmp(ctx, "prs", [128, 512]) for _ in range(2)]
        vstg = k.tmp(ctx, "prv", [128, 512])
        vt_stage = k.tmp(ctx, "prvt", [128, 4 * D])
        ps = k.ptmp(ctx, "prps", [128, 1024])
        ps_t = k.ptmp(ctx, "prpst", [128, 512])

        ldx = LD.bump(sy.dma_start(
            out=xn_sb[:, :].rearrange("p (c t) -> p c t", c=NC),
            in_=xin.rearrange("(c p) t -> p c t", p=128)), 16)

        evd = {}
        std = {}
        git = 0
        for si, spec in enumerate(specs):
            noc = spec["nout"] // 128
            if si > 0:
                sy.wait_ge(PE.sem, PE.v)
            ldw = LD.bump(sy.dma_start(
                out=w_sb[:, 0:NC * spec["nout"]].rearrange("p (c o) -> p c o", c=NC),
                in_=k.ins[spec["w"]].rearrange("(c p) o -> p c o", p=128)), 16)
            bias = k.bias[spec["bias"]] if spec["bias"] else None
            t.wait_ge(LD.sem, max(ldx, ldw))

            if spec.get("dst_tm") is None:
                for oc in range(noc):
                    for ts4 in range(4):
                        bank = git % 2
                        if git >= 2:
                            eng_, cval = evd[git - 2]
                            eng_.wait_ge(cval[0], cval[1])
                        last = None
                        for c in range(NC):
                            last = t.matmul(
                                ps[:, bank * 512:bank * 512 + 512],
                                w_sb[:, c * spec["nout"] + oc * 128:
                                     c * spec["nout"] + oc * 128 + 128],
                                xn_sb[:, c * S + ts4 * 512:c * S + ts4 * 512 + 512],
                                start=(c == 0), stop=(c == NC - 1))
                        pc = PE.bump(last)
                        if git >= 2 and (git - 2) in std:
                            STP2 = k.s_sta if git % 2 == 0 else k.s_stb
                            (v if bias else s).wait_ge(STP2.sem, std[git - 2])
                        if bias:
                            v.wait_ge(PE.sem, pc)
                            VE.bump(v.tensor_scalar(
                                stg[bank][:, :], ps[:, bank * 512:bank * 512 + 512],
                                bias[:, oc:oc + 1], None, op0=ALU.add))
                            fc = k.vflush()
                            evd[git] = (t, (VE.sem, fc))
                            g.wait_ge(VE.sem, fc)
                        else:
                            s.wait_ge(PE.sem, pc)
                            sc_ = SC.bump(s.copy(
                                stg[bank][:, :], ps[:, bank * 512:bank * 512 + 512]))
                            evd[git] = (t, (SC.sem, sc_))
                            g.wait_ge(SC.sem, sc_)
                        STP = k.s_sta if git % 2 == 0 else k.s_stb
                        std[git] = STP.bump(g.dma_start(
                            out=spec["dst"][oc * 128:(oc + 1) * 128,
                                            ts4 * 512:(ts4 + 1) * 512],
                            in_=stg[bank][:, :]), 16)
                        git += 1
                g.wait_ge(k.s_sta.sem, k.s_sta.v)
                g.wait_ge(k.s_stb.sem, k.s_stb.v)
            else:
                for ts4 in range(4):
                    for oc in range(noc):
                        it = ts4 * noc + oc
                        bank = git % 2
                        if git >= 2:
                            eng_, cval = evd[git - 2]
                            eng_.wait_ge(cval[0], cval[1])
                        last = None
                        for c in range(NC):
                            last = t.matmul(
                                ps[:, bank * 512:bank * 512 + 512],
                                w_sb[:, c * spec["nout"] + oc * 128:
                                     c * spec["nout"] + oc * 128 + 128],
                                xn_sb[:, c * S + ts4 * 512:c * S + ts4 * 512 + 512],
                                start=(c == 0), stop=(c == NC - 1))
                        pc = PE.bump(last)
                        if it == 0:
                            trd, cpd = {}, {}
                        if bias:
                            v.wait_ge(PE.sem, pc)
                            ev_ = VE.bump(v.tensor_scalar(
                                vstg[:, :], ps[:, bank * 512:bank * 512 + 512],
                                bias[:, oc:oc + 1], None, op0=ALU.add))
                            evd[git] = (t, (VE.sem, ev_))
                            t.wait_ge(VE.sem, ev_)
                        else:
                            s.wait_ge(PE.sem, pc)
                            ev_ = SC.bump(s.copy(
                                vstg[:, :], ps[:, bank * 512:bank * 512 + 512]))
                            evd[git] = (t, (SC.sem, ev_))
                            t.wait_ge(SC.sem, ev_)
                        if it >= 1:
                            t.wait_ge(SC.sem, cpd[it - 1])
                        last = None
                        for tc in range(4):
                            last = t.transpose(ps_t[:, tc * 128:tc * 128 + 128],
                                               vstg[:, tc * 128:tc * 128 + 128],
                                               k.ident[:, :])
                        trd[it] = PE.bump(last)
                        s.wait_ge(PE.sem, trd[it])
                        last = None
                        for tc in range(4):
                            last = s.copy(
                                vt_stage[:, tc * spec["nout"] + oc * 128:
                                         tc * spec["nout"] + oc * 128 + 128],
                                ps_t[:, tc * 128:tc * 128 + 128])
                        cpd[it] = SC.bump(last)
                        git += 1
                    g.wait_ge(SC.sem, cpd[ts4 * noc + noc - 1])
                    stv = ST.bump(g.dma_start(
                        out=spec["dst_tm"].rearrange("(tb tc p) f -> tb p tc f",
                                                     p=128, tc=4)[ts4][:, :,
                                                     spec["col0"]:spec["col0"] + spec["nout"]],
                        in_=vt_stage[:, 0:4 * spec["nout"]].rearrange(
                            "p (tc f) -> p tc f", tc=4)), 16)
                    s.wait_ge(ST.sem, stv)
        g.wait_ge(ST.sem, ST.v)
        g.wait_ge(k.s_sta.sem, k.s_sta.v)
        g.wait_ge(k.s_stb.sem, k.s_stb.v)


def emit_attn(k: K, b: int, kind: str):
    nc = k.nc
    LD, PE, VE, SC, ST = k.s_ld, k.s_pe, k.s_ve, k.s_sc, k.s_st
    v, s, t, g, sy = k.ve, k.sc, k.te, k.gp, k.sy
    scale = {"l0a": 1.0 / 32.0, "l0f": 1.0 / 8.0, "l1": 1.0 / 8.0}[kind]
    inv_scale = 1.0 / scale
    CK = 128
    causal = (kind == "l1")
    outscr = k.scr["ofT"] if kind == "l0f" else k.scr["oaT"]

    with ExitStack() as ctx:
        qh = [k.tmp(ctx, "aq", [128, S]) for _ in range(2)]
        kh = [k.tmp(ctx, "ak", [128, S]) for _ in range(2)]
        vt = [k.tmp(ctx, "av", [128, NT * DH]) for _ in range(2)]
        uT = [k.tmp(ctx, "au", [128, NT * 512]) for _ in range(2)]
        pdum = k.tmp(ctx, "apd", [128, S])
        ofm = k.tmp(ctx, "aof", [128, S])
        mall = k.tmp(ctx, "am", [128, NT])
        sall = k.tmp(ctx, "as", [128, NT])
        lns = k.tmp(ctx, "al", [128, NT])
        smt = k.tmp(ctx, "asm", [128, 8])
        nm16 = k.tmp(ctx, "anm", [16, 128])
        srow = k.tmp(ctx, "asr", [1, S])
        srow_a = k.tmp(ctx, "asra", [1, S])
        ones2k = k.tmp(ctx, "ao2", [1, S])
        if causal:
            iota_j = k.tmp(ctx, "aij", [1, S])
            mtri = k.tmp(ctx, "amt", [128, 4 * 512])
            mc1 = k.tmp(ctx, "amc", [128, 4 * 512])
            dstage = k.tmp(ctx, "ads", [128, 4 * 512])
        mgall = k.tmp(ctx, "amg", [128, 64])
        sgall = k.tmp(ctx, "asg", [128, 64])
        tm64 = k.tmp(ctx, "atm", [128, 64])
        s_ps = k.ptmp(ctx, "aps", [128, 2048])
        oav_ps = k.ptmp(ctx, "apo", [64, 1024])

        v.memset(ones2k[0:1, :], 1.0)
        vz = None
        for bb in range(2):
            v.memset(qh[bb][64:128, :], 0.0)
            vz = v.memset(kh[bb][64:128, :], 0.0)
        vz = VE.bump(vz)
        if causal:
            gi = k.s_gp.bump(g.iota(iota_j[0:1, :], pattern=[[1, S]], base=0,
                                    channel_multiplier=0,
                                    allow_small_or_imprecise_dtypes=True))
            v.wait_ge(k.s_gp.sem, gi)
            for r_ in range(4):
                v.tensor_scalar(mtri[:, r_ * 512:(r_ + 1) * 512], k.r512[:, :],
                                float(128 * r_), -BIG, op0=ALU.is_lt, op1=ALU.mult)
            for r_ in range(4):
                v.tensor_scalar(mc1[:, r_ * 512:(r_ + 1) * 512], k.r512[:, :],
                                float(128 * r_), -BIG, op0=ALU.is_gt, op1=ALU.mult)
            k.vs(v.memset(smt[:, 7:8], 0.0))

        def issue_loads(h):
            hb = h % 2
            LDP = k.s_lda if hb == 0 else k.s_ldb
            kvh = h // 4 if kind == "l1" else h
            LDP.bump(sy.dma_start(out=qh[hb][0:64, :],
                                  in_=k.scr["qT"][b][h * 64:(h + 1) * 64, :]), 16)
            LDP.bump(sy.dma_start(out=kh[hb][0:64, :],
                                  in_=k.scr["kT"][b][kvh * 64:(kvh + 1) * 64, :]), 16)
            l3 = LDP.bump(sy.dma_start(
                out=vt[hb][:, :].rearrange("p (kc d) -> p kc d", kc=NT),
                in_=k.scr["vTm"][b].rearrange("(kc p) f -> p kc f", p=128)[
                    :, :, kvh * 64:(kvh + 1) * 64]), 16)
            if kind == "l0f":
                l3 = LDP.bump(sy.dma_start(out=kh[hb][65:66, :],
                                           in_=k.scr["augr"][b, :]), 16)
            return l3

        def aug_rows(h, ldt):
            hb = h % 2
            LDP = k.s_lda if hb == 0 else k.s_ldb
            LA = k.s_la
            v.wait_ge(LDP.sem, ldt)
            v.wait_ge(VE.sem, vz)
            v.tensor_copy(kh[hb][64:65, :], ones2k[0:1, :])
            av = VE.bump(v.memset(qh[hb][64:65, :], 0.0))
            al = LA.v
            if kind == "l0f":
                sy.wait_ge(VE.sem, av)
                al = LA.bump(sy.dma_start(out=qh[hb][65:66, :],
                                          in_=ones2k[0:1, :]), 16)
            elif kind == "l1":
                vsr = k.vs(v.tensor_scalar(srow[0:1, :], ones2k[0:1, :],
                                           k.slop[0:1, h:h + 1], inv_scale,
                                           op0=ALU.mult, op1=ALU.mult))
                s.wait_ge(VE.sem, vsr)
                if h > 0:
                    s.wait_ge(LA.sem, LA.v)
                sca = SC.bump(s.copy(srow_a[0:1, :], srow[0:1, :]))
                sy.wait_ge(SC.sem, sca)
                LA.bump(sy.dma_start(out=kh[hb][65:66, :],
                                     in_=iota_j[0:1, :]), 16)
                al = LA.bump(sy.dma_start(out=qh[hb][65:66, :],
                                          in_=srow_a[0:1, :]), 16)
            return av, al

        ld_t = issue_loads(0)
        aug_t, aug_l = aug_rows(0, ld_t)
        free_at = {}
        for h in range(H):
            hb = h % 2
            if h + 1 < H:
                if h >= 1:
                    sy.wait_ge(PE.sem, free_at[h - 1])
                ld_next = issue_loads(h + 1)

            t.wait_ge((k.s_lda if hb == 0 else k.s_ldb).sem, ld_t)
            t.wait_ge(k.s_la.sem, aug_l)
            t.wait_ge(VE.sem, aug_t)

            t.wait_ge(SC.sem, SC.v)
            expg = {}
            gcnt = 0
            if causal:
                v.memset(mgall[:, :], -BIG)
                k.vs(v.memset(sgall[:, :], 0.0))
            for qt in range(NT):
                nk = (qt // 4 + 1) if causal else 4
                for kc in range(nk):
                    bank = gcnt % 4
                    slot = gcnt % 4
                    col = qt * 4 + kc
                    if gcnt >= 4:
                        t.wait_ge(SC.sem, expg[gcnt - 4])
                    pc = PE.bump(t.matmul(s_ps[:, bank * 512:(bank + 1) * 512],
                                          qh[hb][0:CK, qt * 128:(qt + 1) * 128],
                                          kh[hb][0:CK, kc * 512:(kc + 1) * 512],
                                          start=True, stop=True))
                    v.wait_ge(PE.sem, pc)
                    if causal and kc == qt // 4:
                        vmc = VE.bump(v.tensor_add(
                            s_ps[:, bank * 512:(bank + 1) * 512],
                            s_ps[:, bank * 512:(bank + 1) * 512],
                            mc1[:, (qt % 4) * 512:(qt % 4 + 1) * 512]))
                        v.wait_ge(VE.sem, vmc)
                    if gcnt >= 4:
                        v.wait_ge(SC.sem, expg[gcnt - 4])
                    k.vs(v.reduce_max(mgall[:, col:col + 1],
                                      s_ps[:, bank * 512:(bank + 1) * 512], axis=AX.X))
                    vneg = k.vs(v.tensor_scalar(smt[:, slot:slot + 1],
                                                mgall[:, col:col + 1],
                                                -scale, None, op0=ALU.mult))
                    s.wait_ge(VE.sem, vneg)
                    expg[gcnt] = SC.bump(s.activation(
                        pdum[:, kc * 512:(kc + 1) * 512],
                        s_ps[:, bank * 512:(bank + 1) * 512],
                        AF.Exp, bias=smt[:, slot:slot + 1], scale=scale,
                        accum_out=sgall[:, col:col + 1]))
                    gcnt += 1

            exp1_last = expg[gcnt - 1]
            v.wait_ge(SC.sem, exp1_last)
            mg3 = mgall[:, :].rearrange("p (q g) -> p q g", g=4)
            k.vs(v.tensor_reduce(mall[:, :], mg3, axis=AX.X, op=ALU.max))
            last = None
            for g_ in range(4):
                last = v.scalar_tensor_tensor(
                    tm64[:, :].rearrange("p (q g) -> p q g", g=4)[:, :, g_],
                    mall[:, :], -1.0,
                    mg3[:, :, g_], op0=ALU.mult, op1=ALU.add)
            k.vs(last)
            vc4 = k.vs(v.tensor_scalar(tm64[:, :], tm64[:, :], scale, None,
                                       op0=ALU.mult))
            s.wait_ge(VE.sem, vc4)
            scm = SC.bump(s.activation(tm64[:, :], tm64[:, :], AF.Exp, bias=0.0,
                                       scale=1.0))
            v.wait_ge(SC.sem, scm)
            k.vs(v.tensor_mul(tm64[:, :], tm64[:, :], sgall[:, :]))
            k.vs(v.tensor_reduce(sall[:, :], tm64[:, :].rearrange("p (q g) -> p q g", g=4),
                                 axis=AX.X, op=ALU.add))

            s.wait_ge(VE.sem, VE.v)
            sln = SC.bump(s.activation(lns[:, :], sall[:, :], AF.Ln, bias=0.0,
                                       scale=1.0))
            v.wait_ge(SC.sem, sln)
            k.vs(v.tensor_scalar(pdum[:, 0:NT], lns[:, :], inv_scale, None,
                                 op0=ALU.mult))
            vw = k.vs(v.tensor_add(pdum[:, 0:NT], pdum[:, 0:NT], mall[:, :]))
            if kind == "l0f":
                vw = k.vs(v.tensor_sub(pdum[:, 0:NT], pdum[:, 0:NT],
                                       k.sb["lngsc_sb"][:, b * NT:(b + 1) * NT]))
            t.wait_ge(VE.sem, vw)
            t.wait_ge(SC.sem, SC.v)
            last = None
            for qt_ in range(NT):
                last = t.transpose(s_ps[0:1, qt_ * 128:(qt_ + 1) * 128],
                                   pdum[:, qt_:qt_ + 1], k.ident[:, :])
            ptr = PE.bump(last)
            s.wait_ge(PE.sem, ptr)
            vng = SC.bump(s.activation(qh[hb][64:65, :], s_ps[0:1, 0:S],
                                       AF.Copy, bias=0.0, scale=-1.0))

            uexp = {}
            for qt4 in range(4):
                ub = uT[qt4 % 2]
                nkk = 16 if not causal else (4 * qt4 + 4)
                for kc in range(nkk):
                    bank = kc % 4
                    if qt4 == 0 and kc == 0:
                        t.wait_ge(SC.sem, max(exp1_last, vng))
                    if kc >= 4:
                        t.wait_ge(SC.sem, uexp[(qt4, kc - 4)])
                    elif qt4 > 0:
                        t.wait_ge(SC.sem, uexp[(qt4 - 1, prev_nkk - 4 + kc)])
                    pc = PE.bump(t.matmul(s_ps[:, bank * 512:(bank + 1) * 512],
                                          kh[hb][0:CK, kc * 128:(kc + 1) * 128],
                                          qh[hb][0:CK, qt4 * 512:(qt4 + 1) * 512],
                                          start=True, stop=True))
                    diag = causal and (kc >= 4 * qt4)
                    if diag:
                        v.wait_ge(PE.sem, pc)
                        r = kc - 4 * qt4
                        vst = VE.bump(v.scalar_tensor_tensor(
                            dstage[:, r * 512:(r + 1) * 512],
                            s_ps[:, bank * 512:(bank + 1) * 512],
                            scale, mtri[:, r * 512:(r + 1) * 512],
                            op0=ALU.mult, op1=ALU.add))
                        s.wait_ge(VE.sem, vst)
                        uexp[(qt4, kc)] = SC.bump(s.activation(
                            ub[:, kc * 512:(kc + 1) * 512],
                            dstage[:, r * 512:(r + 1) * 512],
                            AF.Exp, bias=0.0, scale=1.0))
                    else:
                        s.wait_ge(PE.sem, pc)
                        uexp[(qt4, kc)] = SC.bump(s.activation(
                            ub[:, kc * 512:(kc + 1) * 512],
                            s_ps[:, bank * 512:(bank + 1) * 512],
                            AF.Exp, bias=0.0, scale=scale))
                prev_nkk = nkk
                t.wait_ge(SC.sem, uexp[(qt4, nkk - 1)])
                if h > 0 or qt4 >= 2:
                    t.wait_ge(SC.sem, oev[qt4 % 2])
                obank = oav_ps[:, (qt4 % 2) * 512:(qt4 % 2) * 512 + 512]
                last = None
                for kc in range(nkk):
                    last = t.matmul(obank,
                                    vt[hb][:, kc * 64:(kc + 1) * 64],
                                    ub[:, kc * 512:(kc + 1) * 512],
                                    start=(kc == 0), stop=(kc == nkk - 1))
                pav = PE.bump(last)
                s.wait_ge(PE.sem, pav)
                if h == 0 and qt4 == 0:
                    oev = {}
                oev[qt4 % 2] = SC.bump(s.copy(
                    ofm[64 * (h % 2):64 * (h % 2) + 64, qt4 * 512:(qt4 + 1) * 512],
                    obank))
            free_at[h] = PE.v
            if h % 2 == 1:
                g.wait_ge(SC.sem, SC.v)
                stf = ST.bump(g.dma_start(
                    out=outscr[b][(h // 2) * 128:(h // 2) * 128 + 128, :],
                    in_=ofm[:, :]), 16)
                s.wait_ge(ST.sem, stf)
            if h + 1 < H:
                ld_t = ld_next
                aug_t, aug_l = aug_rows(h + 1, ld_t)
        g.wait_ge(ST.sem, ST.v)


def ln_tile(k, v, s, buf, xc, st1, sqd):
    VE, SC = k.s_ve, k.s_sc
    k.vs(v.reduce_sum(st1[:, 0:1], buf[:, :], axis=AX.X))
    vc = k.vs(v.tensor_scalar(st1[:, 1:2], st1[:, 0:1], -1.0 / D, None, op0=ALU.mult))
    s.wait_ge(VE.sem, vc)
    sc0 = SC.bump(s.activation(sqd[:, :], buf[:, :], AF.Square, bias=0.0,
                               scale=1.0, accum_out=st1[:, 2:3]))
    v.wait_ge(SC.sem, sc0)
    v.tensor_scalar(st1[:, 3:4], st1[:, 2:3], 1.0 / D, None, op0=ALU.mult)
    k.vs(v.tensor_mul(st1[:, 4:5], st1[:, 1:2], st1[:, 1:2]))
    k.vs(v.tensor_sub(st1[:, 3:4], st1[:, 3:4], st1[:, 4:5]))
    vc = k.vs(v.tensor_scalar(st1[:, 3:4], st1[:, 3:4], EPS, None, op0=ALU.add))
    s.wait_ge(VE.sem, vc)
    k.ss(s.activation(st1[:, 7:8], st1[:, 3:4], AF.Ln, bias=0.0, scale=1.0))
    sc1 = k.s_sc.bump(s.activation(st1[:, 4:5], st1[:, 7:8], AF.Exp, bias=0.0,
                                   scale=-0.5))
    v.wait_ge(SC.sem, sc1)
    k.vs(v.tensor_mul(st1[:, 5:6], st1[:, 4:5], st1[:, 4:5]))
    k.vs(v.tensor_mul(st1[:, 5:6], st1[:, 5:6], st1[:, 3:4]))
    k.vs(v.tensor_scalar(st1[:, 5:6], st1[:, 5:6], -0.5, 1.5, op0=ALU.mult, op1=ALU.add))
    k.vs(v.tensor_mul(st1[:, 6:7], st1[:, 4:5], st1[:, 5:6]))
    k.vs(v.tensor_scalar(xc[:, :], buf[:, :], st1[:, 1:2], None, op0=ALU.add))
    return k.s_ve.bump(v.tensor_scalar(xc[:, :], xc[:, :], st1[:, 6:7], None,
                                       op0=ALU.mult))


def emit_combine_l0(k: K, b: int):
    nc = k.nc
    LD, PE, VE, SC, ST = k.s_ld, k.s_pe, k.s_ve, k.s_sc, k.s_st
    v, s, t, g, sy = k.ve, k.sc, k.te, k.gp, k.sy
    with ExitStack() as ctx:
        a0_sb = k.tmp(ctx, "cba", [128, NC * 512])
        f0_sb = k.tmp(ctx, "cbf", [128, NC * 512])
        x_sb = k.tmp(ctx, "cbx", [128, NC * 512])
        x1stg = k.tmp(ctx, "cbs", [128, NC * 512])
        lnt = k.tmp(ctx, "cbl", [128, D])
        xnt = k.tmp(ctx, "cbn", [128, D])
        sqd = k.tmp(ctx, "cbq", [128, D])
        st1 = k.tmp(ctx, "cbst", [128, 12])
        xn1stg = k.tmp(ctx, "cbg", [128, D])
        ps_t = k.ptmp(ctx, "cbp", [128, 1024])
        ps_t2 = k.ptmp(ctx, "cbp2", [128, 1024])

        for ts4 in range(4):
            sl = slice(ts4 * 512, (ts4 + 1) * 512)
            l1 = LD.bump(sy.dma_start(
                out=a0_sb[:, :].rearrange("p (c t) -> p c t", c=NC),
                in_=k.scr["qT"][b].rearrange("(c p) t -> p c t", p=128)[:, :, sl]), 16)
            l2 = LD.bump(sy.dma_start(
                out=f0_sb[:, :].rearrange("p (c t) -> p c t", c=NC),
                in_=k.scr["kT"][b].rearrange("(c p) t -> p c t", p=128)[:, :, sl]), 16)
            l3 = LD.bump(sy.dma_start(
                out=x_sb[:, :].rearrange("p (c t) -> p c t", c=NC),
                in_=k.scr["xT"][b].rearrange("(c p) t -> p c t", p=128)[:, :, sl]), 16)
            v.wait_ge(LD.sem, max(l1, l2, l3))
            lastv = None
            for c in range(NC):
                csl = slice(c * 512, (c + 1) * 512)
                lastv = v.tensor_add(x1stg[:, csl], x_sb[:, csl], a0_sb[:, csl])
            k.vs(lastv)
            vx1 = None
            for c in range(NC):
                csl = slice(c * 512, (c + 1) * 512)
                vx1 = v.tensor_add(x1stg[:, csl], x1stg[:, csl], f0_sb[:, csl])
            VE.bump(vx1)
            vx1c = k.vflush()
            g.wait_ge(VE.sem, vx1c)
            st1c = ST.bump(g.dma_start(
                out=k.scr["x1T"][b].rearrange("(c p) t -> p c t", p=128)[:, :, sl],
                in_=x1stg[:, :].rearrange("p (c t) -> p c t", c=NC)), 16)
            for tc in range(4):
                t.wait_ge(VE.sem, vx1c)
                if not (ts4 == 0 and tc == 0):
                    t.wait_ge(SC.sem, sc_ev)
                last = None
                for c in range(NC):
                    last = t.transpose(ps_t[:, c * 128:(c + 1) * 128],
                                       x1stg[:, c * 512 + tc * 128:c * 512 + tc * 128 + 128],
                                       k.ident[:, :])
                ptc = PE.bump(last)
                s.wait_ge(PE.sem, ptc)
                last = None
                for c in range(NC):
                    last = s.copy(lnt[:, c * 128:(c + 1) * 128],
                                  ps_t[:, c * 128:(c + 1) * 128])
                sc_ev = SC.bump(last)
                v.wait_ge(SC.sem, sc_ev)
                vxn = ln_tile(k, v, s, lnt, xnt, st1, sqd)
                t.wait_ge(VE.sem, vxn)
                if not (ts4 == 0 and tc == 0):
                    t.wait_ge(SC.sem, sc_ev2)
                last = None
                for c in range(NC):
                    last = t.transpose(ps_t2[:, c * 128:(c + 1) * 128],
                                       xnt[:, c * 128:(c + 1) * 128], k.ident[:, :])
                ptc2 = PE.bump(last)
                s.wait_ge(PE.sem, ptc2)
                if not (ts4 == 0 and tc == 0):
                    s.wait_ge(ST.sem, st_xn)
                last = None
                for c in range(NC):
                    last = s.copy(xn1stg[:, c * 128:(c + 1) * 128],
                                  ps_t2[:, c * 128:(c + 1) * 128])
                sc_ev2 = SC.bump(last)
                g.wait_ge(SC.sem, sc_ev2)
                st_xn = ST.bump(g.dma_start(
                    out=k.scr["xn1T"][b].rearrange("(c p) t -> p c t", p=128)[
                        :, :, ts4 * 512 + tc * 128:ts4 * 512 + tc * 128 + 128],
                    in_=xn1stg[:, :].rearrange("p (c t) -> p c t", c=NC)), 16)
            sy.wait_ge(PE.sem, PE.v)
            sy.wait_ge(ST.sem, st1c)
        g.wait_ge(ST.sem, ST.v)


def emit_moe(k: K):
    nc = k.nc
    LD, PE, VE, SC, ST = k.s_ld, k.s_pe, k.s_ve, k.s_sc, k.s_st
    v, s, t, g, sy = k.ve, k.sc, k.te, k.gp, k.sy
    with ExitStack() as ctx:
        xn_sb = [k.tmp(ctx, "mx", [128, NC * 512]) for _ in range(2)]
        wg_sb = k.tmp(ctx, "mwg", [128, NC * 8])
        lgs = k.tmp(ctx, "mlg", [8, 512])
        lgtm = k.tmp(ctx, "mlt", [128, 8])
        cw = k.tmp(ctx, "mcw", [128, 16])
        coeff = k.tmp(ctx, "mcf", [128, 32 * 8])
        crow = k.tmp(ctx, "mcr", [1, 512])
        cbc = k.tmp(ctx, "mcb", [128, 512])
        w1_sb = k.tmp(ctx, "mw1", [128, NC * 2048])
        w2_sb = k.tmp(ctx, "mw2", [128, 16 * D])
        b1_sb = k.tmp(ctx, "mb1", [128, 16])
        b2_sb = k.tmp(ctx, "mb2", [128, 8])
        h_sb = k.tmp(ctx, "mh", [128, 16 * 512])
        mstg = [k.tmp(ctx, "mst", [128, 512]) for _ in range(2)]
        ps_g = k.ptmp(ctx, "mpg", [128, 512])
        ps_t = k.ptmp(ctx, "mpt", [128, 128])
        ps_h = k.ptmp(ctx, "mph", [128, 1024])
        ps_e = k.ptmp(ctx, "mpe", [128, 1024])

        ldg = LD.bump(sy.dma_start(
            out=wg_sb[:, :].rearrange("p (c o) -> p c o", c=NC),
            in_=k.ins["w1g"].rearrange("(c p) o -> p c o", p=128)), 16)

        for tb in range(8):
            b, ts4 = tb // 4, tb % 4
            sl = slice(ts4 * 512, (ts4 + 1) * 512)
            if tb > 0:
                sy.wait_ge(PE.sem, PE.v)
            lx = LD.bump(sy.dma_start(
                out=xn_sb[0][:, :].rearrange("p (c t) -> p c t", c=NC),
                in_=k.scr["xn1T"][b].rearrange("(c p) t -> p c t", p=128)[:, :, sl]), 16)
            t.wait_ge(LD.sem, max(ldg, lx))
            if tb > 0:
                t.wait_ge(SC.sem, lg_ev)
            last = None
            for c in range(NC):
                last = t.matmul(ps_g[0:8, :], wg_sb[:, c * 8:(c + 1) * 8],
                                xn_sb[0][:, c * 512:(c + 1) * 512],
                                start=(c == 0), stop=(c == NC - 1))
            pg = PE.bump(last)
            s.wait_ge(PE.sem, pg)
            lg_ev = SC.bump(s.copy(lgs[:, :], ps_g[0:8, :]))
            v.wait_ge(SC.sem, lg_ev)
            vb = k.vs(v.tensor_scalar(lgs[:, :], lgs[:, :], k.bias["b1g"][:, 0:1],
                                      None, op0=ALU.add))
            for tc in range(4):
                t.wait_ge(VE.sem, vb)
                if not (tb == 0 and tc == 0):
                    t.wait_ge(SC.sem, tr_ev)
                ptc = PE.bump(t.transpose(ps_t[0:128, 0:8],
                                          lgs[:, tc * 128:(tc + 1) * 128],
                                          k.ident[0:8, 0:8]))
                s.wait_ge(PE.sem, ptc)
                tr_ev = SC.bump(s.copy(lgtm[:, :], ps_t[0:128, 0:8]))
                v.wait_ge(SC.sem, tr_ev)
                k.vs(v.reduce_max(cw[:, 0:1], lgtm[:, :], axis=AX.X))
                k.vs(v.tensor_scalar(cw[:, 8:16], lgtm[:, :], cw[:, 0:1],
                                     None, op0=ALU.is_ge))
                k.vs(v.reduce_sum(cw[:, 1:2], cw[:, 8:16], axis=AX.X))
                k.vs(v.scalar_tensor_tensor(cbc[:, 0:8], cw[:, 8:16], -BIG,
                                            lgtm[:, :], op0=ALU.mult, op1=ALU.add))
                k.vs(v.reduce_max(cw[:, 2:3], cbc[:, 0:8], axis=AX.X))
                k.vs(v.tensor_scalar(cw[:, 3:4], cw[:, 1:2], 2.0, None, op0=ALU.is_lt))
                k.vs(v.tensor_scalar(cbc[:, 8:16], lgtm[:, :], cw[:, 2:3], None,
                                     op0=ALU.is_ge))
                k.vs(v.tensor_sub(cbc[:, 8:16], cbc[:, 8:16], cw[:, 8:16]))
                k.vs(v.scalar_tensor_tensor(cbc[:, 8:16], cbc[:, 8:16], cw[:, 3:4],
                                            cw[:, 8:16], op0=ALU.mult, op1=ALU.add))
                vng = k.vs(v.tensor_scalar(cw[:, 4:5], cw[:, 0:1], -1.0, None,
                                           op0=ALU.mult))
                s.wait_ge(VE.sem, vng)
                swx = SC.bump(s.activation(cbc[:, 16:24], lgtm[:, :], AF.Exp,
                                           bias=cw[:, 4:5], scale=1.0))
                v.wait_ge(SC.sem, swx)
                k.vs(v.tensor_mul(cbc[:, 16:24], cbc[:, 16:24], cbc[:, 8:16]))
                k.vs(v.reduce_sum(cw[:, 5:6], cbc[:, 16:24], axis=AX.X))
                k.vs(v.reciprocal(cw[:, 6:7], cw[:, 5:6]))
                k.vs(v.tensor_scalar(coeff[:, (tb * 4 + tc) * 8:(tb * 4 + tc) * 8 + 8],
                                     cbc[:, 16:24], cw[:, 6:7], None, op0=ALU.mult))

        vcoeff_done = VE.v
        t.wait_ge(VE.sem, vcoeff_done)
        t.wait_ge(SC.sem, SC.v)
        for e in range(8):
            sy.wait_ge(PE.sem, PE.v)
            lw1 = LD.bump(sy.dma_start(
                out=w1_sb[:, :].rearrange("p (c o) -> p c o", c=NC),
                in_=k.ins["w1e1"][e].rearrange("(c p) o -> p c o", p=128)), 16)
            lw2 = LD.bump(sy.dma_start(
                out=w2_sb[:, :].rearrange("p (c o) -> p c o", c=16),
                in_=k.ins["w1e2"][e].rearrange("(c p) o -> p c o", p=128)), 16)
            lb1 = LD.bump(sy.dma_start(out=b1_sb[:, :], in_=k.ins["b1e1"][e]), 16)
            lb2 = LD.bump(sy.dma_start(out=b2_sb[:, :], in_=k.ins["b1e2"][e]), 16)
            for tb in range(8):
                b, ts4 = tb // 4, tb % 4
                sl = slice(ts4 * 512, (ts4 + 1) * 512)
                xb = xn_sb[tb % 2]
                if e > 0 or tb >= 2:
                    sy.wait_ge(PE.sem, PE.v)
                lx = LD.bump(sy.dma_start(
                    out=xb[:, :].rearrange("p (c t) -> p c t", c=NC),
                    in_=k.scr["xn1T"][b].rearrange("(c p) t -> p c t", p=128)[:, :, sl]), 16)
                for tc in range(4):
                    if not (e == 0 and tb == 0 and tc == 0):
                        t.wait_ge(SC.sem, cr_ev)
                    ptc = PE.bump(t.transpose(
                        ps_t[0:1, 0:128],
                        coeff[:, (tb * 4 + tc) * 8 + e:(tb * 4 + tc) * 8 + e + 1],
                        k.ident[:, :]))
                    s.wait_ge(PE.sem, ptc)
                    cr_ev = SC.bump(s.copy(crow[0:1, tc * 128:(tc + 1) * 128],
                                           ps_t[0:1, 0:128]))
                t.wait_ge(SC.sem, cr_ev)
                if not (e == 0 and tb == 0):
                    t.wait_ge(VE.sem, cb_ev)
                pbc = PE.bump(t.matmul(ps_g[:, :], k.ones_row[:, :], crow[0:1, :],
                                       start=True, stop=True))
                v.wait_ge(PE.sem, pbc)
                cb_ev = k.vs(v.tensor_copy(cbc[:, :], ps_g[:, :]))
                t.wait_ge(LD.sem, max(lw1, lb1, lx))
                h_prev = h_ev if not (e == 0 and tb == 0) else None
                h_ev = {}
                for ffc in range(16):
                    bank = ffc % 2
                    if ffc >= 2:
                        t.wait_ge(SC.sem, h_ev[ffc - 2])
                    elif h_prev is not None:
                        t.wait_ge(SC.sem, h_prev[14 + ffc])
                    last = None
                    for c in range(NC):
                        last = t.matmul(ps_h[:, bank * 512:bank * 512 + 512],
                                        w1_sb[:, c * 2048 + ffc * 128:
                                              c * 2048 + ffc * 128 + 128],
                                        xb[:, c * 512:(c + 1) * 512],
                                        start=(c == 0), stop=(c == NC - 1))
                    ph = PE.bump(last)
                    s.wait_ge(PE.sem, ph)
                    if ffc == 0 and h_prev is not None:
                        s.wait_ge(PE.sem, eo_pe_last)
                    h_ev[ffc] = SC.bump(s.activation(
                        h_sb[:, ffc * 512:(ffc + 1) * 512],
                        ps_h[:, bank * 512:bank * 512 + 512],
                        AF.Relu, bias=b1_sb[:, ffc:ffc + 1], scale=1.0))
                t.wait_ge(LD.sem, max(lw2, lb2))
                t.wait_ge(SC.sem, h_ev[15])
                for dc in range(8):
                    bank = dc % 2
                    if dc >= 2:
                        t.wait_ge(VE.sem, m_ev[dc - 2])
                    elif not (e == 0 and tb == 0):
                        t.wait_ge(VE.sem, m_ev_prev[6 + dc])
                    last = None
                    for ffc in range(16):
                        last = t.matmul(ps_e[:, bank * 512:bank * 512 + 512],
                                        w2_sb[:, ffc * D + dc * 128:
                                              ffc * D + dc * 128 + 128],
                                        h_sb[:, ffc * 512:(ffc + 1) * 512],
                                        start=(ffc == 0), stop=(ffc == 15))
                    pe_ = PE.bump(last)
                    if dc == 0:
                        m_ev = {}
                    v.wait_ge(PE.sem, pe_)
                    if e > 0 or tb > 0 or dc >= 2:
                        STP2 = k.s_sta if dc % 2 == 0 else k.s_stb
                        v.wait_ge(STP2.sem, st_prev.get(dc, 0) if dc < 2 else st_d[dc - 2])
                    if dc == 0:
                        st_d = {}
                    VE.bump(v.scalar_tensor_tensor(
                        mstg[dc % 2][:, :], ps_e[:, bank * 512:bank * 512 + 512],
                        b2_sb[:, dc:dc + 1], cbc[:, :], op0=ALU.add, op1=ALU.mult))
                    m_ev[dc] = k.vflush()
                    g.wait_ge(VE.sem, m_ev[dc])
                    STP = k.s_sta if dc % 2 == 0 else k.s_stb
                    st_d[dc] = STP.bump(g.dma_start(
                        out=k.scr["mixT"][b][dc * 128:(dc + 1) * 128, sl],
                        in_=mstg[dc % 2][:, :],
                        accum_op=(ALU.add if e > 0 else ALU.bypass)), 16)
                eo_pe_last = PE.v
                m_ev_prev = dict(m_ev)
                st_prev = {0: st_d[6], 1: st_d[7]}
        g.wait_ge(ST.sem, ST.v)
        g.wait_ge(k.s_sta.sem, k.s_sta.v)
        g.wait_ge(k.s_stb.sem, k.s_stb.v)


def emit_final(k: K):
    nc = k.nc
    LD, PE, VE, SC, ST = k.s_ld, k.s_pe, k.s_ve, k.s_sc, k.s_st
    v, s, t, g, sy = k.ve, k.sc, k.te, k.gp, k.sy
    with ExitStack() as ctx:
        mix_sb = k.tmp(ctx, "fmx", [128, NC * 512])
        xn_sb = k.tmp(ctx, "fxn", [128, NC * 512])
        x1_sb = k.tmp(ctx, "fx1", [128, NC * 512])
        a1_sb = k.tmp(ctx, "fa1", [128, NC * 512])
        lnt = k.tmp(ctx, "fln", [128, D])
        xnt = k.tmp(ctx, "fxt", [128, D])
        sqd = k.tmp(ctx, "fsq", [128, D])
        st1 = k.tmp(ctx, "fst", [128, 12])
        ynT = k.tmp(ctx, "fyt", [128, D])
        outT = k.tmp(ctx, "fot", [128, D])
        outtm = k.tmp(ctx, "fom", [128, D])
        ps_1 = k.ptmp(ctx, "fp1", [128, 1024])
        ps_2 = k.ptmp(ctx, "fp2", [128, 1024])
        ps_3 = k.ptmp(ctx, "fp3", [128, 1024])

        for b in range(NB):
            for ts4 in range(4):
                sl = slice(ts4 * 512, (ts4 + 1) * 512)
                if not (b == 0 and ts4 == 0):
                    sy.wait_ge(VE.sem, VE.v)
                    sy.wait_ge(PE.sem, PE.v)
                l1_ = LD.bump(sy.dma_start(
                    out=mix_sb[:, :].rearrange("p (c t) -> p c t", c=NC),
                    in_=k.scr["mixT"][b].rearrange("(c p) t -> p c t", p=128)[:, :, sl]), 16)
                l2 = LD.bump(sy.dma_start(
                    out=xn_sb[:, :].rearrange("p (c t) -> p c t", c=NC),
                    in_=k.scr["xn1T"][b].rearrange("(c p) t -> p c t", p=128)[:, :, sl]), 16)
                l3 = LD.bump(sy.dma_start(
                    out=x1_sb[:, :].rearrange("p (c t) -> p c t", c=NC),
                    in_=k.scr["x1T"][b].rearrange("(c p) t -> p c t", p=128)[:, :, sl]), 16)
                l4 = LD.bump(sy.dma_start(
                    out=a1_sb[:, :].rearrange("p (c t) -> p c t", c=NC),
                    in_=k.scr["a1T"][b].rearrange("(c p) t -> p c t", p=128)[:, :, sl]), 16)
                v.wait_ge(LD.sem, max(l1_, l2, l3, l4))
                vy = VE.bump(v.tensor_add(mix_sb[:, :], mix_sb[:, :], xn_sb[:, :]))
                for tc in range(4):
                    csl = lambda c: slice(c * 512 + tc * 128, c * 512 + tc * 128 + 128)
                    t.wait_ge(VE.sem, vy)
                    if not (b == 0 and ts4 == 0 and tc == 0):
                        t.wait_ge(SC.sem, ev1)
                    last = None
                    for c in range(NC):
                        last = t.transpose(ps_1[:, c * 128:(c + 1) * 128],
                                           mix_sb[:, csl(c)], k.ident[:, :])
                    pt1 = PE.bump(last)
                    s.wait_ge(PE.sem, pt1)
                    last = None
                    for c in range(NC):
                        last = s.copy(lnt[:, c * 128:(c + 1) * 128],
                                      ps_1[:, c * 128:(c + 1) * 128])
                    ev1 = SC.bump(last)
                    v.wait_ge(SC.sem, ev1)
                    vxn = ln_tile(k, v, s, lnt, xnt, st1, sqd)
                    t.wait_ge(VE.sem, vxn)
                    if not (b == 0 and ts4 == 0 and tc == 0):
                        t.wait_ge(SC.sem, ev2)
                    last = None
                    for c in range(NC):
                        last = t.transpose(ps_2[:, c * 128:(c + 1) * 128],
                                           xnt[:, c * 128:(c + 1) * 128], k.ident[:, :])
                    pt2 = PE.bump(last)
                    s.wait_ge(PE.sem, pt2)
                    last = None
                    for c in range(NC):
                        last = s.copy(ynT[:, c * 128:(c + 1) * 128],
                                      ps_2[:, c * 128:(c + 1) * 128])
                    ev2 = SC.bump(last)
                    v.wait_ge(SC.sem, ev2)
                    lastv = None
                    for c in range(NC):
                        lastv = v.tensor_scalar(outT[:, c * 128:(c + 1) * 128],
                                        ynT[:, c * 128:(c + 1) * 128],
                                        k.bias["gamma"][:, c:c + 1], None, op0=ALU.mult)
                    k.vs(lastv)
                    for c in range(NC):
                        lastv = v.scalar_tensor_tensor(outT[:, c * 128:(c + 1) * 128],
                                               outT[:, c * 128:(c + 1) * 128],
                                               k.bias["beta"][:, c:c + 1],
                                               x1_sb[:, csl(c)],
                                               op0=ALU.add, op1=ALU.add)
                    k.vs(lastv)
                    vo = None
                    for c in range(NC):
                        vo = v.tensor_add(outT[:, c * 128:(c + 1) * 128],
                                          outT[:, c * 128:(c + 1) * 128],
                                          a1_sb[:, csl(c)])
                    vou = VE.bump(vo)
                    t.wait_ge(VE.sem, vou)
                    if not (b == 0 and ts4 == 0 and tc == 0):
                        t.wait_ge(SC.sem, ev3)
                    last = None
                    for c in range(NC):
                        last = t.transpose(ps_3[:, c * 128:(c + 1) * 128],
                                           outT[:, c * 128:(c + 1) * 128],
                                           k.ident[:, :])
                    pt3 = PE.bump(last)
                    s.wait_ge(PE.sem, pt3)
                    if not (b == 0 and ts4 == 0 and tc == 0):
                        s.wait_ge(ST.sem, sto)
                    last = None
                    for c in range(NC):
                        last = s.copy(outtm[:, c * 128:(c + 1) * 128],
                                      ps_3[:, c * 128:(c + 1) * 128])
                    ev3 = SC.bump(last)
                    g.wait_ge(SC.sem, ev3)
                    t0 = ts4 * 512 + tc * 128
                    sto = ST.bump(g.dma_start(
                        out=k.out[b, t0:t0 + 128, :], in_=outtm[:, :]), 16)
        g.wait_ge(ST.sem, ST.v)


def build_program(dbg=False):
    nc = bass.Bass("TRN2", target_bir_lowering=False)
    with ExitStack() as ctx:
        k = K(nc, ctx, dbg)
        declare_io(k)
        k.s_ld = k.sem("s_ld")
        k.s_st = k.sem("s_st")
        k.s_pe = k.sem("s_pe")
        k.s_ve = k.sem("s_ve")
        k.s_sc = k.sem("s_sc")
        k.s_gp = k.sem("s_gp")
        k.s_ld2 = k.sem("s_ld2")
        k.s_lda = k.sem("s_lda")
        k.s_ldb = k.sem("s_ldb")
        k.s_la = k.sem("s_la")
        k.s_lm = k.sem("s_lm")
        k.s_sta = k.sem("s_sta")
        k.s_stb = k.sem("s_stb")
        k.s_pe2 = k.sem("s_pe2")
        k.s_sc2 = k.sem("s_sc2")
        k.s_ve2 = k.sem("s_ve2")
        k.s_st2 = k.sem("s_st2")
        k.sbuf("bst_sb", [1, 32])
        k.sbuf("gsccol_sb", [128, 2 * NT])
        k.sbuf("lngsc_sb", [128, 2 * NT])
        k.sbuf("flush_sb", [1, 8])

        emit_init(k)
        k.barrier()
        for b in range(NB):
            with nc.sbuf_tensor(f"z_sb_{b}", [1, S], F32) as zt:
                k.sb["z_sb"] = zt
                emit_p0(k, b)
                k.barrier()
                emit_bisect(k, b)
                k.barrier()
        for b in range(NB):
            emit_proj(k, b, k.scr["xnT"][b], [
                dict(w="w0aq", bias="b0aq", nout=D, dst=k.scr["qT"][b], dst_tm=None),
                dict(w="w0ak", bias="b0ak", nout=D, dst=k.scr["kT"][b], dst_tm=None),
                dict(w="w0av", bias="b0av", nout=D, dst=None,
                     dst_tm=k.scr["vTm"][b], col0=0),
            ])
            k.barrier()
            emit_attn(k, b, "l0a")
            k.barrier()
            emit_proj(k, b, k.scr["xnT"][b], [
                dict(w="w0fq", bias=None, nout=D, dst=k.scr["qT"][b], dst_tm=None),
                dict(w="w0fk", bias=None, nout=D, dst=k.scr["kT"][b], dst_tm=None),
                dict(w="w0fv", bias=None, nout=D, dst=None,
                     dst_tm=k.scr["vTm"][b], col0=0),
            ])
            k.barrier()
            emit_attn(k, b, "l0f")
            k.barrier()
            emit_proj(k, b, k.scr["oaT"][b], [
                dict(w="w0ao", bias="b0ao", nout=D, dst=k.scr["qT"][b], dst_tm=None)])
            k.barrier()
            emit_proj(k, b, k.scr["ofT"][b], [
                dict(w="w0fo", bias=None, nout=D, dst=k.scr["kT"][b], dst_tm=None)])
            k.barrier()
            emit_combine_l0(k, b)
            k.barrier()
            emit_proj(k, b, k.scr["xn1T"][b], [
                dict(w="w1aq", bias="b1aq", nout=D, dst=k.scr["qT"][b], dst_tm=None),
                dict(w="w1ak", bias="b1ak", nout=256, dst=k.scr["kT"][b], dst_tm=None),
                dict(w="w1av", bias="b1av", nout=256, dst=None,
                     dst_tm=k.scr["vTm"][b], col0=0),
            ])
            k.barrier()
            emit_attn(k, b, "l1")
            k.barrier()
            emit_proj(k, b, k.scr["oaT"][b], [
                dict(w="w1ao", bias="b1ao", nout=D, dst=k.scr["a1T"][b], dst_tm=None)])
            k.barrier()
        emit_moe(k)
        k.barrier()
        emit_final(k)
    return nc


def finalize(k: K):
    nc = k.nc
    with ExitStack() as ctx:
        zt = ctx.enter_context(nc.sbuf_tensor("fin_z", [128, D], F32))
        vc = k.s_ve.bump(k.ve.memset(zt[:, :], 0.0))
        k.gp.wait_ge(k.s_ve.sem, vc)
        ST = k.s_st
        for b in range(NB):
            for tt in range(NT):
                ST.bump(k.gp.dma_start(out=k.out[b, tt * 128:(tt + 1) * 128, :],
                                       in_=zt[:, :]), 16)
        k.gp.wait_ge(ST.sem, ST.v)


def prep_inputs(full):
    T = lambda a: np.ascontiguousarray(np.asarray(a).T)
    C = lambda a: np.ascontiguousarray(np.asarray(a), dtype=np.float32)
    bias_cols = lambda a, nc_: np.ascontiguousarray(np.asarray(a).reshape(nc_, 128).T)
    base = {
        "w0aq": T(full["L0_attn_Wq"]), "w0ak": T(full["L0_attn_Wk"]),
        "w0av": T(full["L0_attn_Wv"]), "w0ao": T(full["L0_attn_Wo"]),
        "b0aq": bias_cols(full["L0_attn_bq"], NC), "b0ak": bias_cols(full["L0_attn_bk"], NC),
        "b0av": bias_cols(full["L0_attn_bv"], NC), "b0ao": bias_cols(full["L0_attn_bo"], NC),
        "w0fg": np.ascontiguousarray(np.repeat(bias_cols(full["L0_ffn_Wg"][:, 0], NC), 2, axis=1)),
        "w0fq": C(full["L0_ffn_Wq"]), "w0fk": C(full["L0_ffn_Wk"]),
        "w0fv": C(full["L0_ffn_Wv"]), "w0fo": C(full["L0_ffn_Wo"]),
        "w1aq": C(full["L1_attn_Wq"]), "w1ak": C(full["L1_attn_Wk"]),
        "w1av": C(full["L1_attn_Wv"]), "w1ao": C(full["L1_attn_Wo"]),
        "b1aq": bias_cols(full["L1_attn_bq"], NC), "b1ao": bias_cols(full["L1_attn_bo"], NC),
        "b1ak": bias_cols(full["L1_attn_bk"], 2), "b1av": bias_cols(full["L1_attn_bv"], 2),
        "slopes": C(full["L1_attn_slopes"]).reshape(1, H),
        "w1g": C(full["L1_ffn_Wg"]), "b1g": C(full["L1_ffn_bg"]).reshape(8, 1),
        "w1e1": C(full["L1_ffn_W1"]),
        "b1e1": np.ascontiguousarray(np.asarray(full["L1_ffn_b1"]).reshape(8, 16, 128).transpose(0, 2, 1)),
        "w1e2": C(full["L1_ffn_W2"]),
        "b1e2": np.ascontiguousarray(np.asarray(full["L1_ffn_b2"]).reshape(8, 8, 128).transpose(0, 2, 1)),
        "gamma": bias_cols(full["L1_ffn_gamma"], NC),
        "beta": bias_cols(full["L1_ffn_beta"], NC),
    }
    x = np.asarray(full["L0_attn_x"], dtype=np.float32)
    maps = []
    for c in range(8):
        m = dict(base)
        m["x"] = np.ascontiguousarray(x[2 * c:2 * c + 2])
        maps.append(m)
    return maps


def kernel(**inputs):
    maps = prep_inputs(inputs)
    nc = build_program(dbg=False)
    res = run_bass_kernel_spmd(nc, maps, core_ids=list(range(8)))
    return np.concatenate([r["out"] for r in res.results], axis=0)
